# version 01
# baseline (speedup 1.0000x reference)
import math
from contextlib import ExitStack

import numpy as np
import ml_dtypes

import concourse.bass as bass
import concourse.mybir as mybir
from concourse.bass_utils import run_bass_kernel_spmd

F32 = mybir.dt.float32
BF16 = mybir.dt.bfloat16
AF = mybir.ActivationFunctionType
ALU = mybir.AluOpType
AX = mybir.AxisListType
BF = ml_dtypes.bfloat16

NCORES = 8
T, E, H, G, D = 8192, 2048, 16, 4, 128
HID, LAT, NEXP = 1024, 512, 16
TOWN = T // NCORES
NTILE = T // 128
JT = 8
ECH = E // 128
SCALE_ATT = 1.0 / math.sqrt(D)
NEG = -1.0e30
EPS = 1e-5
RSCALE = 2.5

DEBUG = 0
PHASES = 99
NEXP_RUN = NEXP


class Ctr:
    def __init__(self, name, step=1):
        self.name, self.step, self.val, self.sem = name, step, 0, None

    def bump(self):
        self.val += self.step
        return self.val

    def snap(self):
        return (self, self.val)


class Prog:
    ORDER = ("sp", "gp", "te", "act", "dve")
    QNAME = {"sp": "sync", "gp": "gpsimd", "te": "tensor",
             "act": "scalar", "dve": "vector"}

    def __init__(self):
        self.ops = []

    def op(self, eng, fn, waits=(), inc=()):
        ws = []
        for w in waits:
            if w is None:
                continue
            ws.append((w, w.val) if isinstance(w, Ctr) else w)
        incs = [(c, c.bump()) for c in inc]
        self.ops.append((eng, fn, ws, incs))

    def barrier(self, ctrs):
        snap = [c.snap() for c in ctrs]
        for eng in self.ORDER:
            self.ops.append((eng, None, snap, []))

    def replay(self, block):
        for name in self.ORDER:
            mine = [(fn, ws, incs) for (e, fn, ws, incs) in self.ops
                    if e == name]
            if not mine:
                continue

            def section(eng, mine=mine):
                for fn, ws, incs in mine:
                    for c, v in ws:
                        if v > 0:
                            eng.wait_ge(c.sem, v)
                    if fn is None:
                        continue
                    inst = fn(eng)
                    for c, _ in incs:
                        inst = inst.then_inc(c.sem, c.step)

            getattr(block, self.QNAME[name])(section)


def _stage_inputs(ins):
    x = np.ascontiguousarray(np.asarray(ins["L0_attn_x"]).reshape(T, E))
    asnp = lambda a: np.ascontiguousarray(np.asarray(a))
    base = {
        "x_full": x,
        "wq": asnp(ins["L0_attn_Wq"]), "wk": asnp(ins["L0_attn_Wk"]),
        "wv": asnp(ins["L0_attn_Wv"]),
        "bqT": asnp(np.asarray(ins["L0_attn_bq"]).T),
        "bkT": asnp(np.asarray(ins["L0_attn_bk"]).T),
        "bvrow": asnp(np.asarray(ins["L0_attn_bv"]).reshape(1, G * D)),
        "sw1T": asnp(np.asarray(ins["L0_ffn_sw1"]).T),
        "swgT": asnp(np.asarray(ins["L0_ffn_swg"]).T),
        "sw2T": asnp(np.asarray(ins["L0_ffn_sw2"]).T),
        "wdT": asnp(np.asarray(ins["L0_ffn_Wdown"]).T),
        "wgT": asnp(np.asarray(ins["L0_ffn_Wg"]).T),
        "wupT": asnp(np.asarray(ins["L0_ffn_Wup"]).T),
        "rw1T": asnp(np.asarray(ins["L0_ffn_rw1"]).transpose(0, 2, 1)),
        "rwgT": asnp(np.asarray(ins["L0_ffn_rwg"]).transpose(0, 2, 1)),
        "rw2T": asnp(np.asarray(ins["L0_ffn_rw2"]).transpose(0, 2, 1)),
        "btrow": asnp(np.tile(np.asarray(ins["L0_ffn_biases"]), JT)[None, :]),
        "ident": np.eye(128, dtype=BF),
        "tri": np.where(np.arange(128)[:, None] > np.arange(128)[None, :],
                        np.float32(NEG), np.float32(0.0)).astype(np.float32),
        "eps": np.full((128, 1), EPS, np.float32),
        "onesb": np.ones((1, 128), BF),
        "onesf": np.ones((1, 128), np.float32),
    }
    xt = x.reshape(NTILE, 128, E)
    maps = []
    for c in range(NCORES):
        m = dict(base)
        m["x_own"] = np.ascontiguousarray(xt[c::8].reshape(TOWN, E))
        db = np.zeros((128, JT), np.float32)
        db[:, c:] = NEG
        m["dbias"] = db
        maps.append(m)
    return maps


def _assemble(results):
    out = np.empty((T, E), np.float32)
    ot = out.reshape(NTILE, 128, E)
    for c in range(NCORES):
        ot[c::8] = np.asarray(results[c]["out"]).reshape(JT, 128, E)
    return out.reshape(1, T, E)


class Builder:
    def __init__(self):
        self.nc = bass.Bass("TRN2", target_bir_lowering=False)
        self.P = Prog()
        self.c_sp = Ctr("c_sp", 16)
        self.c_gp = Ctr("c_gp", 16)
        self.c_te = Ctr("c_te", 1)
        self.c_act = Ctr("c_act", 1)
        self.c_dve = Ctr("c_dve", 1)
        self.ctrs = [self.c_sp, self.c_gp, self.c_te, self.c_act, self.c_dve]

    def dram_in(self, name, shape, dtype=F32):
        return self.nc.dram_tensor(name, shape, dtype, kind="ExternalInput").ap()

    def dram_out(self, name, shape, dtype=F32):
        return self.nc.dram_tensor(name, shape, dtype, kind="ExternalOutput").ap()

    def declare(self):
        di = self.dram_in
        self.x_full = di("x_full", [T, E])
        self.x_own = di("x_own", [TOWN, E])
        self.wq_in = di("wq", [H, E, D])
        self.wk_in = di("wk", [G, E, D])
        self.wv_in = di("wv", [G, E, D])
        self.bqT_in = di("bqT", [D, H])
        self.bkT_in = di("bkT", [D, G])
        self.bvrow_in = di("bvrow", [1, G * D])
        self.sw1T_in = di("sw1T", [E, HID])
        self.swgT_in = di("swgT", [E, HID])
        self.sw2T_in = di("sw2T", [HID, E])
        self.wdT_in = di("wdT", [E, LAT])
        self.wgT_in = di("wgT", [E, NEXP])
        self.wupT_in = di("wupT", [LAT, E])
        self.rw1T_in = di("rw1T", [NEXP, LAT, HID])
        self.rwgT_in = di("rwgT", [NEXP, LAT, HID])
        self.rw2T_in = di("rw2T", [NEXP, HID, LAT])
        self.btrow_in = di("btrow", [1, JT * NEXP])
        self.ident_in = di("ident", [128, 128], BF16)
        self.tri_in = di("tri", [128, 128])
        self.eps_in = di("eps", [128, 1])
        self.onesb_in = di("onesb", [1, 128], BF16)
        self.onesf_in = di("onesf", [1, 128])
        self.dbias_in = di("dbias", [128, JT])
        self.out = self.dram_out("out", [TOWN, E])
        self.k_hbm = self.nc.dram_tensor("k_hbm", [G, D, T], BF16).ap()
        self.v_hbm = self.nc.dram_tensor("v_hbm", [G, T, D], BF16).ap()
        self.s_hbm = self.nc.dram_tensor("s_hbm", [TOWN, E], F32).ap()

    def sb(self, es, name, shape, dtype):
        return es.enter_context(self.nc.sbuf_tensor(name, shape, dtype))

    def ps(self, es, name, shape, dtype=F32):
        return es.enter_context(self.nc.psum_tensor(name, shape, dtype))

    def bar(self):
        self.P.barrier(self.ctrs)

    def ln_chunk(self, bufs, src_dma_fn, xnT_write_fn, st):
        P = self.P
        c_sp, c_te, c_act, c_dve = self.c_sp, self.c_te, self.c_act, self.c_dve
        xt, sqs, stats = bufs["xt"], bufs["sqs"], bufs["stats"]
        msq, var4, xnb, tp_ps = (bufs["msq"], bufs["var4"], bufs["xnb"],
                                 bufs["tp_ps"])
        P.op("sp", src_dma_fn, waits=[st.get("xt_free")], inc=[c_sp])
        xin = c_sp.snap()
        for t in range(4):
            P.op("dve", lambda e, t=t: e.reduce_sum(
                stats[:, t:t + 1], xt[:, t * E:(t + 1) * E], axis=AX.X),
                waits=[xin, st.get("stats_free")], inc=[c_dve])
            P.op("act", lambda e, t=t: e.activation(
                sqs[:, 0:E],
                xt[:, t * E:(t + 1) * E], AF.Square,
                accum_out=stats[:, 4 + t:5 + t]),
                waits=[xin, st.get("stats_free")],
                inc=[c_act])
        sums_done = c_dve.snap()
        sqs_done = c_act.snap()
        P.op("dve", lambda e: e.tensor_scalar(
            stats[:, 8:12], stats[:, 0:4], 1.0 / E, None, ALU.mult),
            waits=[sums_done], inc=[c_dve])
        P.op("dve", lambda e: e.tensor_tensor(
            msq[:, :], stats[:, 8:12], stats[:, 8:12], ALU.mult),
            waits=[c_dve], inc=[c_dve])
        P.op("dve", lambda e: e.scalar_tensor_tensor(
            var4[:, :], stats[:, 4:8], 1.0 / E, msq[:, :],
            ALU.mult, ALU.subtract),
            waits=[sqs_done, c_dve], inc=[c_dve])
        P.op("act", lambda e: e.activation(
            var4[:, :], var4[:, :], AF.Ln, bias=self.epsb[:, 0:1]),
            waits=[c_dve], inc=[c_act])
        P.op("act", lambda e: e.activation(
            stats[:, 12:16], var4[:, :], AF.Exp, scale=-0.5),
            waits=[c_act], inc=[c_act])
        rstd_done = c_act.snap()
        for t in range(4):
            P.op("dve", lambda e, t=t: e.tensor_scalar(
                xnb[:, t * E:(t + 1) * E], xt[:, t * E:(t + 1) * E],
                stats[:, 8 + t:9 + t], stats[:, 12 + t:13 + t],
                ALU.subtract, ALU.mult),
                waits=[rstd_done], inc=[c_dve])
        xnb_done = c_dve.snap()
        st["xt_free"] = xnb_done
        st["stats_free"] = xnb_done
        for b8 in range(8):
            for k in range(8):
                idx = b8 * 8 + k
                t, ech = idx // ECH, idx % ECH
                P.op("te", lambda e, t=t, ech=ech, k=k: e.transpose(
                    tp_ps[:, k * 128:(k + 1) * 128],
                    xnb[:, t * E + ech * 128: t * E + (ech + 1) * 128],
                    self.identb[:, :]),
                    waits=[xnb_done, st.get("tp_free")], inc=[c_te])
            tp_done = c_te.snap()
            t0, ech0 = (b8 * 8) // ECH, (b8 * 8) % ECH
            P.op("dve", lambda e, b8=b8, t0=t0, ech0=ech0:
                 xnT_write_fn(e, b8, t0, ech0),
                 waits=[tp_done, st.get("xnT_free")], inc=[c_dve])
            st["tp_free"] = c_dve.snap()
        return c_dve.snap()

    def build(self):
        self.declare()
        with ExitStack() as outer:
            with nc_sem_stack(self.nc, self.ctrs):
                s = self.sb
                self.identb = s(outer, "identb", [128, 128], BF16)
                self.trib = s(outer, "trib", [128, 128], F32)
                self.epsb = s(outer, "epsb", [128, 1], F32)
                self.onesbb = s(outer, "onesbb", [1, 128], BF16)
                self.onesfb = s(outer, "onesfb", [1, 128], F32)
                self.onescol = s(outer, "onescol", [128, 1], BF16)
                self.dbiasb = s(outer, "dbiasb", [128, JT], F32)
                self.bqTb = s(outer, "bqTb", [128, H], F32)
                self.bkTb = s(outer, "bkTb", [128, G], F32)
                self.bvrep = s(outer, "bvrep", [128, G * 512], F32)
                self.brep8 = s(outer, "brep8", [128, JT * NEXP], F32)
                self.wdense = s(outer, "wdense", [128, JT * NEXP], F32)

                with self.nc.Block() as block:
                    with ExitStack() as p0:
                        self.phase0(p0)
                    self.bar()
                    if PHASES >= 1:
                        with ExitStack() as p1:
                            self.phase1(p1)
                        self.bar()
                    if PHASES >= 2:
                        self.hhT = s(outer, "hhT", [128, 8 * TOWN], BF16)
                        self.xlT = s(outer, "xlT", [128, 4 * TOWN], BF16)
                        with ExitStack() as mid:
                            self.xnT_own = s(mid, "xnT_own",
                                             [128, ECH * TOWN], BF16)
                            self.q_sb = s(mid, "q_sb", [128, H * TOWN], BF16)
                            self.kown = s(mid, "kown", [128, G * TOWN], BF16)
                            self.vown = s(mid, "vown", [128, G * JT * 129],
                                          BF16)
                            with ExitStack() as p1b:
                                self.phase1b(p1b)
                            self.bar()
                            if PHASES >= 3:
                                with ExitStack() as p15:
                                    self.phase15(p15)
                                self.bar()
                            if PHASES >= 4:
                                with ExitStack() as p2:
                                    self.phase2(p2)
                                self.bar()
                        if PHASES >= 5:
                            with ExitStack() as p3:
                                self.phase3(p3)
                            self.bar()
                    self.finale()
                    self.P.replay(block)
        return self.nc

    def phase0(self, es):
        P = self.P
        c_sp, c_te, c_dve = self.c_sp, self.c_te, self.c_dve
        for dst, src in [
            (self.identb, self.ident_in), (self.trib, self.tri_in),
            (self.epsb, self.eps_in), (self.onesbb, self.onesb_in), (self.onesfb, self.onesf_in),
            (self.dbiasb, self.dbias_in), (self.bqTb, self.bqT_in),
            (self.bkTb, self.bkT_in),
        ]:
            P.op("sp", lambda e, d=dst, s=src: e.dma_start(out=d[:, :], in_=s),
                 inc=[c_sp])
        bvrow = self.sb(es, "bvrow_s", [1, G * D], F32)
        btrow = self.sb(es, "btrow_s", [1, JT * NEXP], F32)
        P.op("sp", lambda e: e.dma_start(out=bvrow[:, :], in_=self.bvrow_in),
             inc=[c_sp])
        P.op("sp", lambda e: e.dma_start(out=btrow[:, :], in_=self.btrow_in),
             inc=[c_sp])
        ld = c_sp.snap()
        P.op("dve", lambda e: e.memset(self.onescol[:, :], 1.0), inc=[c_dve])
        rep_ps = self.ps(es, "rep_ps", [128, G * D])
        P.op("te", lambda e: e.matmul(rep_ps[:, :], self.onesfb[0:1, :],
                                      bvrow[0:1, :], start=True, stop=True),
             waits=[ld], inc=[c_te])
        mm1 = self.c_te.snap()
        for gi in range(G):
            for r in range(4):
                P.op("dve", lambda e, gi=gi, r=r: e.tensor_copy(
                    self.bvrep[:, gi * 512 + r * 128: gi * 512 + (r + 1) * 128],
                    rep_ps[:, gi * 128:(gi + 1) * 128]),
                    waits=[mm1], inc=[c_dve])
        done_bv = c_dve.snap()
        P.op("te", lambda e: e.matmul(rep_ps[:, 0:JT * NEXP],
                                      self.onesfb[0:1, :], btrow[0:1, :],
                                      start=True, stop=True),
             waits=[ld, done_bv], inc=[c_te])
        mm2 = self.c_te.snap()
        P.op("dve", lambda e: e.tensor_copy(self.brep8[:, :],
                                            rep_ps[:, 0:JT * NEXP]),
             waits=[mm2], inc=[c_dve])

    def phase1(self, es):
        P = self.P
        c_sp, c_gp, c_te, c_act, c_dve = (self.c_sp, self.c_gp, self.c_te,
                                          self.c_act, self.c_dve)
        s = self.sb
        wkb = s(es, "wkb", [128, G * ECH * 128], BF16)
        wvb = s(es, "wvb", [128, G * ECH * 128], BF16)
        bufs = {
            "xt": s(es, "xt", [128, 4 * E], F32),
            "sqs": s(es, "sqs", [128, E], F32),
            "stats": s(es, "stats", [128, 16], F32),
            "msq": s(es, "msq", [128, 4], F32),
            "var4": s(es, "var4", [128, 4], F32),
            "xnb": s(es, "xnb", [128, 4 * E], BF16),
            "tp_ps": self.ps(es, "tp_ps", [128, 1024], BF16),
        }
        xnT = [s(es, f"xnT{i}", [128, ECH * 512], BF16) for i in (0, 1)]
        kst = s(es, "kst", [128, G * 512], BF16)
        vst = s(es, "vst", [128, G * 512], BF16)
        kv_ps = [self.ps(es, f"kv_ps{i}", [128, 512]) for i in range(4)]

        for gi in range(G):
            P.op("gp", lambda e, gi=gi: e.dma_start(
                out=wkb[:, gi * ECH * 128:(gi + 1) * ECH * 128]
                .rearrange("p (c d) -> p c d", c=ECH),
                in_=self.wk_in[gi].rearrange("(c p) d -> p c d", p=128)),
                inc=[c_gp])
            P.op("gp", lambda e, gi=gi: e.dma_start(
                out=wvb[:, gi * ECH * 128:(gi + 1) * ECH * 128]
                .rearrange("p (c d) -> p c d", c=ECH),
                in_=self.wv_in[gi].rearrange("(c p) d -> p c d", p=128)),
                inc=[c_gp])
        wload = c_gp.snap()

        st = {}
        kst_free = None
        vst_free = None
        xnT_free = [None, None]

        for ch in range(T // 512):
            bi = ch % 2

            def src_dma(e, ch=ch):
                return e.dma_start(
                    out=bufs["xt"][:, :].rearrange("p (t e) -> p t e", t=4),
                    in_=self.x_full[512 * ch:512 * (ch + 1), :]
                    .rearrange("(t p) e -> p t e", p=128))

            def xnT_wr(e, b8, t0, ech0, bi=bi):
                return e.tensor_copy(
                    xnT[bi][:, :].rearrange("p (c n) -> p c n", c=ECH)
                    [:, ech0:ech0 + 8, t0 * 128:(t0 + 1) * 128],
                    bufs["tp_ps"][:, :].rearrange("p (k n) -> p k n", k=8))

            st["xnT_free"] = xnT_free[bi]
            xnT_done = self.ln_chunk(bufs, src_dma, xnT_wr, st)

            for gi in range(G):
                for ech in range(ECH):
                    P.op("te", lambda e, gi=gi, ech=ech, bi=bi: e.matmul(
                        kv_ps[gi][:, :],
                        wkb[:, (gi * ECH + ech) * 128:(gi * ECH + ech + 1) * 128],
                        xnT[bi][:, ech * 512:(ech + 1) * 512],
                        start=(ech == 0), stop=(ech == ECH - 1)),
                        waits=[xnT_done, wload, kst_free, vst_free],
                        inc=[c_te])
            kmm_done = self.c_te.snap()
            for gi in range(G):
                P.op("act", lambda e, gi=gi: e.activation(
                    kst[:, gi * 512:(gi + 1) * 512], kv_ps[gi][:, :],
                    AF.Identity, bias=self.bkTb[:, gi:gi + 1]),
                    waits=[kmm_done], inc=[c_act])
            kst_done = self.c_act.snap()
            for gi in range(G):
                P.op("sp", lambda e, ch=ch, gi=gi: e.dma_start(
                    out=self.k_hbm[gi, :, ch * 512:(ch + 1) * 512],
                    in_=kst[:, gi * 512:(gi + 1) * 512]),
                    waits=[kst_done], inc=[c_sp])
            kst_free = self.c_sp.snap()

            for gi in range(G):
                for t in range(4):
                    for ech in range(ECH):
                        P.op("te", lambda e, gi=gi, t=t, ech=ech, bi=bi: e.matmul(
                            kv_ps[gi][:, t * 128:(t + 1) * 128],
                            xnT[bi][:, ech * 512 + t * 128:
                                    ech * 512 + (t + 1) * 128],
                            wvb[:, (gi * ECH + ech) * 128:
                                (gi * ECH + ech + 1) * 128],
                            start=(ech == 0), stop=(ech == ECH - 1)),
                            waits=[xnT_done, wload, kst_free, vst_free],
                            inc=[c_te])
            vmm_done = self.c_te.snap()
            xnT_free[bi] = vmm_done
            for gi in range(G):
                P.op("dve", lambda e, gi=gi: e.tensor_tensor(
                    vst[:, gi * 512:(gi + 1) * 512], kv_ps[gi][:, :],
                    self.bvrep[:, gi * 512:(gi + 1) * 512], ALU.add),
                    waits=[vmm_done], inc=[c_dve])
            vst_done = self.c_dve.snap()
            for gi in range(G):
                P.op("sp", lambda e, ch=ch, gi=gi: e.dma_start(
                    out=self.v_hbm[gi, ch * 512:(ch + 1) * 512, :]
                    .rearrange("(t p) d -> p t d", p=128),
                    in_=vst[:, gi * 512:(gi + 1) * 512]
                    .rearrange("p (t d) -> p t d", t=4)),
                    waits=[vst_done], inc=[c_sp])
            vst_free = self.c_sp.snap()

        if DEBUG:
            fin = self.c_sp.snap()
            self.dbg_khbm = self.dram_out("dbg_khbm", [G, D, T], BF16)
            self.dbg_vhbm = self.dram_out("dbg_vhbm", [G, T, D], BF16)
            P.op("sp", lambda e: e.dma_start(out=self.dbg_khbm,
                                             in_=self.k_hbm), waits=[fin],
                 inc=[c_sp])
            P.op("sp", lambda e: e.dma_start(out=self.dbg_vhbm,
                                             in_=self.v_hbm), waits=[fin],
                 inc=[c_sp])

    def phase1b(self, es):
        P = self.P
        c_sp, c_gp, c_te, c_act, c_dve = (self.c_sp, self.c_gp, self.c_te,
                                          self.c_act, self.c_dve)
        s = self.sb
        wkb = [s(es, f"wkb2{i}", [128, ECH * 128], BF16) for i in (0, 1)]
        wvb = [s(es, f"wvb2{i}", [128, ECH * 128], BF16) for i in (0, 1)]
        wqb = [s(es, f"wqb{i}", [128, ECH * 128], BF16) for i in (0, 1)]
        bufs = {
            "xt": s(es, "xtb", [128, 4 * E], F32),
            "sqs": s(es, "sqsb", [128, E], F32),
            "stats": s(es, "statsb", [128, 16], F32),
            "msq": s(es, "msqb", [128, 4], F32),
            "var4": s(es, "var4b", [128, 4], F32),
            "xnb": s(es, "xnbb", [128, 4 * E], BF16),
            "tp_ps": self.ps(es, "tp_psb", [128, 1024], BF16),
        }
        mm_ps = [self.ps(es, f"mm_psb{i}", [128, 512]) for i in range(4)]

        xnT_view = self.xnT_own[:, :].rearrange("p (c n) -> p c n", c=ECH)
        st = {}
        for ch in range(TOWN // 512):
            def src_dma(e, ch=ch):
                return e.dma_start(
                    out=bufs["xt"][:, :].rearrange("p (t e) -> p t e", t=4),
                    in_=self.x_own[512 * ch:512 * (ch + 1), :]
                    .rearrange("(t p) e -> p t e", p=128))

            def xnT_wr(e, b8, t0, ech0, ch=ch):
                return e.tensor_copy(
                    xnT_view[:, ech0:ech0 + 8,
                             ch * 512 + t0 * 128: ch * 512 + (t0 + 1) * 128],
                    bufs["tp_ps"][:, :].rearrange("p (k n) -> p k n", k=8))

            self.ln_chunk(bufs, src_dma, xnT_wr, st)
        xnT_done = self.c_dve.snap()

        wq_free = [None, None]
        ps_free = [None, None, None, None]
        for h in range(H):
            wi = h % 2
            P.op("gp", lambda e, h=h, wi=wi: e.dma_start(
                out=wqb[wi][:, :].rearrange("p (c d) -> p c d", c=ECH),
                in_=self.wq_in[h].rearrange("(c p) d -> p c d", p=128)),
                waits=[wq_free[wi]], inc=[c_gp])
            wq_ld = c_gp.snap()
            for tc in range(2):
                pi = tc
                for ech in range(ECH):
                    P.op("te", lambda e, wi=wi, ech=ech, tc=tc, pi=pi: e.matmul(
                        mm_ps[pi][:, :],
                        wqb[wi][:, ech * 128:(ech + 1) * 128],
                        xnT_view[:, ech, tc * 512:(tc + 1) * 512],
                        start=(ech == 0), stop=(ech == ECH - 1)),
                        waits=[xnT_done, wq_ld, ps_free[pi]], inc=[c_te])
                qmm = self.c_te.snap()
                P.op("act", lambda e, h=h, tc=tc, pi=pi: e.activation(
                    self.q_sb[:, h * TOWN + tc * 512:
                              h * TOWN + (tc + 1) * 512],
                    mm_ps[pi][:, :], AF.Identity, bias=self.bqTb[:, h:h + 1]),
                    waits=[qmm], inc=[c_act])
                ps_free[pi] = self.c_act.snap()
            wq_free[wi] = self.c_act.snap()

        P.op("dve", lambda e: e.memset(
            self.vown[:, :].rearrange("p (q s) -> p q s", s=129)
            [:, :, 128:129], 1.0), inc=[c_dve])
        ones_set = self.c_dve.snap()

        kvw_free = [None, None]
        for gi in range(G):
            wi = gi % 2
            P.op("gp", lambda e, gi=gi, wi=wi: e.dma_start(
                out=wkb[wi][:, :].rearrange("p (c d) -> p c d", c=ECH),
                in_=self.wk_in[gi].rearrange("(c p) d -> p c d", p=128)),
                waits=[kvw_free[wi]], inc=[c_gp])
            P.op("gp", lambda e, gi=gi, wi=wi: e.dma_start(
                out=wvb[wi][:, :].rearrange("p (c d) -> p c d", c=ECH),
                in_=self.wv_in[gi].rearrange("(c p) d -> p c d", p=128)),
                waits=[kvw_free[wi]], inc=[c_gp])
            kv_wload = c_gp.snap()
            for tc in range(2):
                pi = 2 + tc
                for ech in range(ECH):
                    P.op("te", lambda e, wi=wi, tc=tc, ech=ech, pi=pi: e.matmul(
                        mm_ps[pi][:, :],
                        wkb[wi][:, ech * 128:(ech + 1) * 128],
                        xnT_view[:, ech, tc * 512:(tc + 1) * 512],
                        start=(ech == 0), stop=(ech == ECH - 1)),
                        waits=[xnT_done, kv_wload, ps_free[pi]], inc=[c_te])
                kmm = self.c_te.snap()
                P.op("act", lambda e, gi=gi, tc=tc, pi=pi: e.activation(
                    self.kown[:, gi * TOWN + tc * 512:
                              gi * TOWN + (tc + 1) * 512],
                    mm_ps[pi][:, :], AF.Identity, bias=self.bkTb[:, gi:gi + 1]),
                    waits=[kmm], inc=[c_act])
                ps_free[pi] = self.c_act.snap()
            for t in range(JT):
                pi = t % 2
                for ech in range(ECH):
                    P.op("te", lambda e, wi=wi, t=t, ech=ech, pi=pi: e.matmul(
                        mm_ps[pi][:, 0:128],
                        xnT_view[:, ech, t * 128:(t + 1) * 128],
                        wvb[wi][:, ech * 128:(ech + 1) * 128],
                        start=(ech == 0), stop=(ech == ECH - 1)),
                        waits=[xnT_done, kv_wload, ps_free[pi]], inc=[c_te])
                vmm = self.c_te.snap()
                P.op("dve", lambda e, gi=gi, t=t, pi=pi: e.tensor_tensor(
                    self.vown[:, (gi * JT + t) * 129:
                              (gi * JT + t) * 129 + 128],
                    mm_ps[pi][:, 0:128],
                    self.bvrep[:, gi * 512:gi * 512 + 128], ALU.add),
                    waits=[vmm, ones_set], inc=[c_dve])
                ps_free[pi] = self.c_dve.snap()
            kvw_free[wi] = self.c_dve.snap()

        if DEBUG:
            fin = [self.c_act.snap(), self.c_dve.snap()]
            self.dbg_xnt = self.dram_out("dbg_xnt", [128, ECH * TOWN], BF16)
            self.dbg_q = self.dram_out("dbg_q", [128, H * TOWN], BF16)
            self.dbg_kown = self.dram_out("dbg_kown", [128, G * TOWN], BF16)
            self.dbg_vown = self.dram_out("dbg_vown", [128, G * JT * 129], BF16)
            for dst, src in [(self.dbg_xnt, self.xnT_own),
                             (self.dbg_q, self.q_sb),
                             (self.dbg_kown, self.kown),
                             (self.dbg_vown, self.vown)]:
                P.op("sp", lambda e, d=dst, s=src: e.dma_start(out=d,
                                                               in_=s[:, :]),
                     waits=fin, inc=[c_sp])

    def phase15(self, es):
        P = self.P
        c_sp, c_gp, c_te, c_act, c_dve = (self.c_sp, self.c_gp, self.c_te,
                                          self.c_act, self.c_dve)
        s = self.sb
        sw1b = s(es, "sw1b", [128, ECH * HID], BF16)
        swgb = s(es, "swgb", [128, ECH * HID], BF16)
        wdb = s(es, "wdb", [128, ECH * LAT], BF16)
        wgb = s(es, "wgb", [128, ECH * NEXP], BF16)
        prob = s(es, "prob", [128, JT * NEXP], F32)
        vbuf = s(es, "vbuf_g", [128, JT * NEXP], F32)
        cnt = [s(es, f"cnt{i}", [128, JT * NEXP], F32) for i in (0, 1)]
        rbuf = s(es, "rbuf", [128, HID], BF16)
        rbuf2 = s(es, "rbuf2", [128, HID], BF16)
        rsum = s(es, "rsum", [128, JT], F32)
        z_ps = self.ps(es, "z_ps", [128, 512])
        h_ps = self.ps(es, "h_ps", [128, 1024])
        g_ps = self.ps(es, "g_ps", [128, 1024])

        xnT_view = self.xnT_own[:, :].rearrange("p (c n) -> p c n", c=ECH)

        P.op("gp", lambda e: e.dma_start(
            out=sw1b[:, :].rearrange("p (c n) -> p c n", c=ECH),
            in_=self.sw1T_in.rearrange("(c p) n -> p c n", p=128)),
            inc=[c_gp])
        P.op("gp", lambda e: e.dma_start(
            out=swgb[:, :].rearrange("p (c n) -> p c n", c=ECH),
            in_=self.swgT_in.rearrange("(c p) n -> p c n", p=128)),
            inc=[c_gp])
        P.op("gp", lambda e: e.dma_start(
            out=wdb[:, :].rearrange("p (c n) -> p c n", c=ECH),
            in_=self.wdT_in.rearrange("(c p) n -> p c n", p=128)),
            inc=[c_gp])
        P.op("gp", lambda e: e.dma_start(
            out=wgb[:, :].rearrange("p (c n) -> p c n", c=ECH),
            in_=self.wgT_in.rearrange("(c p) n -> p c n", p=128)),
            inc=[c_gp])
        wload = c_gp.snap()

        for t in range(JT):
            for ech in range(ECH):
                P.op("te", lambda e, t=t, ech=ech: e.matmul(
                    z_ps[:, t * NEXP:(t + 1) * NEXP],
                    xnT_view[:, ech, t * 128:(t + 1) * 128],
                    wgb[:, ech * NEXP:(ech + 1) * NEXP],
                    start=(ech == 0), stop=(ech == ECH - 1)),
                    waits=[wload], inc=[c_te])
        zmm = self.c_te.snap()
        P.op("act", lambda e: e.activation(
            prob[:, :], z_ps[:, 0:JT * NEXP], AF.Exp, scale=-1.0),
            waits=[zmm], inc=[c_act])
        P.op("dve", lambda e: e.tensor_scalar(
            prob[:, :], prob[:, :], 1.0, None, ALU.add),
            waits=[c_act], inc=[c_dve])
        P.op("dve", lambda e: e.reciprocal(prob[:, :], prob[:, :]),
            waits=[c_dve], inc=[c_dve])
        P.op("dve", lambda e: e.tensor_tensor(
            vbuf[:, :], prob[:, :], self.brep8[:, :], ALU.add),
            waits=[c_dve], inc=[c_dve])
        P.op("dve", lambda e: e.memset(cnt[0][:, :], 0.0),
            waits=[c_dve], inc=[c_dve])
        cur = 0
        for ep in range(NEXP):
            for t in range(JT):
                P.op("dve", lambda e, ep=ep, t=t, cur=cur: e.scalar_tensor_tensor(
                    cnt[1 - cur][:, t * NEXP:(t + 1) * NEXP],
                    vbuf[:, t * NEXP:(t + 1) * NEXP],
                    vbuf[:, t * NEXP + ep:t * NEXP + ep + 1],
                    cnt[cur][:, t * NEXP:(t + 1) * NEXP],
                    ALU.is_lt, ALU.add),
                    waits=[c_dve], inc=[c_dve])
            cur = 1 - cur
        P.op("dve", lambda e, cur=cur: e.tensor_scalar(
            cnt[1 - cur][:, :], cnt[cur][:, :], 7.5, None, ALU.is_le),
            waits=[c_dve], inc=[c_dve])
        P.op("dve", lambda e, cur=cur: e.tensor_tensor(
            self.wdense[:, :], prob[:, :], cnt[1 - cur][:, :], ALU.mult),
            waits=[c_dve], inc=[c_dve])
        P.op("dve", lambda e: e.reduce_sum(
            rsum[:, :].rearrange("p (j o) -> p j o", o=1),
            self.wdense[:, :].rearrange("p (j n) -> p j n", j=JT), axis=AX.X),
            waits=[c_dve], inc=[c_dve])
        P.op("dve", lambda e: e.reciprocal(rsum[:, :], rsum[:, :]),
            waits=[c_dve], inc=[c_dve])
        for t in range(JT):
            P.op("dve", lambda e, t=t: e.tensor_scalar(
                self.wdense[:, t * NEXP:(t + 1) * NEXP],
                self.wdense[:, t * NEXP:(t + 1) * NEXP],
                rsum[:, t:t + 1], RSCALE, ALU.mult, ALU.mult),
                waits=[c_dve], inc=[c_dve])

        r_free = None
        for hch in range(8):
            for tc in range(2):
                for ech in range(ECH):
                    P.op("te", lambda e, hch=hch, tc=tc, ech=ech: e.matmul(
                        h_ps[:, tc * 512:(tc + 1) * 512],
                        sw1b[:, ech * HID + hch * 128:
                             ech * HID + (hch + 1) * 128],
                        xnT_view[:, ech, tc * 512:(tc + 1) * 512],
                        start=(ech == 0), stop=(ech == ECH - 1)),
                        waits=[wload, c_dve], inc=[c_te])
                    P.op("te", lambda e, hch=hch, tc=tc, ech=ech: e.matmul(
                        g_ps[:, tc * 512:(tc + 1) * 512],
                        swgb[:, ech * HID + hch * 128:
                             ech * HID + (hch + 1) * 128],
                        xnT_view[:, ech, tc * 512:(tc + 1) * 512],
                        start=(ech == 0), stop=(ech == ECH - 1)),
                        waits=[wload, c_dve], inc=[c_te])
            hmm = self.c_te.snap()
            P.op("act", lambda e: e.activation(
                rbuf[:, :], g_ps[:, :], AF.Relu),
                waits=[hmm, r_free], inc=[c_act])
            P.op("dve", lambda e: e.tensor_tensor(
                rbuf2[:, :], rbuf[:, :], rbuf[:, :], ALU.mult),
                waits=[c_act], inc=[c_dve])
            P.op("dve", lambda e, hch=hch: e.tensor_tensor(
                self.hhT[:, hch * TOWN:(hch + 1) * TOWN],
                h_ps[:, :], rbuf2[:, :], ALU.mult),
                waits=[c_dve], inc=[c_dve])
            r_free = self.c_dve.snap()

        for lch in range(4):
            for tc in range(2):
                for ech in range(ECH):
                    P.op("te", lambda e, lch=lch, tc=tc, ech=ech: e.matmul(
                        h_ps[:, tc * 512:(tc + 1) * 512],
                        wdb[:, ech * LAT + lch * 128:
                            ech * LAT + (lch + 1) * 128],
                        xnT_view[:, ech, tc * 512:(tc + 1) * 512],
                        start=(ech == 0), stop=(ech == ECH - 1)),
                        waits=[wload, c_act, c_dve], inc=[c_te])
            xmm = self.c_te.snap()
            P.op("act", lambda e, lch=lch: e.activation(
                self.xlT[:, lch * TOWN:(lch + 1) * TOWN], h_ps[:, :],
                AF.Identity),
                waits=[xmm], inc=[c_act])

        if DEBUG:
            fin = [self.c_act.snap(), self.c_dve.snap()]
            self.dbg_w = self.dram_out("dbg_w", [128, JT * NEXP], F32)
            self.dbg_hh = self.dram_out("dbg_hh", [128, 8 * TOWN], BF16)
            self.dbg_xl = self.dram_out("dbg_xl", [128, 4 * TOWN], BF16)
            for dst, src in [(self.dbg_w, self.wdense), (self.dbg_hh, self.hhT),
                             (self.dbg_xl, self.xlT)]:
                P.op("sp", lambda e, d=dst, s=src: e.dma_start(out=d,
                                                               in_=s[:, :]),
                     waits=fin, inc=[c_sp])

    def phase2(self, es):
        P = self.P
        c_sp, c_te, c_act, c_dve = self.c_sp, self.c_te, self.c_act, self.c_dve
        s = self.sb
        kbuf = [s(es, f"kbuf{i}", [128, T], BF16) for i in (0, 1)]
        vbuf = [s(es, f"vbuf{i}", [128, 64 * 129], BF16) for i in (0, 1)]
        pt = [s(es, f"pt{i}", [128, 1024], BF16) for i in (0, 1)]
        sst = s(es, "sst", [128, JT * 128], F32)
        qsq = s(es, "qsqb", [128, 1024], BF16)
        ksq = s(es, "ksqb", [128, 512], BF16)
        kmaxb = s(es, "kmaxb", [1, 20], F32)
        cbuf = s(es, "cbuf", [1, 8], F32)
        crep = s(es, "crep", [128, 1], F32)
        dbadj = s(es, "dbadj", [128, JT], F32)
        rcpb = s(es, "rcpb", [128, JT], F32)
        zbuf = s(es, "zbuf", [128, 512], BF16)
        s_ps = [self.ps(es, f"s_ps{i}", [128, 1024]) for i in (0, 1)]
        o_ps = [self.ps(es, f"o_ps{i}", [128, 512]) for i in range(3)]
        cp_ps = self.ps(es, "cp_ps", [128, 512])

        def o_tile(j):
            return o_ps[j // 3][:, (j % 3) * 129:(j % 3) * 129 + 129]

        for i in (0, 1):
            P.op("dve", lambda e, i=i: e.memset(
                vbuf[i][:, :].rearrange("p (q s) -> p q s", s=129)
                [:, :, 128:129], 1.0), inc=[c_dve])
        P.op("dve", lambda e: e.memset(zbuf[:, :], 0.0), inc=[c_dve])
        ones_set = c_dve.snap()

        exp_done = [None, None]
        av_done = [None, None]
        head_eplg = None
        sst_free = None
        qsq_free = None
        group_done = [None, None]
        pp = 0

        for gi in range(G):
            bi = gi % 2
            P.op("sp", lambda e, gi=gi, bi=bi: e.dma_start(
                out=kbuf[bi][:, :], in_=self.k_hbm[gi]),
                waits=[group_done[bi]], inc=[c_sp])
            P.op("sp", lambda e, gi=gi, bi=bi: e.dma_start(
                out=vbuf[bi][:, :].rearrange("p (q s) -> p q s", s=129)
                [:, :, 0:128],
                in_=self.v_hbm[gi].rearrange("(q p) d -> p q d", p=128)),
                waits=[group_done[bi], ones_set], inc=[c_sp])
            ready = self.c_sp.snap()

            kprev = None
            for chk in range(T // 512):
                P.op("act", lambda e, bi=bi, chk=chk: e.activation(
                    ksq[:, :], kbuf[bi][:, chk * 512:(chk + 1) * 512],
                    AF.Square),
                    waits=[ready, kprev], inc=[c_act])
                sq_done = c_act.snap()
                P.op("te", lambda e: e.matmul(
                    cp_ps[0:1, 0:512], self.onescol[:, 0:1], ksq[:, :],
                    start=True, stop=True),
                    waits=[sq_done, kprev], inc=[c_te])
                mm = self.c_te.snap()
                P.op("dve", lambda e, chk=chk: e.reduce_max(
                    kmaxb[0:1, chk + 4:chk + 5], cp_ps[0:1, 0:512], axis=AX.X),
                    waits=[mm], inc=[c_dve])
                kprev = c_dve.snap()
            P.op("dve", lambda e: e.reduce_max(
                kmaxb[0:1, 0:1], kmaxb[0:1, 4:4 + T // 512], axis=AX.X),
                waits=[kprev], inc=[c_dve])
            P.op("act", lambda e: e.activation(
                kmaxb[0:1, 1:2], kmaxb[0:1, 0:1], AF.Ln),
                waits=[c_dve], inc=[c_act])
            lnk_done = c_act.snap()

            for hh in range(4):
                h = gi * 4 + hh
                o_free = head_eplg
                P.op("act", lambda e, h=h: e.activation(
                    qsq[:, :], self.q_sb[:, h * TOWN:(h + 1) * TOWN],
                    AF.Square), waits=[qsq_free], inc=[c_act])
                qsq_done = c_act.snap()
                qprev = None
                for half in range(2):
                    P.op("te", lambda e, half=half: e.matmul(
                        cp_ps[0:1, 0:512], self.onescol[:, 0:1],
                        qsq[:, half * 512:(half + 1) * 512],
                        start=True, stop=True),
                        waits=[qsq_done, qprev], inc=[c_te])
                    mm = self.c_te.snap()
                    P.op("dve", lambda e, half=half: e.reduce_max(
                        cbuf[0:1, half:half + 1], cp_ps[0:1, 0:512],
                        axis=AX.X), waits=[mm], inc=[c_dve])
                    qprev = c_dve.snap()
                qsq_free = qprev
                P.op("dve", lambda e: e.reduce_max(
                    cbuf[0:1, 2:3], cbuf[0:1, 0:2], axis=AX.X),
                    waits=[qprev], inc=[c_dve])
                P.op("act", lambda e: e.activation(
                    cbuf[0:1, 3:4], cbuf[0:1, 2:3], AF.Ln),
                    waits=[c_dve], inc=[c_act])
                P.op("dve", lambda e: e.tensor_tensor(
                    cbuf[0:1, 4:5], cbuf[0:1, 3:4], kmaxb[0:1, 1:2], ALU.add),
                    waits=[c_act, lnk_done], inc=[c_dve])
                P.op("act", lambda e: e.activation(
                    cbuf[0:1, 5:6], cbuf[0:1, 4:5], AF.Exp, scale=0.5),
                    waits=[c_dve], inc=[c_act])
                P.op("dve", lambda e: e.tensor_scalar(
                    cbuf[0:1, 6:7], cbuf[0:1, 5:6], -SCALE_ATT, None,
                    ALU.mult), waits=[c_act], inc=[c_dve])
                P.op("te", lambda e: e.matmul(
                    cp_ps[:, 0:1], self.onesfb[0:1, :], cbuf[0:1, 6:7],
                    start=True, stop=True), waits=[c_dve], inc=[c_te])
                P.op("dve", lambda e: e.tensor_copy(crep[:, :],
                                                    cp_ps[:, 0:1]),
                     waits=[c_te], inc=[c_dve])
                P.op("dve", lambda e: e.tensor_scalar(
                    dbadj[:, :], self.dbiasb[:, :], crep[:, 0:1], None,
                    ALU.add), waits=[c_dve], inc=[c_dve])
                cprep = c_dve.snap()

                for b in range(3):
                    P.op("te", lambda e, b=b: e.matmul(
                        o_ps[b][:, :], self.identb[:, :], zbuf[:, :],
                        start=True, stop=False),
                        waits=[ones_set, o_free], inc=[c_te])
                qoff = h * TOWN

                def emit_job(pp, s_pieces, width, bias_ap, av_list,
                             tri=False):
                    for dst_off, w, lf, rf in s_pieces:
                        P.op("te", lambda e, dst_off=dst_off, w=w, lf=lf,
                             rf=rf, pp=pp: e.matmul(
                                 s_ps[pp][:, dst_off:dst_off + w],
                                 lf(), rf(), start=True, stop=True),
                             waits=[ready, cprep, exp_done[pp]], inc=[c_te])
                    smm = self.c_te.snap()
                    extra = []
                    if tri:
                        tri_prev = smm
                        for j in range(JT):
                            P.op("dve", lambda e, j=j, pp=pp: e.tensor_tensor(
                                s_ps[pp][:, j * 128:(j + 1) * 128],
                                s_ps[pp][:, j * 128:(j + 1) * 128],
                                self.trib[:, :], ALU.add),
                                waits=[tri_prev], inc=[c_dve])
                            tri_prev = c_dve.snap()
                        extra = [tri_prev]
                    P.op("act", lambda e, pp=pp, width=width,
                         bias_ap=bias_ap: e.activation(
                             pt[pp][:, 0:width], s_ps[pp][:, 0:width],
                             AF.Exp, bias=bias_ap, scale=SCALE_ATT),
                         waits=[smm, av_done[pp]] + extra, inc=[c_act])
                    exp_done[pp] = c_act.snap()
                    for j, pt_off, vf in av_list:
                        P.op("te", lambda e, j=j, pt_off=pt_off, vf=vf,
                             pp=pp: e.matmul(
                                 o_tile(j), pt[pp][:, pt_off:pt_off + 128],
                                 vf(), start=False, stop=False),
                             waits=[exp_done[pp]], inc=[c_te])
                    av_done[pp] = self.c_te.snap()

                for kt in range(56):
                    jmin = kt // 8 + 1
                    width = (8 - jmin) * 128
                    pieces = []
                    off = 0
                    while off < width:
                        w = min(512, width - off)
                        lf = (lambda bi=bi, kt=kt:
                              kbuf[bi][:, kt * 128:(kt + 1) * 128])
                        q0 = qoff + jmin * 128 + off
                        rf = (lambda q0=q0, w=w: self.q_sb[:, q0:q0 + w])
                        pieces.append((off, w, lf, rf))
                        off += w
                    avs = [(j, (j - jmin) * 128,
                            (lambda bi=bi, kt=kt:
                             vbuf[bi][:, kt * 129:(kt + 1) * 129]))
                           for j in range(jmin, JT)]
                    emit_job(pp, pieces, width, crep[:, 0:1], avs)
                    pp = 1 - pp
                for dd in range(8):
                    pieces = []
                    for j in range(JT):
                        kk = 8 * j + dd
                        lf = (lambda bi=bi, kk=kk:
                              kbuf[bi][:, kk * 128:(kk + 1) * 128])
                        q0 = qoff + j * 128
                        rf = (lambda q0=q0: self.q_sb[:, q0:q0 + 128])
                        pieces.append((j * 128, 128, lf, rf))
                    avs = [(j, j * 128,
                            (lambda bi=bi, kk=8 * j + dd:
                             vbuf[bi][:, kk * 129:(kk + 1) * 129]))
                           for j in range(JT)]
                    emit_job(pp, pieces, 1024, dbadj[:, dd:dd + 1], avs)
                    pp = 1 - pp
                pieces = []
                for j in range(JT):
                    k0 = gi * TOWN + j * 128
                    lf = (lambda k0=k0: self.kown[:, k0:k0 + 128])
                    q0 = qoff + j * 128
                    rf = (lambda q0=q0: self.q_sb[:, q0:q0 + 128])
                    pieces.append((j * 128, 128, lf, rf))
                avs = [(j, j * 128,
                        (lambda q=gi * JT + j:
                         self.vown[:, q * 129:(q + 1) * 129]))
                       for j in range(JT)]
                emit_job(pp, pieces, 1024, crep[:, 0:1], avs, tri=True)
                all_av = self.c_te.snap()
                pp = 1 - pp

                eprev = None
                for j in range(JT):
                    P.op("dve", lambda e, j=j: e.reciprocal(
                        rcpb[:, j:j + 1], o_tile(j)[:, 128:129]),
                        waits=[all_av, eprev], inc=[c_dve])
                    eprev = c_dve.snap()
                    P.op("dve", lambda e, j=j: e.tensor_scalar(
                        sst[:, j * 128:(j + 1) * 128], o_tile(j)[:, 0:128],
                        rcpb[:, j:j + 1], None, ALU.mult),
                        waits=[eprev, sst_free], inc=[c_dve])
                    eprev = c_dve.snap()
                head_eplg = eprev
                P.op("sp", lambda e, h=h: e.dma_start(
                    out=self.s_hbm.rearrange("(j p) e -> p j e", p=128)
                    [:, :, h * 128:(h + 1) * 128],
                    in_=sst[:, :].rearrange("p (j d) -> p j d", j=JT)),
                    waits=[eprev], inc=[c_sp])
                sst_free = self.c_sp.snap()
            group_done[bi] = self.c_te.snap()

        if DEBUG:
            fin = self.c_sp.snap()
            self.dbg_s = self.dram_out("dbg_s", [TOWN, E], F32)
            P.op("sp", lambda e: e.dma_start(out=self.dbg_s, in_=self.s_hbm),
                 waits=[fin], inc=[c_sp])

    def phase3(self, es):
        P = self.P
        c_sp, c_gp, c_te, c_act, c_dve = (self.c_sp, self.c_gp, self.c_te,
                                          self.c_act, self.c_dve)
        s = self.sb
        acc = [s(es, f"acc{i}", [128, JT * 512], F32) for i in (0, 1)]

        with ExitStack() as pa:
            rw1b = [self.sb(pa, f"rw1b{i}", [128, 4 * HID], BF16)
                    for i in (0, 1)]
            rwgb = [self.sb(pa, f"rwgb{i}", [128, 4 * HID], BF16)
                    for i in (0, 1)]
            rw2b = [self.sb(pa, f"rw2b{i}", [128, 8 * LAT], BF16)
                    for i in (0, 1)]
            abT = self.sb(pa, "abT", [128, 8 * TOWN], BF16)
            rb = self.sb(pa, "rb3", [128, 1024], BF16)
            rb2 = self.sb(pa, "rb23", [128, 1024], BF16)
            a_ps = self.ps(pa, "a_ps", [128, 1024])
            b_ps = self.ps(pa, "b_ps", [128, 1024])
            y_ps = [self.ps(pa, f"y_ps{i}", [128, 512]) for i in (0, 1)]

            xlT_view = self.xlT[:, :].rearrange("p (l n) -> p l n", l=4)
            P.op("dve", lambda e: e.memset(acc[0][:, :], 0.0), inc=[c_dve])
            acc_zero = c_dve.snap()

            w_free = [None, None]
            ab_free = None
            y_free = [None, None]
            for ex in range(NEXP_RUN):
                wi = ex % 2
                P.op("gp", lambda e, ex=ex, wi=wi: e.dma_start(
                    out=rw1b[wi][:, :].rearrange("p (l n) -> p l n", l=4),
                    in_=self.rw1T_in[ex].rearrange("(l p) n -> p l n", p=128)),
                    waits=[w_free[wi]], inc=[c_gp])
                P.op("gp", lambda e, ex=ex, wi=wi: e.dma_start(
                    out=rwgb[wi][:, :].rearrange("p (l n) -> p l n", l=4),
                    in_=self.rwgT_in[ex].rearrange("(l p) n -> p l n", p=128)),
                    waits=[c_gp], inc=[c_gp])
                P.op("gp", lambda e, ex=ex, wi=wi: e.dma_start(
                    out=rw2b[wi][:, :].rearrange("p (hh n) -> p hh n", hh=8),
                    in_=self.rw2T_in[ex].rearrange("(hh p) n -> p hh n", p=128)),
                    waits=[c_gp], inc=[c_gp])
                wld = self.c_gp.snap()
                if DEBUG and ex == 0:
                    self.dbg_rwge = self.dram_out("dbg_rwge", [128, 4 * HID], BF16)
                    P.op("sp", lambda e: e.dma_start(out=self.dbg_rwge,
                                                     in_=rwgb[0][:, :]),
                         waits=[wld], inc=[c_sp])

                for hch in range(8):
                    for tc in range(2):
                        for lch in range(4):
                            P.op("te", lambda e, wi=wi, hch=hch, tc=tc,
                                 lch=lch: e.matmul(
                                     a_ps[:, tc * 512:(tc + 1) * 512],
                                     rw1b[wi][:, lch * HID + hch * 128:
                                              lch * HID + (hch + 1) * 128],
                                     xlT_view[:, lch, tc * 512:(tc + 1) * 512],
                                     start=(lch == 0), stop=(lch == 3)),
                                 waits=[wld, ab_free], inc=[c_te])
                            P.op("te", lambda e, wi=wi, hch=hch, tc=tc,
                                 lch=lch: e.matmul(
                                     b_ps[:, tc * 512:(tc + 1) * 512],
                                     rwgb[wi][:, lch * HID + hch * 128:
                                              lch * HID + (hch + 1) * 128],
                                     xlT_view[:, lch, tc * 512:(tc + 1) * 512],
                                     start=(lch == 0), stop=(lch == 3)),
                                 waits=[wld, ab_free], inc=[c_te])
                    abmm = self.c_te.snap()
                    if DEBUG and ex == 0 and hch == 7:
                        self.dbg_bps = self.dram_out("dbg_bps", [128, 1024], F32)
                        self.dbg_aps = self.dram_out("dbg_aps", [128, 1024], F32)
                        dbb = self.sb(pa, "dbb", [128, 1024], F32)
                        dba = self.sb(pa, "dba", [128, 1024], F32)
                        P.op("dve", lambda e: e.tensor_copy(dbb[:, :], b_ps[:, :]),
                             waits=[abmm], inc=[c_dve])
                        P.op("dve", lambda e: e.tensor_copy(dba[:, :], a_ps[:, :]),
                             waits=[abmm, c_dve], inc=[c_dve])
                        dbgc = self.c_dve.snap()
                        P.op("sp", lambda e: e.dma_start(out=self.dbg_bps, in_=dbb[:, :]),
                             waits=[dbgc], inc=[c_sp])
                        P.op("sp", lambda e: e.dma_start(out=self.dbg_aps, in_=dba[:, :]),
                             waits=[dbgc], inc=[c_sp])
                        P.op("sp", lambda e: e.dma_start(
                            out=self.dram_out("dbg_rwg", [128, 4 * HID], BF16),
                            in_=rwgb[0][:, :]), waits=[dbgc], inc=[c_sp])
                    P.op("act", lambda e: e.activation(
                        rb[:, :], b_ps[:, :], AF.Relu),
                        waits=[abmm, c_dve], inc=[c_act])
                    P.op("dve", lambda e: e.tensor_tensor(
                        rb2[:, :], rb[:, :], rb[:, :], ALU.mult),
                        waits=[c_act], inc=[c_dve])
                    P.op("dve", lambda e, hch=hch: e.tensor_tensor(
                        abT[:, hch * TOWN:(hch + 1) * TOWN],
                        a_ps[:, :], rb2[:, :], ALU.mult),
                        waits=[c_dve], inc=[c_dve])
                    ab_free = self.c_dve.snap()
                ab_done = self.c_dve.snap()

                prev_acc = acc[ex % 2]
                next_acc = acc[(ex + 1) % 2]
                for t in range(JT):
                    yi = t % 2
                    for hch in range(8):
                        P.op("te", lambda e, wi=wi, t=t, hch=hch, yi=yi:
                             e.matmul(
                                 y_ps[yi][:, :],
                                 abT[:, hch * TOWN + t * 128:
                                     hch * TOWN + (t + 1) * 128],
                                 rw2b[wi][:, hch * LAT:(hch + 1) * LAT],
                                 start=(hch == 0), stop=(hch == 7)),
                             waits=[ab_done, y_free[yi]], inc=[c_te])
                    ymm = self.c_te.snap()
                    P.op("dve", lambda e, ex=ex, t=t, yi=yi,
                         pa_=prev_acc, na=next_acc: e.scalar_tensor_tensor(
                             na[:, t * 512:(t + 1) * 512],
                             y_ps[yi][:, :],
                             self.wdense[:, t * NEXP + ex:t * NEXP + ex + 1],
                             pa_[:, t * 512:(t + 1) * 512],
                             ALU.mult, ALU.add),
                         waits=[ymm, acc_zero], inc=[c_dve])
                    y_free[yi] = self.c_dve.snap()
                w_free[wi] = self.c_dve.snap()
            acc_done = self.c_dve.snap()
            acc_fin = acc[NEXP_RUN % 2]
            if DEBUG:
                self.dbg_acc = self.dram_out("dbg_acc", [128, JT * 512], F32)
                P.op("sp", lambda e: e.dma_start(out=self.dbg_acc,
                                                 in_=acc_fin[:, :]),
                     waits=[acc_done], inc=[c_sp])
                self.dbg_ab = self.dram_out("dbg_ab", [128, 8 * TOWN], BF16)
                P.op("sp", lambda e: e.dma_start(out=self.dbg_ab,
                                                 in_=abT[:, :]),
                     waits=[acc_done], inc=[c_sp])
                self.dbg_xl2 = self.dram_out("dbg_xl2", [128, 4 * TOWN], BF16)
                P.op("sp", lambda e: e.dma_start(out=self.dbg_xl2,
                                                 in_=self.xlT[:, :]),
                     waits=[acc_done], inc=[c_sp])
                self.dbg_rb = self.dram_out("dbg_rb", [128, HID], BF16)
                P.op("sp", lambda e: e.dma_start(out=self.dbg_rb,
                                                 in_=rb[:, :]),
                     waits=[acc_done], inc=[c_sp])

        with ExitStack() as pb:
            sw2b = self.sb(pb, "sw2b", [128, 8 * E], BF16)
            wupb = self.sb(pb, "wupb", [128, 4 * E], BF16)
            accb = self.sb(pb, "accb", [128, JT * 512], BF16)
            accT = self.sb(pb, "accT", [128, 4 * TOWN], BF16)
            gbuf = self.sb(pb, "gbuf", [128, E], F32)
            stile = self.sb(pb, "stile", [128, E], F32)
            xtile = self.sb(pb, "xtile", [128, E], F32)
            outst = self.sb(pb, "outst", [128, E], F32)
            o4_ps = self.ps(pb, "o4_ps", [128, 2048])
            tp2_ps = self.ps(pb, "tp2_ps", [128, 512], BF16)

            P.op("gp", lambda e: e.dma_start(
                out=sw2b[:, :].rearrange("p (hh n) -> p hh n", hh=8),
                in_=self.sw2T_in.rearrange("(hh p) n -> p hh n", p=128)),
                waits=[acc_done], inc=[c_gp])
            P.op("gp", lambda e: e.dma_start(
                out=wupb[:, :].rearrange("p (l n) -> p l n", l=4),
                in_=self.wupT_in.rearrange("(l p) n -> p l n", p=128)),
                waits=[acc_done], inc=[c_gp])
            wld2 = self.c_gp.snap()

            P.op("dve", lambda e: e.tensor_copy(accb[:, :], acc_fin[:, :]),
                 waits=[acc_done], inc=[c_dve])
            accb_done = self.c_dve.snap()
            tp_free = None
            for t in range(JT):
                for lch in range(4):
                    P.op("te", lambda e, t=t, lch=lch: e.transpose(
                        tp2_ps[:, lch * 128:(lch + 1) * 128],
                        accb[:, t * 512 + lch * 128: t * 512 + (lch + 1) * 128],
                        self.identb[:, :]),
                        waits=[accb_done, tp_free], inc=[c_te])
                tmm = self.c_te.snap()
                P.op("dve", lambda e, t=t: e.tensor_copy(
                    accT[:, :].rearrange("p (l n) -> p l n", l=4)
                    [:, :, t * 128:(t + 1) * 128],
                    tp2_ps[:, :].rearrange("p (l n) -> p l n", l=4)),
                    waits=[tmm], inc=[c_dve])
                tp_free = self.c_dve.snap()
            accT_done = self.c_dve.snap()

            o4_free = None
            st_free = acc_done
            for t in range(JT):
                P.op("sp", lambda e, t=t: e.dma_start(
                    out=stile[:, :],
                    in_=self.s_hbm[t * 128:(t + 1) * 128, :]),
                    waits=[st_free], inc=[c_sp])
                P.op("sp", lambda e, t=t: e.dma_start(
                    out=xtile[:, :],
                    in_=self.x_own[t * 128:(t + 1) * 128, :]),
                    waits=[st_free], inc=[c_sp])
                sx_in = self.c_sp.snap()
                for nch in range(4):
                    for hch in range(8):
                        P.op("te", lambda e, t=t, nch=nch, hch=hch: e.matmul(
                            o4_ps[:, nch * 512:(nch + 1) * 512],
                            self.hhT[:, hch * TOWN + t * 128:
                                     hch * TOWN + (t + 1) * 128],
                            sw2b[:, hch * E + nch * 512:
                                 hch * E + (nch + 1) * 512],
                            start=(hch == 0), stop=False),
                            waits=[wld2, o4_free], inc=[c_te])
                    for lch in range(4):
                        P.op("te", lambda e, t=t, nch=nch, lch=lch: e.matmul(
                            o4_ps[:, nch * 512:(nch + 1) * 512],
                            accT[:, lch * TOWN + t * 128:
                                 lch * TOWN + (t + 1) * 128],
                            wupb[:, lch * E + nch * 512:
                                 lch * E + (nch + 1) * 512],
                            start=False, stop=(lch == 3)),
                            waits=[accT_done, wld2, o4_free], inc=[c_te])
                omm = self.c_te.snap()
                P.op("act", lambda e: e.activation(
                    gbuf[:, :], o4_ps[:, :], AF.Silu),
                    waits=[omm], inc=[c_act])
                o4_free = self.c_act.snap()
                if DEBUG and t == 0:
                    self.dbg_g = self.dram_out("dbg_g", [128, E], F32)
                    P.op("sp", lambda e: e.dma_start(out=self.dbg_g,
                                                     in_=gbuf[:, :]),
                         waits=[o4_free], inc=[c_sp])
                P.op("dve", lambda e: e.tensor_tensor(
                    outst[:, :], stile[:, :], gbuf[:, :], ALU.mult),
                    waits=[c_act, sx_in, st_free], inc=[c_dve])
                P.op("dve", lambda e: e.tensor_tensor(
                    outst[:, :], outst[:, :], xtile[:, :], ALU.add),
                    waits=[c_dve], inc=[c_dve])
                comb = self.c_dve.snap()
                P.op("sp", lambda e, t=t: e.dma_start(
                    out=self.out[t * 128:(t + 1) * 128, :], in_=outst[:, :]),
                    waits=[comb], inc=[c_sp])
                st_free = self.c_sp.snap()

    def finale(self):
        self.P.op("sp", None, waits=[self.c_sp.snap()])



def nc_sem_stack(nc, ctrs):
    class _S:
        def __enter__(self):
            self.es = ExitStack()
            for c in ctrs:
                c.sem = self.es.enter_context(nc.semaphore(c.name))
            return self

        def __exit__(self, *a):
            self.es.close()
            return False
    return _S()


def kernel(**inputs):
    maps = _stage_inputs(inputs)
    b = Builder()
    nc = b.build()
    res = run_bass_kernel_spmd(nc, maps, core_ids=list(range(NCORES)))
    if DEBUG:
        return res
    return _assemble(res.results)


# version 02
# speedup vs baseline: 1.1658x; 1.1658x over previous
import math
from contextlib import ExitStack

import numpy as np
import ml_dtypes

import concourse.bass as bass
import concourse.mybir as mybir
from concourse.bass_utils import run_bass_kernel_spmd

F32 = mybir.dt.float32
BF16 = mybir.dt.bfloat16
AF = mybir.ActivationFunctionType
ALU = mybir.AluOpType
AX = mybir.AxisListType
BF = ml_dtypes.bfloat16

NCORES = 8
T, E, H, G, D = 8192, 2048, 16, 4, 128
HID, LAT, NEXP = 1024, 512, 16
TOWN = T // NCORES
NTILE = T // 128
JT = 8
ECH = E // 128
SCALE_ATT = 1.0 / math.sqrt(D)
NEG = -1.0e30
EPS = 1e-5
RSCALE = 2.5

DEBUG = 0
PHASES = 99
NEXP_RUN = NEXP


class Ctr:
    def __init__(self, name, step=1):
        self.name, self.step, self.val, self.sem = name, step, 0, None

    def bump(self):
        self.val += self.step
        return self.val

    def snap(self):
        return (self, self.val)


class Prog:
    ORDER = ("sp", "gp", "te", "act", "dve")
    QNAME = {"sp": "sync", "gp": "gpsimd", "te": "tensor",
             "act": "scalar", "dve": "vector"}

    def __init__(self):
        self.ops = []

    def op(self, eng, fn, waits=(), inc=()):
        ws = []
        for w in waits:
            if w is None:
                continue
            ws.append((w, w.val) if isinstance(w, Ctr) else w)
        incs = [(c, c.bump()) for c in inc]
        self.ops.append((eng, fn, ws, incs))

    def barrier(self, ctrs):
        snap = [c.snap() for c in ctrs]
        for eng in self.ORDER:
            self.ops.append((eng, None, snap, []))

    def replay(self, block):
        for name in self.ORDER:
            mine = [(fn, ws, incs) for (e, fn, ws, incs) in self.ops
                    if e == name]
            if not mine:
                continue

            def section(eng, mine=mine):
                for fn, ws, incs in mine:
                    for c, v in ws:
                        if v > 0:
                            eng.wait_ge(c.sem, v)
                    if fn is None:
                        continue
                    inst = fn(eng)
                    for c, _ in incs:
                        inst = inst.then_inc(c.sem, c.step)

            getattr(block, self.QNAME[name])(section)


def _stage_inputs(ins):
    x = np.ascontiguousarray(np.asarray(ins["L0_attn_x"]).reshape(T, E))
    asnp = lambda a: np.ascontiguousarray(np.asarray(a))
    base = {
        "x_full": x,
        "wq": asnp(ins["L0_attn_Wq"]), "wk": asnp(ins["L0_attn_Wk"]),
        "wv": asnp(ins["L0_attn_Wv"]),
        "bqT": asnp(np.asarray(ins["L0_attn_bq"]).T),
        "bkT": asnp(np.asarray(ins["L0_attn_bk"]).T),
        "bvrow": asnp(np.asarray(ins["L0_attn_bv"]).reshape(1, G * D)),
        "sw1T": asnp(np.asarray(ins["L0_ffn_sw1"]).T),
        "swgT": asnp(np.asarray(ins["L0_ffn_swg"]).T),
        "sw2T": asnp(np.asarray(ins["L0_ffn_sw2"]).T),
        "wdT": asnp(np.asarray(ins["L0_ffn_Wdown"]).T),
        "wgT": asnp(np.asarray(ins["L0_ffn_Wg"]).T),
        "wupT": asnp(np.asarray(ins["L0_ffn_Wup"]).T),
        "rw1T": asnp(np.asarray(ins["L0_ffn_rw1"]).transpose(0, 2, 1)),
        "rwgT": asnp(np.asarray(ins["L0_ffn_rwg"]).transpose(0, 2, 1)),
        "rw2T": asnp(np.asarray(ins["L0_ffn_rw2"]).transpose(0, 2, 1)),
        "btrow": asnp(np.tile(np.asarray(ins["L0_ffn_biases"]), JT)[None, :]),
        "ident": np.eye(128, dtype=BF),
        "tri": np.where(np.arange(128)[:, None] > np.arange(128)[None, :],
                        np.float32(NEG), np.float32(0.0)).astype(np.float32),
        "eps": np.full((128, 1), EPS, np.float32),
        "onesb": np.ones((1, 128), BF),
        "onesf": np.ones((1, 128), np.float32),
    }
    xt = x.reshape(NTILE, 128, E)
    maps = []
    for c in range(NCORES):
        m = dict(base)
        m["x_own"] = np.ascontiguousarray(xt[c::8].reshape(TOWN, E))
        db = np.zeros((128, JT), np.float32)
        db[:, c:] = NEG
        m["dbias"] = db
        maps.append(m)
    return maps


def _assemble(results):
    out = np.empty((T, E), np.float32)
    ot = out.reshape(NTILE, 128, E)
    for c in range(NCORES):
        ot[c::8] = np.asarray(results[c]["out"]).reshape(JT, 128, E)
    return out.reshape(1, T, E)


class Builder:
    def __init__(self):
        self.nc = bass.Bass("TRN2", target_bir_lowering=False)
        self.P = Prog()
        self.c_sp = Ctr("c_sp", 16)
        self.c_gp = Ctr("c_gp", 16)
        self.c_te = Ctr("c_te", 1)
        self.c_act = Ctr("c_act", 1)
        self.c_dve = Ctr("c_dve", 1)
        self.ctrs = [self.c_sp, self.c_gp, self.c_te, self.c_act, self.c_dve]

    def dram_in(self, name, shape, dtype=F32):
        return self.nc.dram_tensor(name, shape, dtype, kind="ExternalInput").ap()

    def dram_out(self, name, shape, dtype=F32):
        return self.nc.dram_tensor(name, shape, dtype, kind="ExternalOutput").ap()

    def declare(self):
        di = self.dram_in
        self.x_full = di("x_full", [T, E])
        self.x_own = di("x_own", [TOWN, E])
        self.wq_in = di("wq", [H, E, D])
        self.wk_in = di("wk", [G, E, D])
        self.wv_in = di("wv", [G, E, D])
        self.bqT_in = di("bqT", [D, H])
        self.bkT_in = di("bkT", [D, G])
        self.bvrow_in = di("bvrow", [1, G * D])
        self.sw1T_in = di("sw1T", [E, HID])
        self.swgT_in = di("swgT", [E, HID])
        self.sw2T_in = di("sw2T", [HID, E])
        self.wdT_in = di("wdT", [E, LAT])
        self.wgT_in = di("wgT", [E, NEXP])
        self.wupT_in = di("wupT", [LAT, E])
        self.rw1T_in = di("rw1T", [NEXP, LAT, HID])
        self.rwgT_in = di("rwgT", [NEXP, LAT, HID])
        self.rw2T_in = di("rw2T", [NEXP, HID, LAT])
        self.btrow_in = di("btrow", [1, JT * NEXP])
        self.ident_in = di("ident", [128, 128], BF16)
        self.tri_in = di("tri", [128, 128])
        self.eps_in = di("eps", [128, 1])
        self.onesb_in = di("onesb", [1, 128], BF16)
        self.onesf_in = di("onesf", [1, 128])
        self.dbias_in = di("dbias", [128, JT])
        self.out = self.dram_out("out", [TOWN, E])
        self.k_hbm = self.nc.dram_tensor("k_hbm", [G, D, T], BF16).ap()
        self.v_hbm = self.nc.dram_tensor("v_hbm", [G, T, D], BF16).ap()
        self.s_hbm = self.nc.dram_tensor("s_hbm", [TOWN, E], F32).ap()

    def sb(self, es, name, shape, dtype):
        return es.enter_context(self.nc.sbuf_tensor(name, shape, dtype))

    def ps(self, es, name, shape, dtype=F32):
        return es.enter_context(self.nc.psum_tensor(name, shape, dtype))

    def bar(self):
        self.P.barrier(self.ctrs)

    def ln_chunk(self, bufs, src_dma_fn, xnT_write_fn, st):
        P = self.P
        c_sp, c_te, c_act, c_dve = self.c_sp, self.c_te, self.c_act, self.c_dve
        xt, sqs, stats = bufs["xt"], bufs["sqs"], bufs["stats"]
        msq, var4, xnb, tp_ps = (bufs["msq"], bufs["var4"], bufs["xnb"],
                                 bufs["tp_ps"])
        P.op("sp", src_dma_fn, waits=[st.get("xt_free")], inc=[c_sp])
        xin = c_sp.snap()
        for t in range(4):
            P.op("dve", lambda e, t=t: e.reduce_sum(
                stats[:, t:t + 1], xt[:, t * E:(t + 1) * E], axis=AX.X),
                waits=[xin, st.get("stats_free")], inc=[c_dve])
            P.op("act", lambda e, t=t: e.activation(
                sqs[:, 0:E],
                xt[:, t * E:(t + 1) * E], AF.Square,
                accum_out=stats[:, 4 + t:5 + t]),
                waits=[xin, st.get("stats_free")],
                inc=[c_act])
        sums_done = c_dve.snap()
        sqs_done = c_act.snap()
        P.op("dve", lambda e: e.tensor_scalar(
            stats[:, 8:12], stats[:, 0:4], 1.0 / E, None, ALU.mult),
            waits=[sums_done], inc=[c_dve])
        P.op("dve", lambda e: e.tensor_tensor(
            msq[:, :], stats[:, 8:12], stats[:, 8:12], ALU.mult),
            waits=[c_dve], inc=[c_dve])
        P.op("dve", lambda e: e.scalar_tensor_tensor(
            var4[:, :], stats[:, 4:8], 1.0 / E, msq[:, :],
            ALU.mult, ALU.subtract),
            waits=[sqs_done, c_dve], inc=[c_dve])
        P.op("act", lambda e: e.activation(
            var4[:, :], var4[:, :], AF.Ln, bias=self.epsb[:, 0:1]),
            waits=[c_dve], inc=[c_act])
        P.op("act", lambda e: e.activation(
            stats[:, 12:16], var4[:, :], AF.Exp, scale=-0.5),
            waits=[c_act], inc=[c_act])
        rstd_done = c_act.snap()
        for t in range(4):
            P.op("dve", lambda e, t=t: e.tensor_scalar(
                xnb[:, t * E:(t + 1) * E], xt[:, t * E:(t + 1) * E],
                stats[:, 8 + t:9 + t], stats[:, 12 + t:13 + t],
                ALU.subtract, ALU.mult),
                waits=[rstd_done], inc=[c_dve])
        xnb_done = c_dve.snap()
        st["xt_free"] = xnb_done
        st["stats_free"] = xnb_done
        tp_ps2 = bufs["tp_ps2"]
        for b8 in range(8):
            tp = tp_ps if b8 % 2 == 0 else tp_ps2
            for k in range(8):
                idx = b8 * 8 + k
                t, ech = idx // ECH, idx % ECH
                P.op("te", lambda e, t=t, ech=ech, k=k, tp=tp: e.transpose(
                    tp[:, k * 128:(k + 1) * 128],
                    xnb[:, t * E + ech * 128: t * E + (ech + 1) * 128],
                    self.identb[:, :]),
                    waits=[xnb_done, st.get(f"tp_free{b8 % 2}")],
                    inc=([c_te] if k == 7 else []))
            tp_done = c_te.snap()
            t0, ech0 = (b8 * 8) // ECH, (b8 * 8) % ECH
            P.op("dve", lambda e, b8=b8, t0=t0, ech0=ech0:
                 xnT_write_fn(e, b8, t0, ech0),
                 waits=[tp_done, st.get("xnT_free")], inc=[c_dve])
            st[f"tp_free{b8 % 2}"] = c_dve.snap()
        return c_dve.snap()

    def build(self):
        self.declare()
        with ExitStack() as outer:
            with nc_sem_stack(self.nc, self.ctrs):
                s = self.sb
                self.identb = s(outer, "identb", [128, 128], BF16)
                self.trib = s(outer, "trib", [128, 128], F32)
                self.epsb = s(outer, "epsb", [128, 1], F32)
                self.onesbb = s(outer, "onesbb", [1, 128], BF16)
                self.onesfb = s(outer, "onesfb", [1, 128], F32)
                self.onescol = s(outer, "onescol", [128, 1], BF16)
                self.dbiasb = s(outer, "dbiasb", [128, JT], F32)
                self.bqTb = s(outer, "bqTb", [128, H], F32)
                self.bkTb = s(outer, "bkTb", [128, G], F32)
                self.bvrep = s(outer, "bvrep", [128, G * 512], F32)
                self.brep8 = s(outer, "brep8", [128, JT * NEXP], F32)
                self.wdense = s(outer, "wdense", [128, JT * NEXP], F32)

                with self.nc.Block() as block:
                    with ExitStack() as p0:
                        self.phase0(p0)
                    self.bar()
                    if PHASES >= 1:
                        with ExitStack() as p1:
                            self.phase1(p1)
                        self.bar()
                    if PHASES >= 2:
                        self.hhT = s(outer, "hhT", [128, 8 * TOWN], BF16)
                        self.xlT = s(outer, "xlT", [128, 4 * TOWN], BF16)
                        with ExitStack() as mid:
                            self.xnT_own = s(mid, "xnT_own",
                                             [128, ECH * TOWN], BF16)
                            self.q_sb = s(mid, "q_sb", [128, H * TOWN], BF16)
                            self.kown = s(mid, "kown", [128, G * TOWN], BF16)
                            self.vown = s(mid, "vown", [128, G * JT * 129],
                                          BF16)
                            with ExitStack() as p1b:
                                self.phase1b(p1b)
                            self.bar()
                            if PHASES >= 3:
                                with ExitStack() as p15:
                                    self.phase15(p15)
                                self.bar()
                            if PHASES >= 4:
                                with ExitStack() as p2:
                                    self.phase2(p2)
                                self.bar()
                        if PHASES >= 5:
                            with ExitStack() as p3:
                                self.phase3(p3)
                            self.bar()
                    self.finale()
                    self.P.replay(block)
        return self.nc

    def phase0(self, es):
        P = self.P
        c_sp, c_te, c_dve = self.c_sp, self.c_te, self.c_dve
        for dst, src in [
            (self.identb, self.ident_in), (self.trib, self.tri_in),
            (self.epsb, self.eps_in), (self.onesbb, self.onesb_in), (self.onesfb, self.onesf_in),
            (self.dbiasb, self.dbias_in), (self.bqTb, self.bqT_in),
            (self.bkTb, self.bkT_in),
        ]:
            P.op("sp", lambda e, d=dst, s=src: e.dma_start(out=d[:, :], in_=s),
                 inc=[c_sp])
        bvrow = self.sb(es, "bvrow_s", [1, G * D], F32)
        btrow = self.sb(es, "btrow_s", [1, JT * NEXP], F32)
        P.op("sp", lambda e: e.dma_start(out=bvrow[:, :], in_=self.bvrow_in),
             inc=[c_sp])
        P.op("sp", lambda e: e.dma_start(out=btrow[:, :], in_=self.btrow_in),
             inc=[c_sp])
        ld = c_sp.snap()
        P.op("dve", lambda e: e.memset(self.onescol[:, :], 1.0), inc=[c_dve])
        rep_ps = self.ps(es, "rep_ps", [128, G * D])
        P.op("te", lambda e: e.matmul(rep_ps[:, :], self.onesfb[0:1, :],
                                      bvrow[0:1, :], start=True, stop=True),
             waits=[ld], inc=[c_te])
        mm1 = self.c_te.snap()
        for gi in range(G):
            for r in range(4):
                P.op("dve", lambda e, gi=gi, r=r: e.tensor_copy(
                    self.bvrep[:, gi * 512 + r * 128: gi * 512 + (r + 1) * 128],
                    rep_ps[:, gi * 128:(gi + 1) * 128]),
                    waits=[mm1], inc=[c_dve])
        done_bv = c_dve.snap()
        P.op("te", lambda e: e.matmul(rep_ps[:, 0:JT * NEXP],
                                      self.onesfb[0:1, :], btrow[0:1, :],
                                      start=True, stop=True),
             waits=[ld, done_bv], inc=[c_te])
        mm2 = self.c_te.snap()
        P.op("dve", lambda e: e.tensor_copy(self.brep8[:, :],
                                            rep_ps[:, 0:JT * NEXP]),
             waits=[mm2], inc=[c_dve])

    def phase1(self, es):
        P = self.P
        c_sp, c_gp, c_te, c_act, c_dve = (self.c_sp, self.c_gp, self.c_te,
                                          self.c_act, self.c_dve)
        s = self.sb
        wkb = s(es, "wkb", [128, G * ECH * 128], BF16)
        wvb = s(es, "wvb", [128, G * ECH * 128], BF16)
        bufs = {
            "xt": s(es, "xt", [128, 4 * E], F32),
            "sqs": s(es, "sqs", [128, E], F32),
            "stats": s(es, "stats", [128, 16], F32),
            "msq": s(es, "msq", [128, 4], F32),
            "var4": s(es, "var4", [128, 4], F32),
            "xnb": s(es, "xnb", [128, 4 * E], BF16),
            "tp_ps": self.ps(es, "tp_ps", [128, 1024], BF16),
            "tp_ps2": self.ps(es, "tp_ps2", [128, 1024], BF16),
        }
        xnT = [s(es, f"xnT{i}", [128, ECH * 512], BF16) for i in (0, 1)]
        kst = [s(es, f"kst{i}", [128, G * 512], BF16) for i in (0, 1)]
        vst = [s(es, f"vst{i}", [128, G * 512], BF16) for i in (0, 1)]
        kv_ps = [self.ps(es, f"kv_ps{i}", [128, 512]) for i in range(4)]

        for gi in range(G):
            P.op("gp", lambda e, gi=gi: e.dma_start(
                out=wkb[:, gi * ECH * 128:(gi + 1) * ECH * 128]
                .rearrange("p (c d) -> p c d", c=ECH),
                in_=self.wk_in[gi].rearrange("(c p) d -> p c d", p=128)),
                inc=[c_gp])
            P.op("gp", lambda e, gi=gi: e.dma_start(
                out=wvb[:, gi * ECH * 128:(gi + 1) * ECH * 128]
                .rearrange("p (c d) -> p c d", c=ECH),
                in_=self.wv_in[gi].rearrange("(c p) d -> p c d", p=128)),
                inc=[c_gp])
        wload = c_gp.snap()

        st = {}
        kst_free = [None, None]
        vst_free = [None, None]
        xnT_free = [None, None]

        for ch in range(T // 512):
            bi = ch % 2

            def src_dma(e, ch=ch):
                return e.dma_start(
                    out=bufs["xt"][:, :].rearrange("p (t e) -> p t e", t=4),
                    in_=self.x_full[512 * ch:512 * (ch + 1), :]
                    .rearrange("(t p) e -> p t e", p=128))

            def xnT_wr(e, b8, t0, ech0, bi=bi):
                tp = bufs["tp_ps"] if b8 % 2 == 0 else bufs["tp_ps2"]
                return e.tensor_copy(
                    xnT[bi][:, :].rearrange("p (c n) -> p c n", c=ECH)
                    [:, ech0:ech0 + 8, t0 * 128:(t0 + 1) * 128],
                    tp[:, :].rearrange("p (k n) -> p k n", k=8))

            st["xnT_free"] = xnT_free[bi]
            xnT_done = self.ln_chunk(bufs, src_dma, xnT_wr, st)

            for gi in range(G):
                for ech in range(ECH):
                    P.op("te", lambda e, gi=gi, ech=ech, bi=bi: e.matmul(
                        kv_ps[gi][:, :],
                        wkb[:, (gi * ECH + ech) * 128:(gi * ECH + ech + 1) * 128],
                        xnT[bi][:, ech * 512:(ech + 1) * 512],
                        start=(ech == 0), stop=(ech == ECH - 1)),
                        waits=[xnT_done, wload, kst_free[bi], vst_free[bi]],
                        inc=([c_te] if ech == ECH - 1 else []))
            kmm_done = self.c_te.snap()
            for gi in range(G):
                P.op("act", lambda e, gi=gi, bi=bi: e.activation(
                    kst[bi][:, gi * 512:(gi + 1) * 512], kv_ps[gi][:, :],
                    AF.Identity, bias=self.bkTb[:, gi:gi + 1]),
                    waits=[kmm_done], inc=([c_act] if gi == G - 1 else []))
            kst_done = self.c_act.snap()
            for gi in range(G):
                P.op("sp", lambda e, ch=ch, gi=gi, bi=bi: e.dma_start(
                    out=self.k_hbm[gi, :, ch * 512:(ch + 1) * 512],
                    in_=kst[bi][:, gi * 512:(gi + 1) * 512]),
                    waits=[kst_done], inc=[c_sp])
            kst_free[bi] = self.c_sp.snap()

            for gi in range(G):
                for t in range(4):
                    for ech in range(ECH):
                        P.op("te", lambda e, gi=gi, t=t, ech=ech, bi=bi: e.matmul(
                            kv_ps[gi][:, t * 128:(t + 1) * 128],
                            xnT[bi][:, ech * 512 + t * 128:
                                    ech * 512 + (t + 1) * 128],
                            wvb[:, (gi * ECH + ech) * 128:
                                (gi * ECH + ech + 1) * 128],
                            start=(ech == 0), stop=(ech == ECH - 1)),
                            waits=[xnT_done, wload, kst_done, vst_free[bi]],
                            inc=([c_te] if ech == ECH - 1 else []))
            vmm_done = self.c_te.snap()
            xnT_free[bi] = vmm_done
            for gi in range(G):
                P.op("dve", lambda e, gi=gi, bi=bi: e.tensor_tensor(
                    vst[bi][:, gi * 512:(gi + 1) * 512], kv_ps[gi][:, :],
                    self.bvrep[:, gi * 512:(gi + 1) * 512], ALU.add),
                    waits=[vmm_done], inc=([c_dve] if gi == G - 1 else []))
            vst_done = self.c_dve.snap()
            for gi in range(G):
                P.op("sp", lambda e, ch=ch, gi=gi, bi=bi: e.dma_start(
                    out=self.v_hbm[gi, ch * 512:(ch + 1) * 512, :]
                    .rearrange("(t p) d -> p t d", p=128),
                    in_=vst[bi][:, gi * 512:(gi + 1) * 512]
                    .rearrange("p (t d) -> p t d", t=4)),
                    waits=[vst_done], inc=[c_sp])
            vst_free[bi] = self.c_sp.snap()

        if DEBUG:
            fin = self.c_sp.snap()
            self.dbg_khbm = self.dram_out("dbg_khbm", [G, D, T], BF16)
            self.dbg_vhbm = self.dram_out("dbg_vhbm", [G, T, D], BF16)
            P.op("sp", lambda e: e.dma_start(out=self.dbg_khbm,
                                             in_=self.k_hbm), waits=[fin],
                 inc=[c_sp])
            P.op("sp", lambda e: e.dma_start(out=self.dbg_vhbm,
                                             in_=self.v_hbm), waits=[fin],
                 inc=[c_sp])

    def phase1b(self, es):
        P = self.P
        c_sp, c_gp, c_te, c_act, c_dve = (self.c_sp, self.c_gp, self.c_te,
                                          self.c_act, self.c_dve)
        s = self.sb
        wkb = [s(es, f"wkb2{i}", [128, ECH * 128], BF16) for i in (0, 1)]
        wvb = [s(es, f"wvb2{i}", [128, ECH * 128], BF16) for i in (0, 1)]
        wqb = [s(es, f"wqb{i}", [128, ECH * 128], BF16) for i in (0, 1)]
        bufs = {
            "xt": s(es, "xtb", [128, 4 * E], F32),
            "sqs": s(es, "sqsb", [128, E], F32),
            "stats": s(es, "statsb", [128, 16], F32),
            "msq": s(es, "msqb", [128, 4], F32),
            "var4": s(es, "var4b", [128, 4], F32),
            "xnb": s(es, "xnbb", [128, 4 * E], BF16),
            "tp_ps": self.ps(es, "tp_psb", [128, 1024], BF16),
            "tp_ps2": self.ps(es, "tp_psb2", [128, 1024], BF16),
        }
        mm_ps = [self.ps(es, f"mm_psb{i}", [128, 512]) for i in range(4)]

        xnT_view = self.xnT_own[:, :].rearrange("p (c n) -> p c n", c=ECH)
        st = {}
        for ch in range(TOWN // 512):
            def src_dma(e, ch=ch):
                return e.dma_start(
                    out=bufs["xt"][:, :].rearrange("p (t e) -> p t e", t=4),
                    in_=self.x_own[512 * ch:512 * (ch + 1), :]
                    .rearrange("(t p) e -> p t e", p=128))

            def xnT_wr(e, b8, t0, ech0, ch=ch):
                tp = bufs["tp_ps"] if b8 % 2 == 0 else bufs["tp_ps2"]
                return e.tensor_copy(
                    xnT_view[:, ech0:ech0 + 8,
                             ch * 512 + t0 * 128: ch * 512 + (t0 + 1) * 128],
                    tp[:, :].rearrange("p (k n) -> p k n", k=8))

            self.ln_chunk(bufs, src_dma, xnT_wr, st)
        xnT_done = self.c_dve.snap()

        wq_free = [None, None]
        ps_free = [None, None, None, None]
        for h in range(H):
            wi = h % 2
            P.op("gp", lambda e, h=h, wi=wi: e.dma_start(
                out=wqb[wi][:, :].rearrange("p (c d) -> p c d", c=ECH),
                in_=self.wq_in[h].rearrange("(c p) d -> p c d", p=128)),
                waits=[wq_free[wi]], inc=[c_gp])
            wq_ld = c_gp.snap()
            for tc in range(2):
                pi = tc
                for ech in range(ECH):
                    P.op("te", lambda e, wi=wi, ech=ech, tc=tc, pi=pi: e.matmul(
                        mm_ps[pi][:, :],
                        wqb[wi][:, ech * 128:(ech + 1) * 128],
                        xnT_view[:, ech, tc * 512:(tc + 1) * 512],
                        start=(ech == 0), stop=(ech == ECH - 1)),
                        waits=[xnT_done, wq_ld, ps_free[pi]],
                        inc=([c_te] if ech == ECH - 1 else []))
                qmm = self.c_te.snap()
                P.op("act", lambda e, h=h, tc=tc, pi=pi: e.activation(
                    self.q_sb[:, h * TOWN + tc * 512:
                              h * TOWN + (tc + 1) * 512],
                    mm_ps[pi][:, :], AF.Identity, bias=self.bqTb[:, h:h + 1]),
                    waits=[qmm], inc=[c_act])
                ps_free[pi] = self.c_act.snap()
            wq_free[wi] = self.c_act.snap()

        P.op("dve", lambda e: e.memset(
            self.vown[:, :].rearrange("p (q s) -> p q s", s=129)
            [:, :, 128:129], 1.0), inc=[c_dve])
        ones_set = self.c_dve.snap()

        kvw_free = [None, None]
        for gi in range(G):
            wi = gi % 2
            P.op("gp", lambda e, gi=gi, wi=wi: e.dma_start(
                out=wkb[wi][:, :].rearrange("p (c d) -> p c d", c=ECH),
                in_=self.wk_in[gi].rearrange("(c p) d -> p c d", p=128)),
                waits=[kvw_free[wi]], inc=[c_gp])
            P.op("gp", lambda e, gi=gi, wi=wi: e.dma_start(
                out=wvb[wi][:, :].rearrange("p (c d) -> p c d", c=ECH),
                in_=self.wv_in[gi].rearrange("(c p) d -> p c d", p=128)),
                waits=[kvw_free[wi]], inc=[c_gp])
            kv_wload = c_gp.snap()
            for tc in range(2):
                pi = 2 + tc
                for ech in range(ECH):
                    P.op("te", lambda e, wi=wi, tc=tc, ech=ech, pi=pi: e.matmul(
                        mm_ps[pi][:, :],
                        wkb[wi][:, ech * 128:(ech + 1) * 128],
                        xnT_view[:, ech, tc * 512:(tc + 1) * 512],
                        start=(ech == 0), stop=(ech == ECH - 1)),
                        waits=[xnT_done, kv_wload, ps_free[pi]],
                        inc=([c_te] if ech == ECH - 1 else []))
                kmm = self.c_te.snap()
                P.op("act", lambda e, gi=gi, tc=tc, pi=pi: e.activation(
                    self.kown[:, gi * TOWN + tc * 512:
                              gi * TOWN + (tc + 1) * 512],
                    mm_ps[pi][:, :], AF.Identity, bias=self.bkTb[:, gi:gi + 1]),
                    waits=[kmm], inc=[c_act])
                ps_free[pi] = self.c_act.snap()
            for t in range(JT):
                pi = t % 2
                for ech in range(ECH):
                    P.op("te", lambda e, wi=wi, t=t, ech=ech, pi=pi: e.matmul(
                        mm_ps[pi][:, 0:128],
                        xnT_view[:, ech, t * 128:(t + 1) * 128],
                        wvb[wi][:, ech * 128:(ech + 1) * 128],
                        start=(ech == 0), stop=(ech == ECH - 1)),
                        waits=[xnT_done, kv_wload, ps_free[pi]],
                        inc=([c_te] if ech == ECH - 1 else []))
                vmm = self.c_te.snap()
                P.op("dve", lambda e, gi=gi, t=t, pi=pi: e.tensor_tensor(
                    self.vown[:, (gi * JT + t) * 129:
                              (gi * JT + t) * 129 + 128],
                    mm_ps[pi][:, 0:128],
                    self.bvrep[:, gi * 512:gi * 512 + 128], ALU.add),
                    waits=[vmm, ones_set], inc=[c_dve])
                ps_free[pi] = self.c_dve.snap()
            kvw_free[wi] = self.c_dve.snap()

        if DEBUG:
            fin = [self.c_act.snap(), self.c_dve.snap()]
            self.dbg_xnt = self.dram_out("dbg_xnt", [128, ECH * TOWN], BF16)
            self.dbg_q = self.dram_out("dbg_q", [128, H * TOWN], BF16)
            self.dbg_kown = self.dram_out("dbg_kown", [128, G * TOWN], BF16)
            self.dbg_vown = self.dram_out("dbg_vown", [128, G * JT * 129], BF16)
            for dst, src in [(self.dbg_xnt, self.xnT_own),
                             (self.dbg_q, self.q_sb),
                             (self.dbg_kown, self.kown),
                             (self.dbg_vown, self.vown)]:
                P.op("sp", lambda e, d=dst, s=src: e.dma_start(out=d,
                                                               in_=s[:, :]),
                     waits=fin, inc=[c_sp])

    def phase15(self, es):
        P = self.P
        c_sp, c_gp, c_te, c_act, c_dve = (self.c_sp, self.c_gp, self.c_te,
                                          self.c_act, self.c_dve)
        s = self.sb
        sw1b = s(es, "sw1b", [128, ECH * HID], BF16)
        swgb = s(es, "swgb", [128, ECH * HID], BF16)
        wdb = s(es, "wdb", [128, ECH * LAT], BF16)
        wgb = s(es, "wgb", [128, ECH * NEXP], BF16)
        prob = s(es, "prob", [128, JT * NEXP], F32)
        vbuf = s(es, "vbuf_g", [128, JT * NEXP], F32)
        cnt = [s(es, f"cnt{i}", [128, JT * NEXP], F32) for i in (0, 1)]
        rbuf = s(es, "rbuf", [128, HID], BF16)
        rbuf2 = s(es, "rbuf2", [128, HID], BF16)
        rsum = s(es, "rsum", [128, JT], F32)
        z_ps = self.ps(es, "z_ps", [128, 512])
        h_ps = self.ps(es, "h_ps", [128, 1024])
        g_ps = self.ps(es, "g_ps", [128, 1024])

        xnT_view = self.xnT_own[:, :].rearrange("p (c n) -> p c n", c=ECH)

        P.op("gp", lambda e: e.dma_start(
            out=sw1b[:, :].rearrange("p (c n) -> p c n", c=ECH),
            in_=self.sw1T_in.rearrange("(c p) n -> p c n", p=128)),
            inc=[c_gp])
        P.op("gp", lambda e: e.dma_start(
            out=swgb[:, :].rearrange("p (c n) -> p c n", c=ECH),
            in_=self.swgT_in.rearrange("(c p) n -> p c n", p=128)),
            inc=[c_gp])
        P.op("gp", lambda e: e.dma_start(
            out=wdb[:, :].rearrange("p (c n) -> p c n", c=ECH),
            in_=self.wdT_in.rearrange("(c p) n -> p c n", p=128)),
            inc=[c_gp])
        P.op("gp", lambda e: e.dma_start(
            out=wgb[:, :].rearrange("p (c n) -> p c n", c=ECH),
            in_=self.wgT_in.rearrange("(c p) n -> p c n", p=128)),
            inc=[c_gp])
        wload = c_gp.snap()

        for t in range(JT):
            for ech in range(ECH):
                P.op("te", lambda e, t=t, ech=ech: e.matmul(
                    z_ps[:, t * NEXP:(t + 1) * NEXP],
                    xnT_view[:, ech, t * 128:(t + 1) * 128],
                    wgb[:, ech * NEXP:(ech + 1) * NEXP],
                    start=(ech == 0), stop=(ech == ECH - 1)),
                    waits=[wload],
                    inc=([c_te] if ech == ECH - 1 and t == JT - 1 else []))
        zmm = self.c_te.snap()
        P.op("act", lambda e: e.activation(
            prob[:, :], z_ps[:, 0:JT * NEXP], AF.Exp, scale=-1.0),
            waits=[zmm], inc=[c_act])
        P.op("dve", lambda e: e.tensor_scalar(
            prob[:, :], prob[:, :], 1.0, None, ALU.add),
            waits=[c_act], inc=[c_dve])
        P.op("dve", lambda e: e.reciprocal(prob[:, :], prob[:, :]),
            waits=[c_dve], inc=[c_dve])
        P.op("dve", lambda e: e.tensor_tensor(
            vbuf[:, :], prob[:, :], self.brep8[:, :], ALU.add),
            waits=[c_dve], inc=[c_dve])
        P.op("dve", lambda e: e.memset(cnt[0][:, :], 0.0),
            waits=[c_dve], inc=[c_dve])
        cur = 0
        for ep in range(NEXP):
            for t in range(JT):
                P.op("dve", lambda e, ep=ep, t=t, cur=cur: e.scalar_tensor_tensor(
                    cnt[1 - cur][:, t * NEXP:(t + 1) * NEXP],
                    vbuf[:, t * NEXP:(t + 1) * NEXP],
                    vbuf[:, t * NEXP + ep:t * NEXP + ep + 1],
                    cnt[cur][:, t * NEXP:(t + 1) * NEXP],
                    ALU.is_lt, ALU.add),
                    waits=[c_dve], inc=[c_dve])
            cur = 1 - cur
        P.op("dve", lambda e, cur=cur: e.tensor_scalar(
            cnt[1 - cur][:, :], cnt[cur][:, :], 7.5, None, ALU.is_le),
            waits=[c_dve], inc=[c_dve])
        P.op("dve", lambda e, cur=cur: e.tensor_tensor(
            self.wdense[:, :], prob[:, :], cnt[1 - cur][:, :], ALU.mult),
            waits=[c_dve], inc=[c_dve])
        P.op("dve", lambda e: e.reduce_sum(
            rsum[:, :].rearrange("p (j o) -> p j o", o=1),
            self.wdense[:, :].rearrange("p (j n) -> p j n", j=JT), axis=AX.X),
            waits=[c_dve], inc=[c_dve])
        P.op("dve", lambda e: e.reciprocal(rsum[:, :], rsum[:, :]),
            waits=[c_dve], inc=[c_dve])
        for t in range(JT):
            P.op("dve", lambda e, t=t: e.tensor_scalar(
                self.wdense[:, t * NEXP:(t + 1) * NEXP],
                self.wdense[:, t * NEXP:(t + 1) * NEXP],
                rsum[:, t:t + 1], RSCALE, ALU.mult, ALU.mult),
                waits=[c_dve], inc=[c_dve])

        r_free = None
        for hch in range(8):
            for tc in range(2):
                for ech in range(ECH):
                    P.op("te", lambda e, hch=hch, tc=tc, ech=ech: e.matmul(
                        h_ps[:, tc * 512:(tc + 1) * 512],
                        sw1b[:, ech * HID + hch * 128:
                             ech * HID + (hch + 1) * 128],
                        xnT_view[:, ech, tc * 512:(tc + 1) * 512],
                        start=(ech == 0), stop=(ech == ECH - 1)),
                        waits=[wload, c_dve], inc=[])
                    P.op("te", lambda e, hch=hch, tc=tc, ech=ech: e.matmul(
                        g_ps[:, tc * 512:(tc + 1) * 512],
                        swgb[:, ech * HID + hch * 128:
                             ech * HID + (hch + 1) * 128],
                        xnT_view[:, ech, tc * 512:(tc + 1) * 512],
                        start=(ech == 0), stop=(ech == ECH - 1)),
                        waits=[wload, c_dve],
                        inc=([c_te] if ech == ECH - 1 and tc == 1 else []))
            hmm = self.c_te.snap()
            P.op("act", lambda e: e.activation(
                rbuf[:, :], g_ps[:, :], AF.Relu),
                waits=[hmm, r_free], inc=[c_act])
            P.op("dve", lambda e: e.tensor_tensor(
                rbuf2[:, :], rbuf[:, :], rbuf[:, :], ALU.mult),
                waits=[c_act], inc=[c_dve])
            P.op("dve", lambda e, hch=hch: e.tensor_tensor(
                self.hhT[:, hch * TOWN:(hch + 1) * TOWN],
                h_ps[:, :], rbuf2[:, :], ALU.mult),
                waits=[c_dve], inc=[c_dve])
            r_free = self.c_dve.snap()

        for lch in range(4):
            for tc in range(2):
                for ech in range(ECH):
                    P.op("te", lambda e, lch=lch, tc=tc, ech=ech: e.matmul(
                        h_ps[:, tc * 512:(tc + 1) * 512],
                        wdb[:, ech * LAT + lch * 128:
                            ech * LAT + (lch + 1) * 128],
                        xnT_view[:, ech, tc * 512:(tc + 1) * 512],
                        start=(ech == 0), stop=(ech == ECH - 1)),
                        waits=[wload, c_act, c_dve],
                        inc=([c_te] if ech == ECH - 1 and tc == 1 else []))
            xmm = self.c_te.snap()
            P.op("act", lambda e, lch=lch: e.activation(
                self.xlT[:, lch * TOWN:(lch + 1) * TOWN], h_ps[:, :],
                AF.Identity),
                waits=[xmm], inc=[c_act])

        if DEBUG:
            fin = [self.c_act.snap(), self.c_dve.snap()]
            self.dbg_w = self.dram_out("dbg_w", [128, JT * NEXP], F32)
            self.dbg_hh = self.dram_out("dbg_hh", [128, 8 * TOWN], BF16)
            self.dbg_xl = self.dram_out("dbg_xl", [128, 4 * TOWN], BF16)
            for dst, src in [(self.dbg_w, self.wdense), (self.dbg_hh, self.hhT),
                             (self.dbg_xl, self.xlT)]:
                P.op("sp", lambda e, d=dst, s=src: e.dma_start(out=d,
                                                               in_=s[:, :]),
                     waits=fin, inc=[c_sp])

    def phase2(self, es):
        P = self.P
        c_sp, c_te, c_act, c_dve = self.c_sp, self.c_te, self.c_act, self.c_dve
        s = self.sb
        NS = 4
        kbuf = [s(es, f"kbuf{i}", [128, T], BF16) for i in (0, 1)]
        vbuf = [s(es, f"vbuf{i}", [128, 64 * 129], BF16) for i in (0, 1)]
        pt = [s(es, f"pt{i}", [128, 512], BF16) for i in range(NS)]
        sst = s(es, "sst", [128, JT * 128], F32)
        qsq = s(es, "qsqb", [128, 4 * TOWN], BF16)
        ksq = s(es, "ksqb", [128, 512], BF16)
        kmaxb = s(es, "kmaxb", [1, 20], F32)
        cbuf = s(es, "cbuf", [1, 16], F32)
        crep4 = s(es, "crep4", [128, 4], F32)
        dbadj4 = s(es, "dbadj4", [128, 4 * JT], F32)
        rcpb = s(es, "rcpb", [128, JT], F32)
        zbuf = s(es, "zbuf", [128, 512], BF16)
        s_ps = [self.ps(es, f"s_ps{i}", [128, 512]) for i in range(NS)]
        o_ps = [self.ps(es, f"o_ps{i}", [128, 512]) for i in range(3)]
        cp_ps = self.ps(es, "cp_ps", [128, 512])

        def o_tile(j):
            return o_ps[j // 3][:, (j % 3) * 129:(j % 3) * 129 + 129]

        for i in (0, 1):
            P.op("dve", lambda e, i=i: e.memset(
                vbuf[i][:, :].rearrange("p (q s) -> p q s", s=129)
                [:, :, 128:129], 1.0), inc=[c_dve])
        P.op("dve", lambda e: e.memset(zbuf[:, :], 0.0), inc=[c_dve])
        ones_set = c_dve.snap()

        exp_done = [None] * NS
        av_done = [None] * NS
        head_eplg = None
        sst_free = None
        group_done = [None, None]
        slot = 0

        for gi in range(G):
            bi = gi % 2
            P.op("sp", lambda e, gi=gi, bi=bi: e.dma_start(
                out=kbuf[bi][:, :], in_=self.k_hbm[gi]),
                waits=[group_done[bi]], inc=[c_sp])
            P.op("sp", lambda e, gi=gi, bi=bi: e.dma_start(
                out=vbuf[bi][:, :].rearrange("p (q s) -> p q s", s=129)
                [:, :, 0:128],
                in_=self.v_hbm[gi].rearrange("(q p) d -> p q d", p=128)),
                waits=[group_done[bi], ones_set], inc=[c_sp])
            ready = self.c_sp.snap()

            guard = [w for w in exp_done if w] + [ready]
            kprev = [None, None]
            for chk in range(T // 512):
                sl = chk % 2
                P.op("act", lambda e, bi=bi, chk=chk: e.activation(
                    ksq[:, :], kbuf[bi][:, chk * 512:(chk + 1) * 512],
                    AF.Square),
                    waits=guard + [kprev[sl], c_dve], inc=[c_act])
                sq_done = c_act.snap()
                P.op("te", lambda e, sl=sl: e.matmul(
                    s_ps[sl][0:1, 0:512], self.onescol[:, 0:1], ksq[:, :],
                    start=True, stop=True),
                    waits=guard + [sq_done, kprev[sl]], inc=[c_te])
                mm = self.c_te.snap()
                P.op("dve", lambda e, chk=chk, sl=sl: e.reduce_max(
                    kmaxb[0:1, chk + 4:chk + 5], s_ps[sl][0:1, 0:512],
                    axis=AX.X),
                    waits=[mm], inc=[c_dve])
                kprev[sl] = c_dve.snap()
            P.op("dve", lambda e: e.reduce_max(
                kmaxb[0:1, 0:1], kmaxb[0:1, 4:4 + T // 512], axis=AX.X),
                waits=[kprev[0], kprev[1]], inc=[c_dve])
            P.op("act", lambda e: e.activation(
                kmaxb[0:1, 1:2], kmaxb[0:1, 0:1], AF.Ln),
                waits=[c_dve], inc=[c_act])
            lnk_done = c_act.snap()

            P.op("act", lambda e, gi=gi: e.activation(
                qsq[:, :], self.q_sb[:, gi * 4 * TOWN:(gi + 1) * 4 * TOWN],
                AF.Square), waits=[self.c_dve.snap()], inc=[c_act])
            qsq_done = c_act.snap()
            qprev = [None, None]
            for i in range(8):
                sl = i % 2
                P.op("te", lambda e, i=i, sl=sl: e.matmul(
                    s_ps[2 + sl][0:1, 0:512], self.onescol[:, 0:1],
                    qsq[:, i * 512:(i + 1) * 512],
                    start=True, stop=True),
                    waits=guard + [qsq_done, qprev[sl]], inc=[c_te])
                mm = self.c_te.snap()
                P.op("dve", lambda e, i=i, sl=sl: e.reduce_max(
                    cbuf[0:1, i:i + 1], s_ps[2 + sl][0:1, 0:512], axis=AX.X),
                    waits=[mm], inc=[c_dve])
                qprev[sl] = c_dve.snap()
            P.op("dve", lambda e: e.reduce_max(
                cbuf[0:1, 8:12].rearrange("o (h x) -> o h x", h=4),
                cbuf[0:1, 0:8].rearrange("o (h x) -> o h x", h=4),
                axis=AX.X),
                waits=[qprev[0], qprev[1]], inc=[c_dve])
            P.op("act", lambda e: e.activation(
                cbuf[0:1, 12:16], cbuf[0:1, 8:12], AF.Ln),
                waits=[c_dve], inc=[c_act])
            P.op("dve", lambda e: e.tensor_scalar(
                cbuf[0:1, 8:12], cbuf[0:1, 12:16], kmaxb[0:1, 1:2], None,
                ALU.add), waits=[c_act, lnk_done], inc=[c_dve])
            P.op("act", lambda e: e.activation(
                cbuf[0:1, 12:16], cbuf[0:1, 8:12], AF.Exp, scale=0.5),
                waits=[c_dve], inc=[c_act])
            P.op("dve", lambda e: e.tensor_scalar(
                cbuf[0:1, 8:12], cbuf[0:1, 12:16], -SCALE_ATT, None,
                ALU.mult), waits=[c_act], inc=[c_dve])
            P.op("te", lambda e: e.matmul(
                cp_ps[:, 0:4], self.onesfb[0:1, :], cbuf[0:1, 8:12],
                start=True, stop=True), waits=[c_dve], inc=[c_te])
            P.op("dve", lambda e: e.tensor_copy(crep4[:, :], cp_ps[:, 0:4]),
                 waits=[c_te], inc=[c_dve])
            for hh in range(4):
                P.op("dve", lambda e, hh=hh: e.tensor_scalar(
                    dbadj4[:, hh * JT:(hh + 1) * JT], self.dbiasb[:, :],
                    crep4[:, hh:hh + 1], None, ALU.add),
                    waits=[c_dve], inc=[c_dve])
            cprep = c_dve.snap()

            for hh in range(4):
                h = gi * 4 + hh
                o_free = head_eplg
                qoff = h * TOWN
                for b in range(3):
                    P.op("te", lambda e, b=b: e.matmul(
                        o_ps[b][:, :], self.identb[:, :], zbuf[:, :],
                        start=True, stop=False),
                        waits=[ones_set, o_free, cprep],
                        inc=([c_te] if b == 2 else []))

                def emit_job(kind, s_mms, width, bias_ap, av_list):
                    nonlocal slot
                    sl = slot
                    slot = (slot + 1) % NS
                    n = len(s_mms)
                    for i, (dst_off, w, lf, rf) in enumerate(s_mms):
                        P.op("te", lambda e, dst_off=dst_off, w=w, lf=lf,
                             rf=rf, sl=sl: e.matmul(
                                 s_ps[sl][:, dst_off:dst_off + w],
                                 lf(), rf(), start=True, stop=True),
                             waits=[ready, cprep, exp_done[sl]],
                             inc=([c_te] if i == n - 1 else []))
                    smm = self.c_te.snap()
                    extra = []
                    if kind == "tri":
                        tp = smm
                        for i in range(4):
                            P.op("dve", lambda e, i=i, sl=sl: e.tensor_tensor(
                                s_ps[sl][:, i * 128:(i + 1) * 128],
                                s_ps[sl][:, i * 128:(i + 1) * 128],
                                self.trib[:, :], ALU.add),
                                waits=[tp], inc=([c_dve] if i == 3 else []))
                            tp = None
                        tp = c_dve.snap()
                        extra = [tp]
                    P.op("act", lambda e, sl=sl, width=width,
                         bias_ap=bias_ap: e.activation(
                             pt[sl][:, 0:width], s_ps[sl][:, 0:width],
                             AF.Exp, bias=bias_ap, scale=SCALE_ATT),
                         waits=[smm, av_done[sl]] + extra, inc=[c_act])
                    exp_done[sl] = c_act.snap()
                    na = len(av_list)
                    for i, (j, pt_off, vf) in enumerate(av_list):
                        P.op("te", lambda e, j=j, pt_off=pt_off, vf=vf,
                             sl=sl: e.matmul(
                                 o_tile(j), pt[sl][:, pt_off:pt_off + 128],
                                 vf(), start=False, stop=False),
                             waits=[exp_done[sl]],
                             inc=([c_te] if i == na - 1 else []))
                    av_done[sl] = self.c_te.snap()

                for kt in range(56):
                    jmin = kt // 8 + 1
                    w_all = (8 - jmin) * 128
                    off = 0
                    while off < w_all:
                        w = min(512, w_all - off)
                        q0 = qoff + jmin * 128 + off
                        jlo = jmin + off // 128
                        lf = (lambda bi=bi, kt=kt:
                              kbuf[bi][:, kt * 128:(kt + 1) * 128])
                        rf = (lambda q0=q0, w=w: self.q_sb[:, q0:q0 + w])
                        avs = [(j, (j - jlo) * 128,
                                (lambda bi=bi, kt=kt:
                                 vbuf[bi][:, kt * 129:(kt + 1) * 129]))
                               for j in range(jlo, jlo + w // 128)]
                        emit_job("main", [(0, w, lf, rf)], w,
                                 crep4[:, hh:hh + 1], avs)
                        off += w
                for dd in range(8):
                    for half in range(2):
                        smms = []
                        avs = []
                        for i in range(4):
                            j = half * 4 + i
                            kk = 8 * j + dd
                            lf = (lambda bi=bi, kk=kk:
                                  kbuf[bi][:, kk * 128:(kk + 1) * 128])
                            q0 = qoff + j * 128
                            rf = (lambda q0=q0: self.q_sb[:, q0:q0 + 128])
                            smms.append((i * 128, 128, lf, rf))
                            avs.append((j, i * 128,
                                        (lambda bi=bi, kk=kk:
                                         vbuf[bi][:, kk * 129:(kk + 1) * 129])))
                        emit_job("diag", smms, 512,
                                 dbadj4[:, hh * JT + dd:hh * JT + dd + 1], avs)
                for half in range(2):
                    smms = []
                    avs = []
                    for i in range(4):
                        j = half * 4 + i
                        k0 = gi * TOWN + j * 128
                        lf = (lambda k0=k0: self.kown[:, k0:k0 + 128])
                        q0 = qoff + j * 128
                        rf = (lambda q0=q0: self.q_sb[:, q0:q0 + 128])
                        smms.append((i * 128, 128, lf, rf))
                        avs.append((j, i * 128,
                                    (lambda q=gi * JT + j:
                                     self.vown[:, q * 129:(q + 1) * 129])))
                    emit_job("tri", smms, 512, crep4[:, hh:hh + 1], avs)
                all_av = self.c_te.snap()

                for j in range(JT):
                    P.op("dve", lambda e, j=j: e.reciprocal(
                        rcpb[:, j:j + 1], o_tile(j)[:, 128:129]),
                        waits=[all_av, c_dve], inc=[c_dve])
                    P.op("dve", lambda e, j=j: e.tensor_scalar(
                        sst[:, j * 128:(j + 1) * 128], o_tile(j)[:, 0:128],
                        rcpb[:, j:j + 1], None, ALU.mult),
                        waits=[c_dve, sst_free], inc=[c_dve])
                head_eplg = self.c_dve.snap()
                P.op("sp", lambda e, h=h: e.dma_start(
                    out=self.s_hbm.rearrange("(j p) e -> p j e", p=128)
                    [:, :, h * 128:(h + 1) * 128],
                    in_=sst[:, :].rearrange("p (j d) -> p j d", j=JT)),
                    waits=[head_eplg], inc=[c_sp])
                sst_free = self.c_sp.snap()
            group_done[bi] = self.c_te.snap()

        if DEBUG:
            fin = self.c_sp.snap()
            self.dbg_s = self.dram_out("dbg_s", [TOWN, E], F32)
            P.op("sp", lambda e: e.dma_start(out=self.dbg_s, in_=self.s_hbm),
                 waits=[fin], inc=[c_sp])

    def phase3(self, es):
        P = self.P
        c_sp, c_gp, c_te, c_act, c_dve = (self.c_sp, self.c_gp, self.c_te,
                                          self.c_act, self.c_dve)
        s = self.sb
        acc = [s(es, f"acc{i}", [128, JT * 512], F32) for i in (0, 1)]

        with ExitStack() as pa:
            rw1b = [self.sb(pa, f"rw1b{i}", [128, 4 * HID], BF16)
                    for i in (0, 1)]
            rwgb = [self.sb(pa, f"rwgb{i}", [128, 4 * HID], BF16)
                    for i in (0, 1)]
            rw2b = [self.sb(pa, f"rw2b{i}", [128, 8 * LAT], BF16)
                    for i in (0, 1)]
            abT = self.sb(pa, "abT", [128, 8 * TOWN], BF16)
            rb = self.sb(pa, "rb3", [128, 1024], BF16)
            rb2 = self.sb(pa, "rb23", [128, 1024], BF16)
            a_ps = self.ps(pa, "a_ps", [128, 1024])
            b_ps = self.ps(pa, "b_ps", [128, 1024])
            y_ps = [self.ps(pa, f"y_ps{i}", [128, 512]) for i in (0, 1)]

            xlT_view = self.xlT[:, :].rearrange("p (l n) -> p l n", l=4)
            P.op("dve", lambda e: e.memset(acc[0][:, :], 0.0), inc=[c_dve])
            acc_zero = c_dve.snap()

            w_free = [None, None]
            ab_free = None
            y_free = [None, None]
            for ex in range(NEXP_RUN):
                wi = ex % 2
                P.op("gp", lambda e, ex=ex, wi=wi: e.dma_start(
                    out=rw1b[wi][:, :].rearrange("p (l n) -> p l n", l=4),
                    in_=self.rw1T_in[ex].rearrange("(l p) n -> p l n", p=128)),
                    waits=[w_free[wi]], inc=[c_gp])
                P.op("gp", lambda e, ex=ex, wi=wi: e.dma_start(
                    out=rwgb[wi][:, :].rearrange("p (l n) -> p l n", l=4),
                    in_=self.rwgT_in[ex].rearrange("(l p) n -> p l n", p=128)),
                    waits=[c_gp], inc=[c_gp])
                P.op("gp", lambda e, ex=ex, wi=wi: e.dma_start(
                    out=rw2b[wi][:, :].rearrange("p (hh n) -> p hh n", hh=8),
                    in_=self.rw2T_in[ex].rearrange("(hh p) n -> p hh n", p=128)),
                    waits=[c_gp], inc=[c_gp])
                wld = self.c_gp.snap()
                if DEBUG and ex == 0:
                    self.dbg_rwge = self.dram_out("dbg_rwge", [128, 4 * HID], BF16)
                    P.op("sp", lambda e: e.dma_start(out=self.dbg_rwge,
                                                     in_=rwgb[0][:, :]),
                         waits=[wld], inc=[c_sp])

                for hch in range(8):
                    for tc in range(2):
                        for lch in range(4):
                            P.op("te", lambda e, wi=wi, hch=hch, tc=tc,
                                 lch=lch: e.matmul(
                                     a_ps[:, tc * 512:(tc + 1) * 512],
                                     rw1b[wi][:, lch * HID + hch * 128:
                                              lch * HID + (hch + 1) * 128],
                                     xlT_view[:, lch, tc * 512:(tc + 1) * 512],
                                     start=(lch == 0), stop=(lch == 3)),
                                 waits=[wld, ab_free], inc=[])
                            P.op("te", lambda e, wi=wi, hch=hch, tc=tc,
                                 lch=lch: e.matmul(
                                     b_ps[:, tc * 512:(tc + 1) * 512],
                                     rwgb[wi][:, lch * HID + hch * 128:
                                              lch * HID + (hch + 1) * 128],
                                     xlT_view[:, lch, tc * 512:(tc + 1) * 512],
                                     start=(lch == 0), stop=(lch == 3)),
                                 waits=[wld, ab_free],
                                 inc=([c_te] if lch == 3 and tc == 1 else []))
                    abmm = self.c_te.snap()
                    if DEBUG and ex == 0 and hch == 7:
                        self.dbg_bps = self.dram_out("dbg_bps", [128, 1024], F32)
                        self.dbg_aps = self.dram_out("dbg_aps", [128, 1024], F32)
                        dbb = self.sb(pa, "dbb", [128, 1024], F32)
                        dba = self.sb(pa, "dba", [128, 1024], F32)
                        P.op("dve", lambda e: e.tensor_copy(dbb[:, :], b_ps[:, :]),
                             waits=[abmm], inc=[c_dve])
                        P.op("dve", lambda e: e.tensor_copy(dba[:, :], a_ps[:, :]),
                             waits=[abmm, c_dve], inc=[c_dve])
                        dbgc = self.c_dve.snap()
                        P.op("sp", lambda e: e.dma_start(out=self.dbg_bps, in_=dbb[:, :]),
                             waits=[dbgc], inc=[c_sp])
                        P.op("sp", lambda e: e.dma_start(out=self.dbg_aps, in_=dba[:, :]),
                             waits=[dbgc], inc=[c_sp])
                        P.op("sp", lambda e: e.dma_start(
                            out=self.dram_out("dbg_rwg", [128, 4 * HID], BF16),
                            in_=rwgb[0][:, :]), waits=[dbgc], inc=[c_sp])
                    P.op("act", lambda e: e.activation(
                        rb[:, :], b_ps[:, :], AF.Relu),
                        waits=[abmm, c_dve], inc=[c_act])
                    P.op("dve", lambda e: e.tensor_tensor(
                        rb2[:, :], rb[:, :], rb[:, :], ALU.mult),
                        waits=[c_act], inc=[c_dve])
                    P.op("dve", lambda e, hch=hch: e.tensor_tensor(
                        abT[:, hch * TOWN:(hch + 1) * TOWN],
                        a_ps[:, :], rb2[:, :], ALU.mult),
                        waits=[c_dve], inc=[c_dve])
                    ab_free = self.c_dve.snap()
                ab_done = self.c_dve.snap()

                prev_acc = acc[ex % 2]
                next_acc = acc[(ex + 1) % 2]
                for t in range(JT):
                    yi = t % 2
                    for hch in range(8):
                        P.op("te", lambda e, wi=wi, t=t, hch=hch, yi=yi:
                             e.matmul(
                                 y_ps[yi][:, :],
                                 abT[:, hch * TOWN + t * 128:
                                     hch * TOWN + (t + 1) * 128],
                                 rw2b[wi][:, hch * LAT:(hch + 1) * LAT],
                                 start=(hch == 0), stop=(hch == 7)),
                             waits=[ab_done, y_free[yi]],
                             inc=([c_te] if hch == 7 else []))
                    ymm = self.c_te.snap()
                    P.op("dve", lambda e, ex=ex, t=t, yi=yi,
                         pa_=prev_acc, na=next_acc: e.scalar_tensor_tensor(
                             na[:, t * 512:(t + 1) * 512],
                             y_ps[yi][:, :],
                             self.wdense[:, t * NEXP + ex:t * NEXP + ex + 1],
                             pa_[:, t * 512:(t + 1) * 512],
                             ALU.mult, ALU.add),
                         waits=[ymm, acc_zero], inc=[c_dve])
                    y_free[yi] = self.c_dve.snap()
                w_free[wi] = self.c_dve.snap()
            acc_done = self.c_dve.snap()
            acc_fin = acc[NEXP_RUN % 2]
            if DEBUG:
                self.dbg_acc = self.dram_out("dbg_acc", [128, JT * 512], F32)
                P.op("sp", lambda e: e.dma_start(out=self.dbg_acc,
                                                 in_=acc_fin[:, :]),
                     waits=[acc_done], inc=[c_sp])
                self.dbg_ab = self.dram_out("dbg_ab", [128, 8 * TOWN], BF16)
                P.op("sp", lambda e: e.dma_start(out=self.dbg_ab,
                                                 in_=abT[:, :]),
                     waits=[acc_done], inc=[c_sp])
                self.dbg_xl2 = self.dram_out("dbg_xl2", [128, 4 * TOWN], BF16)
                P.op("sp", lambda e: e.dma_start(out=self.dbg_xl2,
                                                 in_=self.xlT[:, :]),
                     waits=[acc_done], inc=[c_sp])
                self.dbg_rb = self.dram_out("dbg_rb", [128, HID], BF16)
                P.op("sp", lambda e: e.dma_start(out=self.dbg_rb,
                                                 in_=rb[:, :]),
                     waits=[acc_done], inc=[c_sp])

        with ExitStack() as pb:
            sw2b = self.sb(pb, "sw2b", [128, 8 * E], BF16)
            wupb = self.sb(pb, "wupb", [128, 4 * E], BF16)
            accb = self.sb(pb, "accb", [128, JT * 512], BF16)
            accT = self.sb(pb, "accT", [128, 4 * TOWN], BF16)
            gbuf = self.sb(pb, "gbuf", [128, E], F32)
            stile = self.sb(pb, "stile", [128, E], F32)
            xtile = self.sb(pb, "xtile", [128, E], F32)
            outst = self.sb(pb, "outst", [128, E], F32)
            o4_ps = self.ps(pb, "o4_ps", [128, 2048])
            tp2_ps = self.ps(pb, "tp2_ps", [128, 512], BF16)

            P.op("gp", lambda e: e.dma_start(
                out=sw2b[:, :].rearrange("p (hh n) -> p hh n", hh=8),
                in_=self.sw2T_in.rearrange("(hh p) n -> p hh n", p=128)),
                waits=[acc_done], inc=[c_gp])
            P.op("gp", lambda e: e.dma_start(
                out=wupb[:, :].rearrange("p (l n) -> p l n", l=4),
                in_=self.wupT_in.rearrange("(l p) n -> p l n", p=128)),
                waits=[acc_done], inc=[c_gp])
            wld2 = self.c_gp.snap()

            P.op("dve", lambda e: e.tensor_copy(accb[:, :], acc_fin[:, :]),
                 waits=[acc_done], inc=[c_dve])
            accb_done = self.c_dve.snap()
            tp_free = None
            for t in range(JT):
                for lch in range(4):
                    P.op("te", lambda e, t=t, lch=lch: e.transpose(
                        tp2_ps[:, lch * 128:(lch + 1) * 128],
                        accb[:, t * 512 + lch * 128: t * 512 + (lch + 1) * 128],
                        self.identb[:, :]),
                        waits=[accb_done, tp_free],
                        inc=([c_te] if lch == 3 else []))
                tmm = self.c_te.snap()
                P.op("dve", lambda e, t=t: e.tensor_copy(
                    accT[:, :].rearrange("p (l n) -> p l n", l=4)
                    [:, :, t * 128:(t + 1) * 128],
                    tp2_ps[:, :].rearrange("p (l n) -> p l n", l=4)),
                    waits=[tmm], inc=[c_dve])
                tp_free = self.c_dve.snap()
            accT_done = self.c_dve.snap()

            o4_free = None
            st_free = acc_done
            for t in range(JT):
                P.op("sp", lambda e, t=t: e.dma_start(
                    out=stile[:, :],
                    in_=self.s_hbm[t * 128:(t + 1) * 128, :]),
                    waits=[st_free], inc=[c_sp])
                P.op("sp", lambda e, t=t: e.dma_start(
                    out=xtile[:, :],
                    in_=self.x_own[t * 128:(t + 1) * 128, :]),
                    waits=[st_free], inc=[c_sp])
                sx_in = self.c_sp.snap()
                for nch in range(4):
                    for hch in range(8):
                        P.op("te", lambda e, t=t, nch=nch, hch=hch: e.matmul(
                            o4_ps[:, nch * 512:(nch + 1) * 512],
                            self.hhT[:, hch * TOWN + t * 128:
                                     hch * TOWN + (t + 1) * 128],
                            sw2b[:, hch * E + nch * 512:
                                 hch * E + (nch + 1) * 512],
                            start=(hch == 0), stop=False),
                            waits=[wld2, o4_free], inc=[])
                    for lch in range(4):
                        P.op("te", lambda e, t=t, nch=nch, lch=lch: e.matmul(
                            o4_ps[:, nch * 512:(nch + 1) * 512],
                            accT[:, lch * TOWN + t * 128:
                                 lch * TOWN + (t + 1) * 128],
                            wupb[:, lch * E + nch * 512:
                                 lch * E + (nch + 1) * 512],
                            start=False, stop=(lch == 3)),
                            waits=[accT_done, wld2, o4_free],
                            inc=([c_te] if lch == 3 and nch == 3 else []))
                omm = self.c_te.snap()
                P.op("act", lambda e: e.activation(
                    gbuf[:, :], o4_ps[:, :], AF.Silu),
                    waits=[omm], inc=[c_act])
                o4_free = self.c_act.snap()
                if DEBUG and t == 0:
                    self.dbg_g = self.dram_out("dbg_g", [128, E], F32)
                    P.op("sp", lambda e: e.dma_start(out=self.dbg_g,
                                                     in_=gbuf[:, :]),
                         waits=[o4_free], inc=[c_sp])
                P.op("dve", lambda e: e.tensor_tensor(
                    outst[:, :], stile[:, :], gbuf[:, :], ALU.mult),
                    waits=[c_act, sx_in, st_free], inc=[c_dve])
                P.op("dve", lambda e: e.tensor_tensor(
                    outst[:, :], outst[:, :], xtile[:, :], ALU.add),
                    waits=[c_dve], inc=[c_dve])
                comb = self.c_dve.snap()
                P.op("sp", lambda e, t=t: e.dma_start(
                    out=self.out[t * 128:(t + 1) * 128, :], in_=outst[:, :]),
                    waits=[comb], inc=[c_sp])
                st_free = self.c_sp.snap()

    def finale(self):
        self.P.op("sp", None, waits=[self.c_sp.snap()])



def nc_sem_stack(nc, ctrs):
    class _S:
        def __enter__(self):
            self.es = ExitStack()
            for c in ctrs:
                c.sem = self.es.enter_context(nc.semaphore(c.name))
            return self

        def __exit__(self, *a):
            self.es.close()
            return False
    return _S()


def kernel(**inputs):
    maps = _stage_inputs(inputs)
    b = Builder()
    nc = b.build()
    res = run_bass_kernel_spmd(nc, maps, core_ids=list(range(NCORES)))
    if DEBUG:
        return res
    return _assemble(res.results)


# version 04
# speedup vs baseline: 1.5779x; 1.3535x over previous
import math
from contextlib import ExitStack

import numpy as np
import ml_dtypes

import concourse.bass as bass
import concourse.mybir as mybir
from concourse.bass_utils import run_bass_kernel_spmd

F32 = mybir.dt.float32
BF16 = mybir.dt.bfloat16
AF = mybir.ActivationFunctionType
ALU = mybir.AluOpType
AX = mybir.AxisListType
BF = ml_dtypes.bfloat16

NCORES = 8
T, E, H, G, D = 8192, 2048, 16, 4, 128
HID, LAT, NEXP = 1024, 512, 16
TOWN = T // NCORES
NTILE = T // 128
JT = 8
ECH = E // 128
SCALE_ATT = 1.0 / math.sqrt(D)
NEG = -1.0e30
EPS = 1e-5
RSCALE = 2.5

DEBUG = 0
PHASES = 99
NEXP_RUN = NEXP


class Ctr:
    def __init__(self, name, step=1):
        self.name, self.step, self.val, self.sem = name, step, 0, None

    def bump(self):
        self.val += self.step
        return self.val

    def snap(self):
        return (self, self.val)


class Prog:
    ORDER = ("sp", "gp", "te", "act", "dve")
    QNAME = {"sp": "sync", "gp": "gpsimd", "te": "tensor",
             "act": "scalar", "dve": "vector"}

    def __init__(self):
        self.ops = []

    def op(self, eng, fn, waits=(), inc=()):
        ws = []
        for w in waits:
            if w is None:
                continue
            ws.append((w, w.val) if isinstance(w, Ctr) else w)
        incs = [(c, c.bump()) for c in inc]
        self.ops.append((eng, fn, ws, incs))

    def barrier(self, ctrs):
        snap = [c.snap() for c in ctrs]
        for eng in self.ORDER:
            self.ops.append((eng, None, snap, []))

    def replay(self, block):
        for name in self.ORDER:
            mine = [(fn, ws, incs) for (e, fn, ws, incs) in self.ops
                    if e == name]
            if not mine:
                continue

            def section(eng, mine=mine):
                for fn, ws, incs in mine:
                    for c, v in ws:
                        if v > 0:
                            eng.wait_ge(c.sem, v)
                    if fn is None:
                        continue
                    inst = fn(eng)
                    for c, _ in incs:
                        inst = inst.then_inc(c.sem, c.step)

            getattr(block, self.QNAME[name])(section)


def _stage_inputs(ins):
    x = np.ascontiguousarray(np.asarray(ins["L0_attn_x"]).reshape(T, E))
    asnp = lambda a: np.ascontiguousarray(np.asarray(a))
    base = {
        "x_full": x,
        "wq": asnp(ins["L0_attn_Wq"]), "wk": asnp(ins["L0_attn_Wk"]),
        "wv": asnp(ins["L0_attn_Wv"]),
        "bqT": asnp(np.asarray(ins["L0_attn_bq"]).T),
        "bkT": asnp(np.asarray(ins["L0_attn_bk"]).T),
        "bvT": asnp(np.asarray(ins["L0_attn_bv"]).T),
        "sw1T": asnp(np.asarray(ins["L0_ffn_sw1"]).T),
        "swgT": asnp(np.asarray(ins["L0_ffn_swg"]).T),
        "sw2T": asnp(np.asarray(ins["L0_ffn_sw2"]).T),
        "wdT": asnp(np.asarray(ins["L0_ffn_Wdown"]).T),
        "wgT": asnp(np.asarray(ins["L0_ffn_Wg"]).T),
        "wupT": asnp(np.asarray(ins["L0_ffn_Wup"]).T),
        "rw1T": asnp(np.asarray(ins["L0_ffn_rw1"]).transpose(0, 2, 1)),
        "rwgT": asnp(np.asarray(ins["L0_ffn_rwg"]).transpose(0, 2, 1)),
        "rw2T": asnp(np.asarray(ins["L0_ffn_rw2"]).transpose(0, 2, 1)),
        "btrow": asnp(np.tile(np.asarray(ins["L0_ffn_biases"]), JT)[None, :]),
        "ident": np.eye(128, dtype=BF),
        "tri": np.where(np.arange(128)[:, None] > np.arange(128)[None, :],
                        np.float32(NEG), np.float32(0.0)).astype(np.float32),
        "eps": np.full((128, 1), EPS, np.float32),
        "onesb": np.ones((1, 128), BF),
        "onesf": np.ones((1, 128), np.float32),
    }
    xt = x.reshape(NTILE, 128, E)
    maps = []
    for c in range(NCORES):
        m = dict(base)
        m["x_own"] = np.ascontiguousarray(xt[c::8].reshape(TOWN, E))
        db = np.zeros((128, JT), np.float32)
        db[:, c:] = NEG
        m["dbias"] = db
        maps.append(m)
    return maps


def _assemble(results):
    out = np.empty((T, E), np.float32)
    ot = out.reshape(NTILE, 128, E)
    for c in range(NCORES):
        ot[c::8] = np.asarray(results[c]["out"]).reshape(JT, 128, E)
    return out.reshape(1, T, E)


class Builder:
    def __init__(self):
        self.nc = bass.Bass("TRN2", target_bir_lowering=False)
        self.P = Prog()
        self.c_sp = Ctr("c_sp", 16)
        self.c_gp = Ctr("c_gp", 16)
        self.c_te = Ctr("c_te", 1)
        self.c_act = Ctr("c_act", 1)
        self.c_dve = Ctr("c_dve", 1)
        self.ctrs = [self.c_sp, self.c_gp, self.c_te, self.c_act, self.c_dve]

    def dram_in(self, name, shape, dtype=F32):
        return self.nc.dram_tensor(name, shape, dtype, kind="ExternalInput").ap()

    def dram_out(self, name, shape, dtype=F32):
        return self.nc.dram_tensor(name, shape, dtype, kind="ExternalOutput").ap()

    def declare(self):
        di = self.dram_in
        self.x_full = di("x_full", [T, E])
        self.x_own = di("x_own", [TOWN, E])
        self.wq_in = di("wq", [H, E, D])
        self.wk_in = di("wk", [G, E, D])
        self.wv_in = di("wv", [G, E, D])
        self.bqT_in = di("bqT", [D, H])
        self.bkT_in = di("bkT", [D, G])
        self.bvT_in = di("bvT", [D, G])
        self.sw1T_in = di("sw1T", [E, HID])
        self.swgT_in = di("swgT", [E, HID])
        self.sw2T_in = di("sw2T", [HID, E])
        self.wdT_in = di("wdT", [E, LAT])
        self.wgT_in = di("wgT", [E, NEXP])
        self.wupT_in = di("wupT", [LAT, E])
        self.rw1T_in = di("rw1T", [NEXP, LAT, HID])
        self.rwgT_in = di("rwgT", [NEXP, LAT, HID])
        self.rw2T_in = di("rw2T", [NEXP, HID, LAT])
        self.btrow_in = di("btrow", [1, JT * NEXP])
        self.ident_in = di("ident", [128, 128], BF16)
        self.tri_in = di("tri", [128, 128])
        self.eps_in = di("eps", [128, 1])
        self.onesb_in = di("onesb", [1, 128], BF16)
        self.onesf_in = di("onesf", [1, 128])
        self.dbias_in = di("dbias", [128, JT])
        self.out = self.dram_out("out", [TOWN, E])
        self.k_hbm = self.nc.dram_tensor("k_hbm", [G, D, T], BF16).ap()
        self.v_hbm = self.nc.dram_tensor("v_hbm", [G, D, T], BF16).ap()
        self.s_hbm = self.nc.dram_tensor("s_hbm", [TOWN, E], F32).ap()

    def sb(self, es, name, shape, dtype):
        return es.enter_context(self.nc.sbuf_tensor(name, shape, dtype))

    def ps(self, es, name, shape, dtype=F32):
        return es.enter_context(self.nc.psum_tensor(name, shape, dtype))

    def bar(self):
        self.P.barrier(self.ctrs)

    def ln_chunk(self, bufs, src_dma_fn, xnT_write_fn, st):
        P = self.P
        c_sp, c_te, c_act, c_dve = self.c_sp, self.c_te, self.c_act, self.c_dve
        xt, sqs, stats = bufs["xt"], bufs["sqs"], bufs["stats"]
        msq, var4, xnb, tp_ps = (bufs["msq"], bufs["var4"], bufs["xnb"],
                                 bufs["tp_ps"])
        P.op("sp", src_dma_fn, waits=[st.get("xt_free")], inc=[c_sp])
        xin = c_sp.snap()
        for t in range(4):
            P.op("dve", lambda e, t=t: e.reduce_sum(
                stats[:, t:t + 1], xt[:, t * E:(t + 1) * E], axis=AX.X),
                waits=[xin, st.get("stats_free")], inc=[c_dve])
            P.op("act", lambda e, t=t: e.activation(
                sqs[:, 0:E],
                xt[:, t * E:(t + 1) * E], AF.Square,
                accum_out=stats[:, 4 + t:5 + t]),
                waits=[xin, st.get("stats_free")],
                inc=[c_act])
        sums_done = c_dve.snap()
        sqs_done = c_act.snap()
        P.op("dve", lambda e: e.tensor_scalar(
            stats[:, 8:12], stats[:, 0:4], 1.0 / E, None, ALU.mult),
            waits=[sums_done], inc=[c_dve])
        P.op("dve", lambda e: e.tensor_tensor(
            msq[:, :], stats[:, 8:12], stats[:, 8:12], ALU.mult),
            waits=[c_dve], inc=[c_dve])
        P.op("dve", lambda e: e.scalar_tensor_tensor(
            var4[:, :], stats[:, 4:8], 1.0 / E, msq[:, :],
            ALU.mult, ALU.subtract),
            waits=[sqs_done, c_dve], inc=[c_dve])
        P.op("act", lambda e: e.activation(
            var4[:, :], var4[:, :], AF.Ln, bias=self.epsb[:, 0:1]),
            waits=[c_dve], inc=[c_act])
        P.op("act", lambda e: e.activation(
            stats[:, 12:16], var4[:, :], AF.Exp, scale=-0.5),
            waits=[c_act], inc=[c_act])
        rstd_done = c_act.snap()
        for t in range(4):
            P.op("dve", lambda e, t=t: e.tensor_scalar(
                xnb[:, t * E:(t + 1) * E], xt[:, t * E:(t + 1) * E],
                stats[:, 8 + t:9 + t], stats[:, 12 + t:13 + t],
                ALU.subtract, ALU.mult),
                waits=[rstd_done], inc=[c_dve])
        xnb_done = c_dve.snap()
        st["xt_free"] = xnb_done
        st["stats_free"] = xnb_done
        tp_ps2 = bufs["tp_ps2"]
        for b8 in range(8):
            tp = tp_ps if b8 % 2 == 0 else tp_ps2
            for k in range(8):
                idx = b8 * 8 + k
                t, ech = idx // ECH, idx % ECH
                P.op("te", lambda e, t=t, ech=ech, k=k, tp=tp: e.transpose(
                    tp[:, k * 128:(k + 1) * 128],
                    xnb[:, t * E + ech * 128: t * E + (ech + 1) * 128],
                    self.identb[:, :]),
                    waits=[xnb_done, st.get(f"tp_free{b8 % 2}")],
                    inc=([c_te] if k == 7 else []))
            tp_done = c_te.snap()
            t0, ech0 = (b8 * 8) // ECH, (b8 * 8) % ECH
            P.op("dve", lambda e, b8=b8, t0=t0, ech0=ech0:
                 xnT_write_fn(e, b8, t0, ech0),
                 waits=[tp_done, st.get("xnT_free")], inc=[c_dve])
            st[f"tp_free{b8 % 2}"] = c_dve.snap()
        return c_dve.snap()

    def build(self):
        self.declare()
        with ExitStack() as outer:
            with nc_sem_stack(self.nc, self.ctrs):
                s = self.sb
                self.identb = s(outer, "identb", [128, 128], BF16)
                self.trib = s(outer, "trib", [128, 128], F32)
                self.epsb = s(outer, "epsb", [128, 1], F32)
                self.onesbb = s(outer, "onesbb", [1, 128], BF16)
                self.onesfb = s(outer, "onesfb", [1, 128], F32)
                self.onescol = s(outer, "onescol", [128, 1], BF16)
                self.dbiasb = s(outer, "dbiasb", [128, JT], F32)
                self.bqTb = s(outer, "bqTb", [128, H], F32)
                self.bkTb = s(outer, "bkTb", [128, G], F32)
                self.bvTb = s(outer, "bvTb", [128, G], F32)
                self.brep8 = s(outer, "brep8", [128, JT * NEXP], F32)
                self.wdense = s(outer, "wdense", [128, JT * NEXP], F32)

                with self.nc.Block() as block:
                    with ExitStack() as p0:
                        self.phase0(p0)
                    self.bar()
                    if PHASES >= 1:
                        with ExitStack() as p1:
                            self.phase1(p1)
                        self.bar()
                    if PHASES >= 2:
                        self.hhT = s(outer, "hhT", [128, 8 * TOWN], BF16)
                        self.xlT = s(outer, "xlT", [128, 4 * TOWN], BF16)
                        with ExitStack() as mid:
                            self.xnT_own = s(mid, "xnT_own",
                                             [128, ECH * TOWN], BF16)
                            self.q_sb = s(mid, "q_sb", [128, H * TOWN], BF16)
                            self.kown = s(mid, "kown", [128, G * TOWN], BF16)
                            self.vown = s(mid, "vown", [128, G * JT * 129],
                                          BF16)
                            with ExitStack() as p1b:
                                self.phase1b(p1b)
                            self.bar()
                            if PHASES >= 3:
                                with ExitStack() as p15:
                                    self.phase15(p15)
                                self.bar()
                            if PHASES >= 4:
                                with ExitStack() as p2:
                                    self.phase2(p2)
                                self.bar()
                        if PHASES >= 5:
                            with ExitStack() as p3:
                                self.phase3(p3)
                            self.bar()
                    self.finale()
                    self.P.replay(block)
        return self.nc

    def phase0(self, es):
        P = self.P
        c_sp, c_te, c_dve = self.c_sp, self.c_te, self.c_dve
        for dst, src in [
            (self.identb, self.ident_in), (self.trib, self.tri_in),
            (self.epsb, self.eps_in), (self.onesbb, self.onesb_in), (self.onesfb, self.onesf_in),
            (self.dbiasb, self.dbias_in), (self.bqTb, self.bqT_in),
            (self.bkTb, self.bkT_in), (self.bvTb, self.bvT_in),
        ]:
            P.op("sp", lambda e, d=dst, s=src: e.dma_start(out=d[:, :], in_=s),
                 inc=[c_sp])
        btrow = self.sb(es, "btrow_s", [1, JT * NEXP], F32)
        P.op("sp", lambda e: e.dma_start(out=btrow[:, :], in_=self.btrow_in),
             inc=[c_sp])
        ld = c_sp.snap()
        P.op("dve", lambda e: e.memset(self.onescol[:, :], 1.0), inc=[c_dve])
        rep_ps = self.ps(es, "rep_ps", [128, JT * NEXP])
        P.op("te", lambda e: e.matmul(rep_ps[:, 0:JT * NEXP],
                                      self.onesfb[0:1, :], btrow[0:1, :],
                                      start=True, stop=True),
             waits=[ld], inc=[c_te])
        mm2 = self.c_te.snap()
        P.op("dve", lambda e: e.tensor_copy(self.brep8[:, :],
                                            rep_ps[:, 0:JT * NEXP]),
             waits=[mm2], inc=[c_dve])

    def phase1(self, es):
        P = self.P
        c_sp, c_gp, c_te, c_act, c_dve = (self.c_sp, self.c_gp, self.c_te,
                                          self.c_act, self.c_dve)
        s = self.sb
        wkb = s(es, "wkb", [128, G * ECH * 128], BF16)
        wvb = s(es, "wvb", [128, G * ECH * 128], BF16)
        bufs2 = []
        tps = [self.ps(es, f"tp_ps{i}", [128, 1024], BF16) for i in range(4)]
        for i in (0, 1):
            tp_a, tp_b = tps[2 * i], tps[2 * i + 1]
            bufs2.append({
                "xt": s(es, f"xt{i}", [128, 4 * E], F32),
                "sqs": s(es, f"sqs{i}", [128, E], F32),
                "stats": s(es, f"stats{i}", [128, 16], F32),
                "msq": s(es, f"msq{i}", [128, 4], F32),
                "var4": s(es, f"var4{i}", [128, 4], F32),
                "xnb": s(es, f"xnb{i}", [128, 4 * E], BF16),
                "tp_ps": tp_a,
                "tp_ps2": tp_b,
            })
        xnT = [s(es, f"xnT{i}", [128, ECH * 512], BF16) for i in (0, 1)]
        kst = [s(es, f"kst{i}", [128, G * 512], BF16) for i in (0, 1)]
        vst = [s(es, f"vst{i}", [128, G * 512], BF16) for i in (0, 1)]
        kv_ps = [self.ps(es, f"kv_ps{i}", [128, 512]) for i in range(4)]

        for gi in range(G):
            P.op("gp", lambda e, gi=gi: e.dma_start(
                out=wkb[:, gi * ECH * 128:(gi + 1) * ECH * 128]
                .rearrange("p (c d) -> p c d", c=ECH),
                in_=self.wk_in[gi].rearrange("(c p) d -> p c d", p=128)),
                inc=[c_gp])
            P.op("gp", lambda e, gi=gi: e.dma_start(
                out=wvb[:, gi * ECH * 128:(gi + 1) * ECH * 128]
                .rearrange("p (c d) -> p c d", c=ECH),
                in_=self.wv_in[gi].rearrange("(c p) d -> p c d", p=128)),
                inc=[c_gp])
        wload = c_gp.snap()

        st2 = [{}, {}]
        kst_free = [None, None]
        vst_free = [None, None]
        xnT_free = [None, None]

        for ch in range(T // 512):
            bi = ch % 2
            bufs = bufs2[bi]
            st = st2[bi]

            def src_dma(e, ch=ch, bufs=bufs):
                return e.dma_start(
                    out=bufs["xt"][:, :].rearrange("p (t e) -> p t e", t=4),
                    in_=self.x_full[512 * ch:512 * (ch + 1), :]
                    .rearrange("(t p) e -> p t e", p=128))

            def xnT_wr(e, b8, t0, ech0, bi=bi, bufs=bufs):
                tp = bufs["tp_ps"] if b8 % 2 == 0 else bufs["tp_ps2"]
                return e.tensor_copy(
                    xnT[bi][:, :].rearrange("p (c n) -> p c n", c=ECH)
                    [:, ech0:ech0 + 8, t0 * 128:(t0 + 1) * 128],
                    tp[:, :].rearrange("p (k n) -> p k n", k=8))

            st["xnT_free"] = xnT_free[bi]
            xnT_done = self.ln_chunk(bufs, src_dma, xnT_wr, st)

            for gi in range(G):
                for ech in range(ECH):
                    P.op("te", lambda e, gi=gi, ech=ech, bi=bi: e.matmul(
                        kv_ps[gi][:, :],
                        wkb[:, (gi * ECH + ech) * 128:(gi * ECH + ech + 1) * 128],
                        xnT[bi][:, ech * 512:(ech + 1) * 512],
                        start=(ech == 0), stop=(ech == ECH - 1)),
                        waits=[xnT_done, wload, kst_free[bi], vst_free[bi]],
                        inc=([c_te] if ech == ECH - 1 else []))
            kmm_done = self.c_te.snap()
            for gi in range(G):
                P.op("act", lambda e, gi=gi, bi=bi: e.activation(
                    kst[bi][:, gi * 512:(gi + 1) * 512], kv_ps[gi][:, :],
                    AF.Identity, bias=self.bkTb[:, gi:gi + 1]),
                    waits=[kmm_done], inc=([c_act] if gi == G - 1 else []))
            kst_done = self.c_act.snap()
            for gi in range(G):
                P.op("sp", lambda e, ch=ch, gi=gi, bi=bi: e.dma_start(
                    out=self.k_hbm[gi, :, ch * 512:(ch + 1) * 512],
                    in_=kst[bi][:, gi * 512:(gi + 1) * 512]),
                    waits=[kst_done], inc=[c_sp])
            kst_free[bi] = self.c_sp.snap()

            for gi in range(G):
                for ech in range(ECH):
                    P.op("te", lambda e, gi=gi, ech=ech, bi=bi: e.matmul(
                        kv_ps[gi][:, :],
                        wvb[:, (gi * ECH + ech) * 128:(gi * ECH + ech + 1) * 128],
                        xnT[bi][:, ech * 512:(ech + 1) * 512],
                        start=(ech == 0), stop=(ech == ECH - 1)),
                        waits=[xnT_done, wload, kst_done, vst_free[bi]],
                        inc=([c_te] if ech == ECH - 1 else []))
            vmm_done = self.c_te.snap()
            xnT_free[bi] = vmm_done
            for gi in range(G):
                P.op("act", lambda e, gi=gi, bi=bi: e.activation(
                    vst[bi][:, gi * 512:(gi + 1) * 512], kv_ps[gi][:, :],
                    AF.Identity, bias=self.bvTb[:, gi:gi + 1]),
                    waits=[vmm_done], inc=([c_act] if gi == G - 1 else []))
            vst_done = self.c_act.snap()
            for gi in range(G):
                P.op("sp", lambda e, ch=ch, gi=gi, bi=bi: e.dma_start(
                    out=self.v_hbm[gi, :, ch * 512:(ch + 1) * 512],
                    in_=vst[bi][:, gi * 512:(gi + 1) * 512]),
                    waits=[vst_done], inc=[c_sp])
            vst_free[bi] = self.c_sp.snap()

        if DEBUG:
            fin = self.c_sp.snap()
            self.dbg_khbm = self.dram_out("dbg_khbm", [G, D, T], BF16)
            self.dbg_vhbm = self.dram_out("dbg_vhbm", [G, D, T], BF16)
            P.op("sp", lambda e: e.dma_start(out=self.dbg_khbm,
                                             in_=self.k_hbm), waits=[fin],
                 inc=[c_sp])
            P.op("sp", lambda e: e.dma_start(out=self.dbg_vhbm,
                                             in_=self.v_hbm), waits=[fin],
                 inc=[c_sp])

    def phase1b(self, es):
        P = self.P
        c_sp, c_gp, c_te, c_act, c_dve = (self.c_sp, self.c_gp, self.c_te,
                                          self.c_act, self.c_dve)
        s = self.sb
        wkb = [s(es, f"wkb2{i}", [128, ECH * 128], BF16) for i in (0, 1)]
        wvb = [s(es, f"wvb2{i}", [128, ECH * 128], BF16) for i in (0, 1)]
        wqb = [s(es, f"wqb{i}", [128, ECH * 128], BF16) for i in (0, 1)]
        bufs = {
            "xt": s(es, "xtb", [128, 4 * E], F32),
            "sqs": s(es, "sqsb", [128, E], F32),
            "stats": s(es, "statsb", [128, 16], F32),
            "msq": s(es, "msqb", [128, 4], F32),
            "var4": s(es, "var4b", [128, 4], F32),
            "xnb": s(es, "xnbb", [128, 4 * E], BF16),
            "tp_ps": self.ps(es, "tp_psb", [128, 1024], BF16),
            "tp_ps2": self.ps(es, "tp_psb2", [128, 1024], BF16),
        }
        mm_ps = [self.ps(es, f"mm_psb{i}", [128, 512]) for i in range(4)]
        vtb = s(es, "vtb", [128, 1024], BF16)
        vt_free = None
        vtp_free = [None, None]

        xnT_view = self.xnT_own[:, :].rearrange("p (c n) -> p c n", c=ECH)
        st = {}
        for ch in range(TOWN // 512):
            def src_dma(e, ch=ch):
                return e.dma_start(
                    out=bufs["xt"][:, :].rearrange("p (t e) -> p t e", t=4),
                    in_=self.x_own[512 * ch:512 * (ch + 1), :]
                    .rearrange("(t p) e -> p t e", p=128))

            def xnT_wr(e, b8, t0, ech0, ch=ch):
                tp = bufs["tp_ps"] if b8 % 2 == 0 else bufs["tp_ps2"]
                return e.tensor_copy(
                    xnT_view[:, ech0:ech0 + 8,
                             ch * 512 + t0 * 128: ch * 512 + (t0 + 1) * 128],
                    tp[:, :].rearrange("p (k n) -> p k n", k=8))

            self.ln_chunk(bufs, src_dma, xnT_wr, st)
        xnT_done = self.c_dve.snap()

        wq_free = [None, None]
        ps_free = [None, None, None, None]
        for h in range(H):
            wi = h % 2
            P.op("gp", lambda e, h=h, wi=wi: e.dma_start(
                out=wqb[wi][:, :].rearrange("p (c d) -> p c d", c=ECH),
                in_=self.wq_in[h].rearrange("(c p) d -> p c d", p=128)),
                waits=[wq_free[wi]], inc=[c_gp])
            wq_ld = c_gp.snap()
            for tc in range(2):
                pi = tc
                for ech in range(ECH):
                    P.op("te", lambda e, wi=wi, ech=ech, tc=tc, pi=pi: e.matmul(
                        mm_ps[pi][:, :],
                        wqb[wi][:, ech * 128:(ech + 1) * 128],
                        xnT_view[:, ech, tc * 512:(tc + 1) * 512],
                        start=(ech == 0), stop=(ech == ECH - 1)),
                        waits=[xnT_done, wq_ld, ps_free[pi]],
                        inc=([c_te] if ech == ECH - 1 else []))
                qmm = self.c_te.snap()
                P.op("act", lambda e, h=h, tc=tc, pi=pi: e.activation(
                    self.q_sb[:, h * TOWN + tc * 512:
                              h * TOWN + (tc + 1) * 512],
                    mm_ps[pi][:, :], AF.Identity, bias=self.bqTb[:, h:h + 1]),
                    waits=[qmm], inc=[c_act])
                ps_free[pi] = self.c_act.snap()
            wq_free[wi] = self.c_act.snap()

        P.op("dve", lambda e: e.memset(
            self.vown[:, :].rearrange("p (q s) -> p q s", s=129)
            [:, :, 128:129], 1.0), inc=[c_dve])
        ones_set = self.c_dve.snap()

        kvw_free = [None, None]
        for gi in range(G):
            wi = gi % 2
            P.op("gp", lambda e, gi=gi, wi=wi: e.dma_start(
                out=wkb[wi][:, :].rearrange("p (c d) -> p c d", c=ECH),
                in_=self.wk_in[gi].rearrange("(c p) d -> p c d", p=128)),
                waits=[kvw_free[wi]], inc=[c_gp])
            P.op("gp", lambda e, gi=gi, wi=wi: e.dma_start(
                out=wvb[wi][:, :].rearrange("p (c d) -> p c d", c=ECH),
                in_=self.wv_in[gi].rearrange("(c p) d -> p c d", p=128)),
                waits=[kvw_free[wi]], inc=[c_gp])
            kv_wload = c_gp.snap()
            for tc in range(2):
                pi = 2 + tc
                for ech in range(ECH):
                    P.op("te", lambda e, wi=wi, tc=tc, ech=ech, pi=pi: e.matmul(
                        mm_ps[pi][:, :],
                        wkb[wi][:, ech * 128:(ech + 1) * 128],
                        xnT_view[:, ech, tc * 512:(tc + 1) * 512],
                        start=(ech == 0), stop=(ech == ECH - 1)),
                        waits=[xnT_done, kv_wload, ps_free[pi]],
                        inc=([c_te] if ech == ECH - 1 else []))
                kmm = self.c_te.snap()
                P.op("act", lambda e, gi=gi, tc=tc, pi=pi: e.activation(
                    self.kown[:, gi * TOWN + tc * 512:
                              gi * TOWN + (tc + 1) * 512],
                    mm_ps[pi][:, :], AF.Identity, bias=self.bkTb[:, gi:gi + 1]),
                    waits=[kmm], inc=[c_act])
                ps_free[pi] = self.c_act.snap()
            for tc in range(2):
                pi = 2 + tc
                for ech in range(ECH):
                    P.op("te", lambda e, wi=wi, tc=tc, ech=ech, pi=pi: e.matmul(
                        mm_ps[pi][:, :],
                        wvb[wi][:, ech * 128:(ech + 1) * 128],
                        xnT_view[:, ech, tc * 512:(tc + 1) * 512],
                        start=(ech == 0), stop=(ech == ECH - 1)),
                        waits=[xnT_done, kv_wload, ps_free[pi]],
                        inc=([c_te] if ech == ECH - 1 else []))
                vmm = self.c_te.snap()
                P.op("act", lambda e, gi=gi, tc=tc, pi=pi: e.activation(
                    vtb[:, tc * 512:(tc + 1) * 512], mm_ps[pi][:, :],
                    AF.Identity, bias=self.bvTb[:, gi:gi + 1]),
                    waits=[vmm, vt_free], inc=[c_act])
                ps_free[pi] = self.c_act.snap()
            vt_done = self.c_act.snap()
            for half in range(2):
                tp = bufs["tp_ps"] if half == 0 else bufs["tp_ps2"]
                for k in range(4):
                    t = half * 4 + k
                    P.op("te", lambda e, t=t, k=k, tp=tp: e.transpose(
                        tp[:, k * 128:(k + 1) * 128],
                        vtb[:, t * 128:(t + 1) * 128], self.identb[:, :]),
                        waits=[vt_done, vtp_free[half]],
                        inc=([c_te] if k == 3 else []))
                tdone = self.c_te.snap()
                P.op("dve", lambda e, gi=gi, half=half, tp=tp: e.tensor_copy(
                    self.vown[:, :].rearrange("p (q s) -> p q s", s=129)
                    [:, gi * JT + half * 4: gi * JT + (half + 1) * 4, 0:128],
                    tp[:, 0:512].rearrange("p (k n) -> p k n", k=4)),
                    waits=[tdone, ones_set], inc=[c_dve])
                vtp_free[half] = self.c_dve.snap()
            vt_free = self.c_dve.snap()
            kvw_free[wi] = self.c_dve.snap()

        if DEBUG:
            fin = [self.c_act.snap(), self.c_dve.snap()]
            self.dbg_xnt = self.dram_out("dbg_xnt", [128, ECH * TOWN], BF16)
            self.dbg_q = self.dram_out("dbg_q", [128, H * TOWN], BF16)
            self.dbg_kown = self.dram_out("dbg_kown", [128, G * TOWN], BF16)
            self.dbg_vown = self.dram_out("dbg_vown", [128, G * JT * 129], BF16)
            for dst, src in [(self.dbg_xnt, self.xnT_own),
                             (self.dbg_q, self.q_sb),
                             (self.dbg_kown, self.kown),
                             (self.dbg_vown, self.vown)]:
                P.op("sp", lambda e, d=dst, s=src: e.dma_start(out=d,
                                                               in_=s[:, :]),
                     waits=fin, inc=[c_sp])

    def phase15(self, es):
        P = self.P
        c_sp, c_gp, c_te, c_act, c_dve = (self.c_sp, self.c_gp, self.c_te,
                                          self.c_act, self.c_dve)
        s = self.sb
        sw1b = s(es, "sw1b", [128, ECH * HID], BF16)
        swgb = s(es, "swgb", [128, ECH * HID], BF16)
        wdb = s(es, "wdb", [128, ECH * LAT], BF16)
        wgb = s(es, "wgb", [128, ECH * NEXP], BF16)
        prob = s(es, "prob", [128, JT * NEXP], F32)
        vbuf = s(es, "vbuf_g", [128, JT * NEXP], F32)
        cnt = [s(es, f"cnt{i}", [128, JT * NEXP], F32) for i in (0, 1)]
        rbuf = s(es, "rbuf", [128, HID], BF16)
        rbuf2 = s(es, "rbuf2", [128, HID], BF16)
        rsum = s(es, "rsum", [128, JT], F32)
        z_ps = self.ps(es, "z_ps", [128, 512])
        h_ps = self.ps(es, "h_ps", [128, 1024])
        g_ps = self.ps(es, "g_ps", [128, 1024])

        xnT_view = self.xnT_own[:, :].rearrange("p (c n) -> p c n", c=ECH)

        P.op("gp", lambda e: e.dma_start(
            out=sw1b[:, :].rearrange("p (c n) -> p c n", c=ECH),
            in_=self.sw1T_in.rearrange("(c p) n -> p c n", p=128)),
            inc=[c_gp])
        P.op("gp", lambda e: e.dma_start(
            out=swgb[:, :].rearrange("p (c n) -> p c n", c=ECH),
            in_=self.swgT_in.rearrange("(c p) n -> p c n", p=128)),
            inc=[c_gp])
        P.op("gp", lambda e: e.dma_start(
            out=wdb[:, :].rearrange("p (c n) -> p c n", c=ECH),
            in_=self.wdT_in.rearrange("(c p) n -> p c n", p=128)),
            inc=[c_gp])
        P.op("gp", lambda e: e.dma_start(
            out=wgb[:, :].rearrange("p (c n) -> p c n", c=ECH),
            in_=self.wgT_in.rearrange("(c p) n -> p c n", p=128)),
            inc=[c_gp])
        wload = c_gp.snap()

        for t in range(JT):
            for ech in range(ECH):
                P.op("te", lambda e, t=t, ech=ech: e.matmul(
                    z_ps[:, t * NEXP:(t + 1) * NEXP],
                    xnT_view[:, ech, t * 128:(t + 1) * 128],
                    wgb[:, ech * NEXP:(ech + 1) * NEXP],
                    start=(ech == 0), stop=(ech == ECH - 1)),
                    waits=[wload],
                    inc=([c_te] if ech == ECH - 1 and t == JT - 1 else []))
        zmm = self.c_te.snap()
        P.op("act", lambda e: e.activation(
            prob[:, :], z_ps[:, 0:JT * NEXP], AF.Exp, scale=-1.0),
            waits=[zmm], inc=[c_act])
        P.op("dve", lambda e: e.tensor_scalar(
            prob[:, :], prob[:, :], 1.0, None, ALU.add),
            waits=[c_act], inc=[c_dve])
        P.op("dve", lambda e: e.reciprocal(prob[:, :], prob[:, :]),
            waits=[c_dve], inc=[c_dve])
        P.op("dve", lambda e: e.tensor_tensor(
            vbuf[:, :], prob[:, :], self.brep8[:, :], ALU.add),
            waits=[c_dve], inc=[c_dve])
        P.op("dve", lambda e: e.memset(cnt[0][:, :], 0.0),
            waits=[c_dve], inc=[c_dve])
        cur = 0
        for ep in range(NEXP):
            for t in range(JT):
                P.op("dve", lambda e, ep=ep, t=t, cur=cur: e.scalar_tensor_tensor(
                    cnt[1 - cur][:, t * NEXP:(t + 1) * NEXP],
                    vbuf[:, t * NEXP:(t + 1) * NEXP],
                    vbuf[:, t * NEXP + ep:t * NEXP + ep + 1],
                    cnt[cur][:, t * NEXP:(t + 1) * NEXP],
                    ALU.is_lt, ALU.add),
                    waits=[c_dve], inc=[c_dve])
            cur = 1 - cur
        P.op("dve", lambda e, cur=cur: e.tensor_scalar(
            cnt[1 - cur][:, :], cnt[cur][:, :], 7.5, None, ALU.is_le),
            waits=[c_dve], inc=[c_dve])
        P.op("dve", lambda e, cur=cur: e.tensor_tensor(
            self.wdense[:, :], prob[:, :], cnt[1 - cur][:, :], ALU.mult),
            waits=[c_dve], inc=[c_dve])
        P.op("dve", lambda e: e.reduce_sum(
            rsum[:, :].rearrange("p (j o) -> p j o", o=1),
            self.wdense[:, :].rearrange("p (j n) -> p j n", j=JT), axis=AX.X),
            waits=[c_dve], inc=[c_dve])
        P.op("dve", lambda e: e.reciprocal(rsum[:, :], rsum[:, :]),
            waits=[c_dve], inc=[c_dve])
        for t in range(JT):
            P.op("dve", lambda e, t=t: e.tensor_scalar(
                self.wdense[:, t * NEXP:(t + 1) * NEXP],
                self.wdense[:, t * NEXP:(t + 1) * NEXP],
                rsum[:, t:t + 1], RSCALE, ALU.mult, ALU.mult),
                waits=[c_dve], inc=[c_dve])

        r_free = None
        for hch in range(8):
            for tc in range(2):
                for ech in range(ECH):
                    P.op("te", lambda e, hch=hch, tc=tc, ech=ech: e.matmul(
                        h_ps[:, tc * 512:(tc + 1) * 512],
                        sw1b[:, ech * HID + hch * 128:
                             ech * HID + (hch + 1) * 128],
                        xnT_view[:, ech, tc * 512:(tc + 1) * 512],
                        start=(ech == 0), stop=(ech == ECH - 1)),
                        waits=[wload, c_dve], inc=[])
                    P.op("te", lambda e, hch=hch, tc=tc, ech=ech: e.matmul(
                        g_ps[:, tc * 512:(tc + 1) * 512],
                        swgb[:, ech * HID + hch * 128:
                             ech * HID + (hch + 1) * 128],
                        xnT_view[:, ech, tc * 512:(tc + 1) * 512],
                        start=(ech == 0), stop=(ech == ECH - 1)),
                        waits=[wload, c_dve],
                        inc=([c_te] if ech == ECH - 1 and tc == 1 else []))
            hmm = self.c_te.snap()
            P.op("act", lambda e: e.activation(
                rbuf[:, :], g_ps[:, :], AF.Relu),
                waits=[hmm, r_free], inc=[c_act])
            P.op("dve", lambda e: e.tensor_tensor(
                rbuf2[:, :], rbuf[:, :], rbuf[:, :], ALU.mult),
                waits=[c_act], inc=[c_dve])
            P.op("dve", lambda e, hch=hch: e.tensor_tensor(
                self.hhT[:, hch * TOWN:(hch + 1) * TOWN],
                h_ps[:, :], rbuf2[:, :], ALU.mult),
                waits=[c_dve], inc=[c_dve])
            r_free = self.c_dve.snap()

        for lch in range(4):
            for tc in range(2):
                for ech in range(ECH):
                    P.op("te", lambda e, lch=lch, tc=tc, ech=ech: e.matmul(
                        h_ps[:, tc * 512:(tc + 1) * 512],
                        wdb[:, ech * LAT + lch * 128:
                            ech * LAT + (lch + 1) * 128],
                        xnT_view[:, ech, tc * 512:(tc + 1) * 512],
                        start=(ech == 0), stop=(ech == ECH - 1)),
                        waits=[wload, c_act, c_dve],
                        inc=([c_te] if ech == ECH - 1 and tc == 1 else []))
            xmm = self.c_te.snap()
            P.op("act", lambda e, lch=lch: e.activation(
                self.xlT[:, lch * TOWN:(lch + 1) * TOWN], h_ps[:, :],
                AF.Identity),
                waits=[xmm], inc=[c_act])

        if DEBUG:
            fin = [self.c_act.snap(), self.c_dve.snap()]
            self.dbg_w = self.dram_out("dbg_w", [128, JT * NEXP], F32)
            self.dbg_hh = self.dram_out("dbg_hh", [128, 8 * TOWN], BF16)
            self.dbg_xl = self.dram_out("dbg_xl", [128, 4 * TOWN], BF16)
            for dst, src in [(self.dbg_w, self.wdense), (self.dbg_hh, self.hhT),
                             (self.dbg_xl, self.xlT)]:
                P.op("sp", lambda e, d=dst, s=src: e.dma_start(out=d,
                                                               in_=s[:, :]),
                     waits=fin, inc=[c_sp])

    def phase2(self, es):
        P = self.P
        c_sp, c_gp, c_te, c_act, c_dve = (self.c_sp, self.c_gp, self.c_te,
                                          self.c_act, self.c_dve)
        s = self.sb
        NS = 2
        kbuf = [s(es, f"kbuf{i}", [128, T], BF16) for i in (0, 1)]
        vbuf = [s(es, f"vbuf{i}", [128, 64 * 144], BF16) for i in (0, 1)]
        pt = [s(es, f"pt{i}", [128, 1024], BF16) for i in range(NS)]
        sst = s(es, "sst", [128, JT * 128], F32)
        qsq = s(es, "qsqb", [128, 4 * TOWN], BF16)
        ksq = s(es, "ksqb", [128, 512], BF16)
        kmaxb = s(es, "kmaxb", [1, 20], F32)
        cbuf = s(es, "cbuf", [1, 16], F32)
        crep4 = s(es, "crep4", [128, 4], F32)
        dbadj4 = s(es, "dbadj4", [128, 4 * JT], F32)
        rcpb = s(es, "rcpb", [128, JT], F32)
        zbuf = s(es, "zbuf", [128, 512], BF16)
        s_ps = [self.ps(es, f"s_ps{i}", [128, 1024]) for i in range(NS)]
        o_ps = [self.ps(es, f"o_ps{i}", [128, 512]) for i in range(3)]
        cp_ps = self.ps(es, "cp_ps", [128, 512])

        def o_tile(j):
            return o_ps[j // 3][:, (j % 3) * 129:(j % 3) * 129 + 129]

        for i in (0, 1):
            P.op("dve", lambda e, i=i: e.memset(
                vbuf[i][:, :].rearrange("p (q s) -> p q s", s=144)
                [:, :, 128:129], 1.0), inc=[c_dve])
        P.op("dve", lambda e: e.memset(zbuf[:, :], 0.0), inc=[c_dve])
        ones_set = c_dve.snap()

        exp_done = [None] * NS
        av_done = [None] * NS
        head_eplg = None
        sst_free = None
        group_done = [None, None]
        slot = 0

        for gi in range(G):
            bi = gi % 2
            P.op("gp", lambda e, gi=gi, bi=bi: e.dma_start(
                out=kbuf[bi][:, :], in_=self.k_hbm[gi]),
                waits=[group_done[bi]], inc=[c_gp])
            for qt in range(64):
                P.op("sp", lambda e, gi=gi, bi=bi, qt=qt: e.dma_start_transpose(
                    vbuf[bi][:, qt * 144:qt * 144 + 128],
                    self.v_hbm[gi, :, qt * 128:(qt + 1) * 128]),
                    waits=([group_done[bi], ones_set] if qt == 0 else []),
                    inc=[c_sp])
            ready_sp = self.c_sp.snap()
            ready_gp = self.c_gp.snap()
            ready = ready_sp

            guard = [w for w in exp_done if w] + [ready_sp, ready_gp]
            kprev = [None, None]
            for chk in range(T // 512):
                sl = chk % 2
                P.op("act", lambda e, bi=bi, chk=chk: e.activation(
                    ksq[:, :], kbuf[bi][:, chk * 512:(chk + 1) * 512],
                    AF.Square),
                    waits=[ready_gp, kprev[sl], c_dve] +
                          ([] if chk else guard), inc=[c_act])
                sq_done = c_act.snap()
                P.op("te", lambda e, sl=sl: e.matmul(
                    s_ps[0][0:1, sl * 512:(sl + 1) * 512],
                    self.onescol[:, 0:1], ksq[:, :],
                    start=True, stop=True),
                    waits=guard + [sq_done, kprev[sl]], inc=[c_te])
                mm = self.c_te.snap()
                P.op("dve", lambda e, chk=chk, sl=sl: e.reduce_max(
                    kmaxb[0:1, chk + 4:chk + 5],
                    s_ps[0][0:1, sl * 512:(sl + 1) * 512],
                    axis=AX.X),
                    waits=[mm], inc=[c_dve])
                kprev[sl] = c_dve.snap()
            P.op("dve", lambda e: e.reduce_max(
                kmaxb[0:1, 0:1], kmaxb[0:1, 4:4 + T // 512], axis=AX.X),
                waits=[kprev[0], kprev[1]], inc=[c_dve])
            P.op("act", lambda e: e.activation(
                kmaxb[0:1, 1:2], kmaxb[0:1, 0:1], AF.Ln),
                waits=[c_dve], inc=[c_act])
            lnk_done = c_act.snap()

            P.op("act", lambda e, gi=gi: e.activation(
                qsq[:, :], self.q_sb[:, gi * 4 * TOWN:(gi + 1) * 4 * TOWN],
                AF.Square), waits=[self.c_dve.snap()], inc=[c_act])
            qsq_done = c_act.snap()
            qprev = [None, None]
            for i in range(8):
                sl = i % 2
                P.op("te", lambda e, i=i, sl=sl: e.matmul(
                    s_ps[1][0:1, sl * 512:(sl + 1) * 512],
                    self.onescol[:, 0:1],
                    qsq[:, i * 512:(i + 1) * 512],
                    start=True, stop=True),
                    waits=guard + [qsq_done, qprev[sl]], inc=[c_te])
                mm = self.c_te.snap()
                P.op("dve", lambda e, i=i, sl=sl: e.reduce_max(
                    cbuf[0:1, i:i + 1],
                    s_ps[1][0:1, sl * 512:(sl + 1) * 512], axis=AX.X),
                    waits=[mm], inc=[c_dve])
                qprev[sl] = c_dve.snap()
            P.op("dve", lambda e: e.reduce_max(
                cbuf[0:1, 8:12].rearrange("o (h x) -> o h x", h=4),
                cbuf[0:1, 0:8].rearrange("o (h x) -> o h x", h=4),
                axis=AX.X),
                waits=[qprev[0], qprev[1]], inc=[c_dve])
            P.op("act", lambda e: e.activation(
                cbuf[0:1, 12:16], cbuf[0:1, 8:12], AF.Ln),
                waits=[c_dve], inc=[c_act])
            P.op("dve", lambda e: e.tensor_scalar(
                cbuf[0:1, 8:12], cbuf[0:1, 12:16], kmaxb[0:1, 1:2], None,
                ALU.add), waits=[c_act, lnk_done], inc=[c_dve])
            P.op("act", lambda e: e.activation(
                cbuf[0:1, 12:16], cbuf[0:1, 8:12], AF.Exp, scale=0.5),
                waits=[c_dve], inc=[c_act])
            P.op("dve", lambda e: e.tensor_scalar(
                cbuf[0:1, 8:12], cbuf[0:1, 12:16], -SCALE_ATT, None,
                ALU.mult), waits=[c_act], inc=[c_dve])
            P.op("te", lambda e: e.matmul(
                cp_ps[:, 0:4], self.onesfb[0:1, :], cbuf[0:1, 8:12],
                start=True, stop=True), waits=[c_dve], inc=[c_te])
            P.op("dve", lambda e: e.tensor_copy(crep4[:, :], cp_ps[:, 0:4]),
                 waits=[c_te], inc=[c_dve])
            for hh in range(4):
                P.op("dve", lambda e, hh=hh: e.tensor_scalar(
                    dbadj4[:, hh * JT:(hh + 1) * JT], self.dbiasb[:, :],
                    crep4[:, hh:hh + 1], None, ALU.add),
                    waits=[c_dve], inc=[c_dve])
            cprep = c_dve.snap()

            for hh in range(4):
                h = gi * 4 + hh
                o_free = head_eplg
                qoff = h * TOWN
                for b in range(3):
                    P.op("te", lambda e, b=b: e.matmul(
                        o_ps[b][:, :], self.identb[:, :], zbuf[:, :],
                        start=True, stop=False),
                        waits=[ones_set, o_free, cprep],
                        inc=([c_te] if b == 2 else []))

                def emit_pair(jobs):
                    nonlocal slot
                    sl = slot
                    slot = (slot + 1) % NS
                    alln = sum(len(j["s_mms"]) for j in jobs)
                    i = 0
                    for base, jb in zip((0, 512), jobs):
                        for dst_off, w, lf, rf in jb["s_mms"]:
                            i += 1
                            P.op("te", lambda e, dst_off=base + dst_off, w=w,
                                 lf=lf, rf=rf, sl=sl: e.matmul(
                                     s_ps[sl][:, dst_off:dst_off + w],
                                     lf(), rf(), start=True, stop=True),
                                 waits=[ready, cprep, exp_done[sl]],
                                 inc=([c_te] if i == alln else []))
                    smm = self.c_te.snap()
                    extra = []
                    ntri = sum(4 for jb in jobs if jb["tri"])
                    it = 0
                    for base, jb in zip((0, 512), jobs):
                        if not jb["tri"]:
                            continue
                        for i4 in range(4):
                            it += 1
                            P.op("dve", lambda e, o=base + i4 * 128, sl=sl:
                                 e.tensor_tensor(
                                     s_ps[sl][:, o:o + 128],
                                     s_ps[sl][:, o:o + 128],
                                     self.trib[:, :], ALU.add),
                                 waits=([smm] if it == 1 else []),
                                 inc=([c_dve] if it == ntri else []))
                    if ntri:
                        extra = [self.c_dve.snap()]
                    width = (512 + jobs[1]["width"]) if len(jobs) == 2 \
                        else jobs[0]["width"]
                    P.op("act", lambda e, sl=sl, width=width,
                         bias_ap=jobs[0]["bias"]: e.activation(
                             pt[sl][:, 0:width], s_ps[sl][:, 0:width],
                             AF.Exp, bias=bias_ap, scale=SCALE_ATT),
                         waits=[smm, av_done[sl]] + extra, inc=[c_act])
                    exp_done[sl] = c_act.snap()
                    na = sum(len(jb["avs"]) for jb in jobs)
                    i = 0
                    for base, jb in zip((0, 512), jobs):
                        for j, pt_off, vf in jb["avs"]:
                            i += 1
                            P.op("te", lambda e, j=j, pt_off=base + pt_off,
                                 vf=vf, sl=sl: e.matmul(
                                     o_tile(j), pt[sl][:, pt_off:pt_off + 128],
                                     vf(), start=False, stop=False),
                                 waits=[exp_done[sl]],
                                 inc=([c_te] if i == na else []))
                    av_done[sl] = self.c_te.snap()

                jl = []
                for kt in range(56):
                    jmin = kt // 8 + 1
                    w_all = (8 - jmin) * 128
                    off = 0
                    while off < w_all:
                        w = min(512, w_all - off)
                        q0 = qoff + jmin * 128 + off
                        jlo = jmin + off // 128
                        lf = (lambda bi=bi, kt=kt:
                              kbuf[bi][:, kt * 128:(kt + 1) * 128])
                        rf = (lambda q0=q0, w=w: self.q_sb[:, q0:q0 + w])
                        avs = [(j, (j - jlo) * 128,
                                (lambda bi=bi, kt=kt:
                                 vbuf[bi][:, kt * 144:kt * 144 + 129]))
                               for j in range(jlo, jlo + w // 128)]
                        jl.append(dict(s_mms=[(0, w, lf, rf)], width=w,
                                       bias=crep4[:, hh:hh + 1], avs=avs,
                                       tri=False, bkey=("c", hh)))
                        off += w
                for dd in range(8):
                    for half in range(2):
                        smms = []
                        avs = []
                        for i in range(4):
                            j = half * 4 + i
                            kk = 8 * j + dd
                            lf = (lambda bi=bi, kk=kk:
                                  kbuf[bi][:, kk * 128:(kk + 1) * 128])
                            q0 = qoff + j * 128
                            rf = (lambda q0=q0: self.q_sb[:, q0:q0 + 128])
                            smms.append((i * 128, 128, lf, rf))
                            avs.append((j, i * 128,
                                        (lambda bi=bi, kk=kk:
                                         vbuf[bi][:, kk * 144:kk * 144 + 129])))
                        jl.append(dict(
                            s_mms=smms, width=512,
                            bias=dbadj4[:, hh * JT + dd:hh * JT + dd + 1],
                            avs=avs, tri=False, bkey=("d", hh, dd)))
                for half in range(2):
                    smms = []
                    avs = []
                    for i in range(4):
                        j = half * 4 + i
                        k0 = gi * TOWN + j * 128
                        lf = (lambda k0=k0: self.kown[:, k0:k0 + 128])
                        q0 = qoff + j * 128
                        rf = (lambda q0=q0: self.q_sb[:, q0:q0 + 128])
                        smms.append((i * 128, 128, lf, rf))
                        avs.append((j, i * 128,
                                    (lambda q=gi * JT + j:
                                     self.vown[:, q * 129:(q + 1) * 129])))
                    jl.append(dict(s_mms=smms, width=512,
                                   bias=crep4[:, hh:hh + 1], avs=avs,
                                   tri=True, bkey=("t", hh)))
                p = 0
                while p < len(jl):
                    if (p + 1 < len(jl)
                            and jl[p]["bkey"] == jl[p + 1]["bkey"]):
                        emit_pair([jl[p], jl[p + 1]])
                        p += 2
                    else:
                        emit_pair([jl[p]])
                        p += 1
                all_av = self.c_te.snap()

                for j in range(JT):
                    P.op("dve", lambda e, j=j: e.reciprocal(
                        rcpb[:, j:j + 1], o_tile(j)[:, 128:129]),
                        waits=[all_av, c_dve], inc=[c_dve])
                    P.op("dve", lambda e, j=j: e.tensor_scalar(
                        sst[:, j * 128:(j + 1) * 128], o_tile(j)[:, 0:128],
                        rcpb[:, j:j + 1], None, ALU.mult),
                        waits=[c_dve, sst_free], inc=[c_dve])
                head_eplg = self.c_dve.snap()
                P.op("sp", lambda e, h=h: e.dma_start(
                    out=self.s_hbm.rearrange("(j p) e -> p j e", p=128)
                    [:, :, h * 128:(h + 1) * 128],
                    in_=sst[:, :].rearrange("p (j d) -> p j d", j=JT)),
                    waits=[head_eplg], inc=[c_sp])
                sst_free = self.c_sp.snap()
            group_done[bi] = self.c_te.snap()

        if DEBUG:
            fin = self.c_sp.snap()
            self.dbg_s = self.dram_out("dbg_s", [TOWN, E], F32)
            P.op("sp", lambda e: e.dma_start(out=self.dbg_s, in_=self.s_hbm),
                 waits=[fin], inc=[c_sp])

    def phase3(self, es):
        P = self.P
        c_sp, c_gp, c_te, c_act, c_dve = (self.c_sp, self.c_gp, self.c_te,
                                          self.c_act, self.c_dve)
        s = self.sb
        acc = [s(es, f"acc{i}", [128, JT * 512], F32) for i in (0, 1)]

        with ExitStack() as pa:
            rw1b = [self.sb(pa, f"rw1b{i}", [128, 4 * HID], BF16)
                    for i in (0, 1)]
            rwgb = [self.sb(pa, f"rwgb{i}", [128, 4 * HID], BF16)
                    for i in (0, 1)]
            rw2b = [self.sb(pa, f"rw2b{i}", [128, 8 * LAT], BF16)
                    for i in (0, 1)]
            abT = self.sb(pa, "abT", [128, 8 * TOWN], BF16)
            rbu = [self.sb(pa, f"rb3{i}", [128, 512], BF16) for i in (0, 1)]
            rb2u = [self.sb(pa, f"rb23{i}", [128, 512], BF16) for i in (0, 1)]
            ab_ps = [self.ps(pa, f"ab_ps{i}", [128, 1024]) for i in (0, 1)]
            y_ps = [self.ps(pa, f"y_ps{i}", [128, 512]) for i in (0, 1)]
            unit_free = [None, None]
            rb_free = [None, None]

            xlT_view = self.xlT[:, :].rearrange("p (l n) -> p l n", l=4)
            P.op("dve", lambda e: e.memset(acc[0][:, :], 0.0), inc=[c_dve])
            acc_zero = c_dve.snap()

            w_free = [None, None]
            y_free = [None, None]
            for ex in range(NEXP_RUN):
                wi = ex % 2
                P.op("gp", lambda e, ex=ex, wi=wi: e.dma_start(
                    out=rw1b[wi][:, :].rearrange("p (l n) -> p l n", l=4),
                    in_=self.rw1T_in[ex].rearrange("(l p) n -> p l n", p=128)),
                    waits=[w_free[wi]], inc=[c_gp])
                P.op("gp", lambda e, ex=ex, wi=wi: e.dma_start(
                    out=rwgb[wi][:, :].rearrange("p (l n) -> p l n", l=4),
                    in_=self.rwgT_in[ex].rearrange("(l p) n -> p l n", p=128)),
                    waits=[c_gp], inc=[c_gp])
                P.op("gp", lambda e, ex=ex, wi=wi: e.dma_start(
                    out=rw2b[wi][:, :].rearrange("p (hh n) -> p hh n", hh=8),
                    in_=self.rw2T_in[ex].rearrange("(hh p) n -> p hh n", p=128)),
                    waits=[c_gp], inc=[c_gp])
                wld = self.c_gp.snap()
                if DEBUG and ex == 0:
                    self.dbg_rwge = self.dram_out("dbg_rwge", [128, 4 * HID], BF16)
                    P.op("sp", lambda e: e.dma_start(out=self.dbg_rwge,
                                                     in_=rwgb[0][:, :]),
                         waits=[wld], inc=[c_sp])

                for hch in range(8):
                    for tc in range(2):
                        pu = (hch * 2 + tc) % 2
                        for lch in range(4):
                            P.op("te", lambda e, wi=wi, hch=hch, tc=tc,
                                 lch=lch, pu=pu: e.matmul(
                                     ab_ps[pu][:, 0:512],
                                     rw1b[wi][:, lch * HID + hch * 128:
                                              lch * HID + (hch + 1) * 128],
                                     xlT_view[:, lch, tc * 512:(tc + 1) * 512],
                                     start=(lch == 0), stop=(lch == 3)),
                                 waits=[wld, unit_free[pu]], inc=[])
                            P.op("te", lambda e, wi=wi, hch=hch, tc=tc,
                                 lch=lch, pu=pu: e.matmul(
                                     ab_ps[pu][:, 512:1024],
                                     rwgb[wi][:, lch * HID + hch * 128:
                                              lch * HID + (hch + 1) * 128],
                                     xlT_view[:, lch, tc * 512:(tc + 1) * 512],
                                     start=(lch == 0), stop=(lch == 3)),
                                 waits=[wld, unit_free[pu]],
                                 inc=([c_te] if lch == 3 else []))
                        abmm = self.c_te.snap()
                        P.op("act", lambda e, pu=pu: e.activation(
                            rbu[pu][:, :], ab_ps[pu][:, 512:1024], AF.Relu),
                            waits=[abmm, rb_free[pu]], inc=[c_act])
                        P.op("dve", lambda e, pu=pu: e.tensor_tensor(
                            rb2u[pu][:, :], rbu[pu][:, :], rbu[pu][:, :],
                            ALU.mult),
                            waits=[c_act], inc=[c_dve])
                        P.op("dve", lambda e, hch=hch, tc=tc, pu=pu:
                             e.tensor_tensor(
                                 abT[:, hch * TOWN + tc * 512:
                                     hch * TOWN + (tc + 1) * 512],
                                 ab_ps[pu][:, 0:512], rb2u[pu][:, :],
                                 ALU.mult),
                             waits=[c_dve], inc=[c_dve])
                        unit_free[pu] = self.c_dve.snap()
                        rb_free[pu] = unit_free[pu]
                ab_done = self.c_dve.snap()

                prev_acc = acc[ex % 2]
                next_acc = acc[(ex + 1) % 2]
                for t in range(JT):
                    yi = t % 2
                    for hch in range(8):
                        P.op("te", lambda e, wi=wi, t=t, hch=hch, yi=yi:
                             e.matmul(
                                 y_ps[yi][:, :],
                                 abT[:, hch * TOWN + t * 128:
                                     hch * TOWN + (t + 1) * 128],
                                 rw2b[wi][:, hch * LAT:(hch + 1) * LAT],
                                 start=(hch == 0), stop=(hch == 7)),
                             waits=[ab_done, y_free[yi]],
                             inc=([c_te] if hch == 7 else []))
                    ymm = self.c_te.snap()
                    P.op("dve", lambda e, ex=ex, t=t, yi=yi,
                         pa_=prev_acc, na=next_acc: e.scalar_tensor_tensor(
                             na[:, t * 512:(t + 1) * 512],
                             y_ps[yi][:, :],
                             self.wdense[:, t * NEXP + ex:t * NEXP + ex + 1],
                             pa_[:, t * 512:(t + 1) * 512],
                             ALU.mult, ALU.add),
                         waits=[ymm, acc_zero], inc=[c_dve])
                    y_free[yi] = self.c_dve.snap()
                w_free[wi] = self.c_dve.snap()
            acc_done = self.c_dve.snap()
            acc_fin = acc[NEXP_RUN % 2]
            if DEBUG:
                self.dbg_acc = self.dram_out("dbg_acc", [128, JT * 512], F32)
                P.op("sp", lambda e: e.dma_start(out=self.dbg_acc,
                                                 in_=acc_fin[:, :]),
                     waits=[acc_done], inc=[c_sp])
                self.dbg_ab = self.dram_out("dbg_ab", [128, 8 * TOWN], BF16)
                P.op("sp", lambda e: e.dma_start(out=self.dbg_ab,
                                                 in_=abT[:, :]),
                     waits=[acc_done], inc=[c_sp])
                self.dbg_xl2 = self.dram_out("dbg_xl2", [128, 4 * TOWN], BF16)
                P.op("sp", lambda e: e.dma_start(out=self.dbg_xl2,
                                                 in_=self.xlT[:, :]),
                     waits=[acc_done], inc=[c_sp])


        with ExitStack() as pb:
            sw2b = self.sb(pb, "sw2b", [128, 8 * E], BF16)
            wupb = self.sb(pb, "wupb", [128, 4 * E], BF16)
            accb = self.sb(pb, "accb", [128, JT * 512], BF16)
            accT = self.sb(pb, "accT", [128, 4 * TOWN], BF16)
            gbuf = self.sb(pb, "gbuf", [128, E], F32)
            stile = self.sb(pb, "stile", [128, E], F32)
            xtile = self.sb(pb, "xtile", [128, E], F32)
            outst = self.sb(pb, "outst", [128, E], F32)
            o4_ps = self.ps(pb, "o4_ps", [128, 2048])
            tp2_ps = self.ps(pb, "tp2_ps", [128, 512], BF16)

            P.op("gp", lambda e: e.dma_start(
                out=sw2b[:, :].rearrange("p (hh n) -> p hh n", hh=8),
                in_=self.sw2T_in.rearrange("(hh p) n -> p hh n", p=128)),
                waits=[acc_done], inc=[c_gp])
            P.op("gp", lambda e: e.dma_start(
                out=wupb[:, :].rearrange("p (l n) -> p l n", l=4),
                in_=self.wupT_in.rearrange("(l p) n -> p l n", p=128)),
                waits=[acc_done], inc=[c_gp])
            wld2 = self.c_gp.snap()

            P.op("dve", lambda e: e.tensor_copy(accb[:, :], acc_fin[:, :]),
                 waits=[acc_done], inc=[c_dve])
            accb_done = self.c_dve.snap()
            tp_free = None
            for t in range(JT):
                for lch in range(4):
                    P.op("te", lambda e, t=t, lch=lch: e.transpose(
                        tp2_ps[:, lch * 128:(lch + 1) * 128],
                        accb[:, t * 512 + lch * 128: t * 512 + (lch + 1) * 128],
                        self.identb[:, :]),
                        waits=[accb_done, tp_free],
                        inc=([c_te] if lch == 3 else []))
                tmm = self.c_te.snap()
                P.op("dve", lambda e, t=t: e.tensor_copy(
                    accT[:, :].rearrange("p (l n) -> p l n", l=4)
                    [:, :, t * 128:(t + 1) * 128],
                    tp2_ps[:, :].rearrange("p (l n) -> p l n", l=4)),
                    waits=[tmm], inc=[c_dve])
                tp_free = self.c_dve.snap()
            accT_done = self.c_dve.snap()

            o4_free = None
            st_free = acc_done
            for t in range(JT):
                P.op("sp", lambda e, t=t: e.dma_start(
                    out=stile[:, :],
                    in_=self.s_hbm[t * 128:(t + 1) * 128, :]),
                    waits=[st_free], inc=[c_sp])
                P.op("sp", lambda e, t=t: e.dma_start(
                    out=xtile[:, :],
                    in_=self.x_own[t * 128:(t + 1) * 128, :]),
                    waits=[st_free], inc=[c_sp])
                sx_in = self.c_sp.snap()
                for nch in range(4):
                    for hch in range(8):
                        P.op("te", lambda e, t=t, nch=nch, hch=hch: e.matmul(
                            o4_ps[:, nch * 512:(nch + 1) * 512],
                            self.hhT[:, hch * TOWN + t * 128:
                                     hch * TOWN + (t + 1) * 128],
                            sw2b[:, hch * E + nch * 512:
                                 hch * E + (nch + 1) * 512],
                            start=(hch == 0), stop=False),
                            waits=[wld2, o4_free], inc=[])
                    for lch in range(4):
                        P.op("te", lambda e, t=t, nch=nch, lch=lch: e.matmul(
                            o4_ps[:, nch * 512:(nch + 1) * 512],
                            accT[:, lch * TOWN + t * 128:
                                 lch * TOWN + (t + 1) * 128],
                            wupb[:, lch * E + nch * 512:
                                 lch * E + (nch + 1) * 512],
                            start=False, stop=(lch == 3)),
                            waits=[accT_done, wld2, o4_free],
                            inc=([c_te] if lch == 3 and nch == 3 else []))
                omm = self.c_te.snap()
                P.op("act", lambda e: e.activation(
                    gbuf[:, :], o4_ps[:, :], AF.Silu),
                    waits=[omm], inc=[c_act])
                o4_free = self.c_act.snap()
                if DEBUG and t == 0:
                    self.dbg_g = self.dram_out("dbg_g", [128, E], F32)
                    P.op("sp", lambda e: e.dma_start(out=self.dbg_g,
                                                     in_=gbuf[:, :]),
                         waits=[o4_free], inc=[c_sp])
                P.op("dve", lambda e: e.tensor_tensor(
                    outst[:, :], stile[:, :], gbuf[:, :], ALU.mult),
                    waits=[c_act, sx_in, st_free], inc=[c_dve])
                P.op("dve", lambda e: e.tensor_tensor(
                    outst[:, :], outst[:, :], xtile[:, :], ALU.add),
                    waits=[c_dve], inc=[c_dve])
                comb = self.c_dve.snap()
                P.op("sp", lambda e, t=t: e.dma_start(
                    out=self.out[t * 128:(t + 1) * 128, :], in_=outst[:, :]),
                    waits=[comb], inc=[c_sp])
                st_free = self.c_sp.snap()

    def finale(self):
        self.P.op("sp", None, waits=[self.c_sp.snap()])



def nc_sem_stack(nc, ctrs):
    class _S:
        def __enter__(self):
            self.es = ExitStack()
            for c in ctrs:
                c.sem = self.es.enter_context(nc.semaphore(c.name))
            return self

        def __exit__(self, *a):
            self.es.close()
            return False
    return _S()


def kernel(**inputs):
    maps = _stage_inputs(inputs)
    b = Builder()
    nc = b.build()
    res = run_bass_kernel_spmd(nc, maps, core_ids=list(range(NCORES)))
    if DEBUG:
        return res
    return _assemble(res.results)


# version 05
# speedup vs baseline: 1.8515x; 1.1734x over previous
import math
from contextlib import ExitStack

import numpy as np
import ml_dtypes

import concourse.bass as bass
import concourse.mybir as mybir
from concourse.bass_utils import run_bass_kernel_spmd

F32 = mybir.dt.float32
BF16 = mybir.dt.bfloat16
AF = mybir.ActivationFunctionType
ALU = mybir.AluOpType
AX = mybir.AxisListType
BF = ml_dtypes.bfloat16

NCORES = 8
T, E, H, G, D = 8192, 2048, 16, 4, 128
HID, LAT, NEXP = 1024, 512, 16
TOWN = T // NCORES
NTILE = T // 128
JT = 8
ECH = E // 128
SCALE_ATT = 1.0 / math.sqrt(D)
NEG = -1.0e30
EPS = 1e-5
RSCALE = 2.5

DEBUG = 0
PHASES = 99
NEXP_RUN = NEXP
PIPE_P1 = 1


class Ctr:
    def __init__(self, name, step=1):
        self.name, self.step, self.val, self.sem = name, step, 0, None

    def bump(self):
        self.val += self.step
        return self.val

    def snap(self):
        return (self, self.val)


class Prog:
    ORDER = ("sp", "gp", "te", "act", "dve")
    QNAME = {"sp": "sync", "gp": "gpsimd", "te": "tensor",
             "act": "scalar", "dve": "vector"}

    def __init__(self):
        self.ops = []

    def op(self, eng, fn, waits=(), inc=()):
        ws = []
        for w in waits:
            if w is None:
                continue
            ws.append((w, w.val) if isinstance(w, Ctr) else w)
        incs = [(c, c.bump()) for c in inc]
        self.ops.append((eng, fn, ws, incs))

    def barrier(self, ctrs):
        snap = [c.snap() for c in ctrs]
        for eng in self.ORDER:
            self.ops.append((eng, None, snap, []))

    def replay(self, block):
        for name in self.ORDER:
            mine = [(fn, ws, incs) for (e, fn, ws, incs) in self.ops
                    if e == name]
            if not mine:
                continue

            def section(eng, mine=mine):
                for fn, ws, incs in mine:
                    for c, v in ws:
                        if v > 0:
                            eng.wait_ge(c.sem, v)
                    if fn is None:
                        continue
                    inst = fn(eng)
                    for c, _ in incs:
                        inst = inst.then_inc(c.sem, c.step)

            getattr(block, self.QNAME[name])(section)


def _stage_inputs(ins):
    x = np.ascontiguousarray(np.asarray(ins["L0_attn_x"]).reshape(T, E))
    asnp = lambda a: np.ascontiguousarray(np.asarray(a))
    base = {
        "x_full": x,
        "wq": asnp(ins["L0_attn_Wq"]), "wk": asnp(ins["L0_attn_Wk"]),
        "wv": asnp(ins["L0_attn_Wv"]),
        "bqT": asnp(np.asarray(ins["L0_attn_bq"]).T),
        "bkT": asnp(np.asarray(ins["L0_attn_bk"]).T),
        "bvT": asnp(np.asarray(ins["L0_attn_bv"]).T),
        "sw1T": asnp(np.asarray(ins["L0_ffn_sw1"]).T),
        "swgT": asnp(np.asarray(ins["L0_ffn_swg"]).T),
        "sw2T": asnp(np.asarray(ins["L0_ffn_sw2"]).T),
        "wdT": asnp(np.asarray(ins["L0_ffn_Wdown"]).T),
        "wgT": asnp(np.asarray(ins["L0_ffn_Wg"]).T),
        "wupT": asnp(np.asarray(ins["L0_ffn_Wup"]).T),
        "rw1T": asnp(np.asarray(ins["L0_ffn_rw1"]).transpose(0, 2, 1)),
        "rwgT": asnp(np.asarray(ins["L0_ffn_rwg"]).transpose(0, 2, 1)),
        "rw2T": asnp(np.asarray(ins["L0_ffn_rw2"]).transpose(0, 2, 1)),
        "btrow": asnp(np.tile(np.asarray(ins["L0_ffn_biases"]), JT)[None, :]),
        "ident": np.eye(128, dtype=BF),
        "tri": np.where(np.arange(128)[:, None] > np.arange(128)[None, :],
                        np.float32(NEG), np.float32(0.0)).astype(np.float32),
        "eps": np.full((128, 1), EPS, np.float32),
        "onesb": np.ones((1, 128), BF),
        "onesf": np.ones((1, 128), np.float32),
    }
    xt = x.reshape(NTILE, 128, E)
    maps = []
    for c in range(NCORES):
        m = dict(base)
        m["x_own"] = np.ascontiguousarray(xt[c::8].reshape(TOWN, E))
        db = np.zeros((128, JT), np.float32)
        db[:, c:] = NEG
        m["dbias"] = db
        maps.append(m)
    return maps


def _assemble(results):
    out = np.empty((T, E), np.float32)
    ot = out.reshape(NTILE, 128, E)
    for c in range(NCORES):
        ot[c::8] = np.asarray(results[c]["out"]).reshape(JT, 128, E)
    return out.reshape(1, T, E)


class Builder:
    def __init__(self):
        self.nc = bass.Bass("TRN2", target_bir_lowering=False)
        self.P = Prog()
        self.c_sp = Ctr("c_sp", 16)
        self.c_gp = Ctr("c_gp", 16)
        self.c_te = Ctr("c_te", 1)
        self.c_act = Ctr("c_act", 1)
        self.c_dve = Ctr("c_dve", 1)
        self.ctrs = [self.c_sp, self.c_gp, self.c_te, self.c_act, self.c_dve]

    def dram_in(self, name, shape, dtype=F32):
        return self.nc.dram_tensor(name, shape, dtype, kind="ExternalInput").ap()

    def dram_out(self, name, shape, dtype=F32):
        return self.nc.dram_tensor(name, shape, dtype, kind="ExternalOutput").ap()

    def declare(self):
        di = self.dram_in
        self.x_full = di("x_full", [T, E])
        self.x_own = di("x_own", [TOWN, E])
        self.wq_in = di("wq", [H, E, D])
        self.wk_in = di("wk", [G, E, D])
        self.wv_in = di("wv", [G, E, D])
        self.bqT_in = di("bqT", [D, H])
        self.bkT_in = di("bkT", [D, G])
        self.bvT_in = di("bvT", [D, G])
        self.sw1T_in = di("sw1T", [E, HID])
        self.swgT_in = di("swgT", [E, HID])
        self.sw2T_in = di("sw2T", [HID, E])
        self.wdT_in = di("wdT", [E, LAT])
        self.wgT_in = di("wgT", [E, NEXP])
        self.wupT_in = di("wupT", [LAT, E])
        self.rw1T_in = di("rw1T", [NEXP, LAT, HID])
        self.rwgT_in = di("rwgT", [NEXP, LAT, HID])
        self.rw2T_in = di("rw2T", [NEXP, HID, LAT])
        self.btrow_in = di("btrow", [1, JT * NEXP])
        self.ident_in = di("ident", [128, 128], BF16)
        self.tri_in = di("tri", [128, 128])
        self.eps_in = di("eps", [128, 1])
        self.onesb_in = di("onesb", [1, 128], BF16)
        self.onesf_in = di("onesf", [1, 128])
        self.dbias_in = di("dbias", [128, JT])
        self.out = self.dram_out("out", [TOWN, E])
        self.k_hbm = self.nc.dram_tensor("k_hbm", [G, D, T], BF16).ap()
        self.v_hbm = self.nc.dram_tensor("v_hbm", [G, D, T], BF16).ap()
        self.s_hbm = self.nc.dram_tensor("s_hbm", [TOWN, E], F32).ap()

    def sb(self, es, name, shape, dtype):
        return es.enter_context(self.nc.sbuf_tensor(name, shape, dtype))

    def ps(self, es, name, shape, dtype=F32):
        return es.enter_context(self.nc.psum_tensor(name, shape, dtype))

    def bar(self):
        self.P.barrier(self.ctrs)

    def ln_chunk(self, bufs, src_dma_fn, xnT_write_fn, st):
        P = self.P
        c_sp, c_te, c_act, c_dve = self.c_sp, self.c_te, self.c_act, self.c_dve
        xt, sqs, stats = bufs["xt"], bufs["sqs"], bufs["stats"]
        msq, var4, xnb, tp_ps = (bufs["msq"], bufs["var4"], bufs["xnb"],
                                 bufs["tp_ps"])
        P.op("sp", src_dma_fn, waits=[st.get("xt_free")], inc=[c_sp])
        xin = c_sp.snap()
        for t in range(4):
            P.op("dve", lambda e, t=t: e.reduce_sum(
                stats[:, t:t + 1], xt[:, t * E:(t + 1) * E], axis=AX.X),
                waits=[xin, st.get("stats_free")], inc=[c_dve])
            P.op("act", lambda e, t=t: e.activation(
                sqs[:, 0:E],
                xt[:, t * E:(t + 1) * E], AF.Square,
                accum_out=stats[:, 4 + t:5 + t]),
                waits=[xin, st.get("stats_free")],
                inc=[c_act])
        sums_done = c_dve.snap()
        sqs_done = c_act.snap()
        P.op("dve", lambda e: e.tensor_scalar(
            stats[:, 8:12], stats[:, 0:4], 1.0 / E, None, ALU.mult),
            waits=[sums_done], inc=[c_dve])
        P.op("dve", lambda e: e.tensor_tensor(
            msq[:, :], stats[:, 8:12], stats[:, 8:12], ALU.mult),
            waits=[c_dve], inc=[c_dve])
        P.op("dve", lambda e: e.scalar_tensor_tensor(
            var4[:, :], stats[:, 4:8], 1.0 / E, msq[:, :],
            ALU.mult, ALU.subtract),
            waits=[sqs_done, c_dve], inc=[c_dve])
        P.op("act", lambda e: e.activation(
            var4[:, :], var4[:, :], AF.Ln, bias=self.epsb[:, 0:1]),
            waits=[c_dve], inc=[c_act])
        P.op("act", lambda e: e.activation(
            stats[:, 12:16], var4[:, :], AF.Exp, scale=-0.5),
            waits=[c_act], inc=[c_act])
        rstd_done = c_act.snap()
        for t in range(4):
            P.op("dve", lambda e, t=t: e.tensor_scalar(
                xnb[:, t * E:(t + 1) * E], xt[:, t * E:(t + 1) * E],
                stats[:, 8 + t:9 + t], stats[:, 12 + t:13 + t],
                ALU.subtract, ALU.mult),
                waits=[rstd_done], inc=[c_dve])
        xnb_done = c_dve.snap()
        st["xt_free"] = xnb_done
        st["stats_free"] = xnb_done
        tp_ps2 = bufs["tp_ps2"]
        for b8 in range(8):
            tp = tp_ps if b8 % 2 == 0 else tp_ps2
            for k in range(8):
                idx = b8 * 8 + k
                t, ech = idx // ECH, idx % ECH
                P.op("te", lambda e, t=t, ech=ech, k=k, tp=tp: e.transpose(
                    tp[:, k * 128:(k + 1) * 128],
                    xnb[:, t * E + ech * 128: t * E + (ech + 1) * 128],
                    self.identb[:, :]),
                    waits=[xnb_done, st.get(f"tp_free{b8 % 2}")],
                    inc=([c_te] if k == 7 else []))
            tp_done = c_te.snap()
            t0, ech0 = (b8 * 8) // ECH, (b8 * 8) % ECH
            P.op("dve", lambda e, b8=b8, t0=t0, ech0=ech0:
                 xnT_write_fn(e, b8, t0, ech0),
                 waits=[tp_done, st.get("xnT_free")], inc=[c_dve])
            st[f"tp_free{b8 % 2}"] = c_dve.snap()
        return c_dve.snap()

    def build(self):
        self.declare()
        with ExitStack() as outer:
            with nc_sem_stack(self.nc, self.ctrs):
                s = self.sb
                self.identb = s(outer, "identb", [128, 128], BF16)
                self.trib = s(outer, "trib", [128, 128], F32)
                self.epsb = s(outer, "epsb", [128, 1], F32)
                self.onesbb = s(outer, "onesbb", [1, 128], BF16)
                self.onesfb = s(outer, "onesfb", [1, 128], F32)
                self.onescol = s(outer, "onescol", [128, 1], BF16)
                self.dbiasb = s(outer, "dbiasb", [128, JT], F32)
                self.bqTb = s(outer, "bqTb", [128, H], F32)
                self.bkTb = s(outer, "bkTb", [128, G], F32)
                self.bvTb = s(outer, "bvTb", [128, G], F32)
                self.brep8 = s(outer, "brep8", [128, JT * NEXP], F32)
                self.wdense = s(outer, "wdense", [128, JT * NEXP], F32)

                with self.nc.Block() as block:
                    with ExitStack() as p0:
                        self.phase0(p0)
                    self.bar()
                    if PHASES >= 1:
                        with ExitStack() as p1:
                            self.phase1(p1)
                        self.bar()
                    if PHASES >= 2:
                        self.hhT = s(outer, "hhT", [128, 8 * TOWN], BF16)
                        self.xlT = s(outer, "xlT", [128, 4 * TOWN], BF16)
                        with ExitStack() as mid:
                            self.xnT_own = s(mid, "xnT_own",
                                             [128, ECH * TOWN], BF16)
                            self.q_sb = s(mid, "q_sb", [128, H * TOWN], BF16)
                            self.kown = s(mid, "kown", [128, G * TOWN], BF16)
                            self.vown = s(mid, "vown", [128, G * JT * 129],
                                          BF16)
                            with ExitStack() as p1b:
                                self.phase1b(p1b)
                            self.bar()
                            if PHASES >= 3:
                                with ExitStack() as p15:
                                    self.phase15(p15)
                                self.bar()
                            if PHASES >= 4:
                                with ExitStack() as p2:
                                    self.phase2(p2)
                                self.bar()
                        if PHASES >= 5:
                            with ExitStack() as p3:
                                self.phase3(p3)
                            self.bar()
                    self.finale()
                    self.P.replay(block)
        return self.nc

    def phase0(self, es):
        P = self.P
        c_sp, c_te, c_dve = self.c_sp, self.c_te, self.c_dve
        for dst, src in [
            (self.identb, self.ident_in), (self.trib, self.tri_in),
            (self.epsb, self.eps_in), (self.onesbb, self.onesb_in), (self.onesfb, self.onesf_in),
            (self.dbiasb, self.dbias_in), (self.bqTb, self.bqT_in),
            (self.bkTb, self.bkT_in), (self.bvTb, self.bvT_in),
        ]:
            P.op("sp", lambda e, d=dst, s=src: e.dma_start(out=d[:, :], in_=s),
                 inc=[c_sp])
        btrow = self.sb(es, "btrow_s", [1, JT * NEXP], F32)
        P.op("sp", lambda e: e.dma_start(out=btrow[:, :], in_=self.btrow_in),
             inc=[c_sp])
        ld = c_sp.snap()
        P.op("dve", lambda e: e.memset(self.onescol[:, :], 1.0), inc=[c_dve])
        rep_ps = self.ps(es, "rep_ps", [128, JT * NEXP])
        P.op("te", lambda e: e.matmul(rep_ps[:, 0:JT * NEXP],
                                      self.onesfb[0:1, :], btrow[0:1, :],
                                      start=True, stop=True),
             waits=[ld], inc=[c_te])
        mm2 = self.c_te.snap()
        P.op("dve", lambda e: e.tensor_copy(self.brep8[:, :],
                                            rep_ps[:, 0:JT * NEXP]),
             waits=[mm2], inc=[c_dve])

    def phase1(self, es):
        P = self.P
        c_sp, c_gp, c_te, c_act, c_dve = (self.c_sp, self.c_gp, self.c_te,
                                          self.c_act, self.c_dve)
        s = self.sb
        wkb = s(es, "wkb", [128, G * ECH * 128], BF16)
        wvb = s(es, "wvb", [128, G * ECH * 128], BF16)
        bufs2 = []
        tps = [self.ps(es, f"tp_ps{i}", [128, 1024], BF16) for i in range(4)]
        for i in (0, 1):
            tp_a, tp_b = tps[2 * i], tps[2 * i + 1]
            bufs2.append({
                "xt": s(es, f"xt{i}", [128, 4 * E], F32),
                "sqs": s(es, f"sqs{i}", [128, E], F32),
                "stats": s(es, f"stats{i}", [128, 16], F32),
                "msq": s(es, f"msq{i}", [128, 4], F32),
                "var4": s(es, f"var4{i}", [128, 4], F32),
                "xnb": s(es, f"xnb{i}", [128, 4 * E], BF16),
                "tp_ps": tp_a,
                "tp_ps2": tp_b,
            })
        xnT = [s(es, f"xnT{i}", [128, ECH * 512], BF16) for i in (0, 1)]
        kst = [s(es, f"kst{i}", [128, G * 512], BF16) for i in (0, 1)]
        vst = [s(es, f"vst{i}", [128, G * 512], BF16) for i in (0, 1)]
        kv_ps = [self.ps(es, f"kv_ps{i}", [128, 512]) for i in range(4)]

        for gi in range(G):
            P.op("gp", lambda e, gi=gi: e.dma_start(
                out=wkb[:, gi * ECH * 128:(gi + 1) * ECH * 128]
                .rearrange("p (c d) -> p c d", c=ECH),
                in_=self.wk_in[gi].rearrange("(c p) d -> p c d", p=128)),
                inc=[c_gp])
            P.op("gp", lambda e, gi=gi: e.dma_start(
                out=wvb[:, gi * ECH * 128:(gi + 1) * ECH * 128]
                .rearrange("p (c d) -> p c d", c=ECH),
                in_=self.wv_in[gi].rearrange("(c p) d -> p c d", p=128)),
                inc=[c_gp])
        wload = c_gp.snap()

        st2 = [{}, {}]
        kst_free = [None, None]
        vst_free = [None, None]
        xnT_free = [None, None]

        prev_vevac = [None]

        def kv_proj(ch, bi, xnT_done):
            for gi in range(G):
                for ech in range(ECH):
                    P.op("te", lambda e, gi=gi, ech=ech, bi=bi: e.matmul(
                        kv_ps[gi][:, :],
                        wkb[:, (gi * ECH + ech) * 128:(gi * ECH + ech + 1) * 128],
                        xnT[bi][:, ech * 512:(ech + 1) * 512],
                        start=(ech == 0), stop=(ech == ECH - 1)),
                        waits=[xnT_done, wload, kst_free[bi], vst_free[bi],
                               prev_vevac[0]],
                        inc=([c_te] if ech == ECH - 1 else []))
            kmm_done = self.c_te.snap()
            for gi in range(G):
                P.op("act", lambda e, gi=gi, bi=bi: e.activation(
                    kst[bi][:, gi * 512:(gi + 1) * 512], kv_ps[gi][:, :],
                    AF.Identity, bias=self.bkTb[:, gi:gi + 1]),
                    waits=[kmm_done], inc=([c_act] if gi == G - 1 else []))
            kst_done = self.c_act.snap()
            for gi in range(G):
                P.op("sp", lambda e, ch=ch, gi=gi, bi=bi: e.dma_start(
                    out=self.k_hbm[gi, :, ch * 512:(ch + 1) * 512],
                    in_=kst[bi][:, gi * 512:(gi + 1) * 512]),
                    waits=[kst_done], inc=[c_sp])
            kst_free[bi] = self.c_sp.snap()
            for gi in range(G):
                for ech in range(ECH):
                    P.op("te", lambda e, gi=gi, ech=ech, bi=bi: e.matmul(
                        kv_ps[gi][:, :],
                        wvb[:, (gi * ECH + ech) * 128:(gi * ECH + ech + 1) * 128],
                        xnT[bi][:, ech * 512:(ech + 1) * 512],
                        start=(ech == 0), stop=(ech == ECH - 1)),
                        waits=[xnT_done, wload, kst_done, vst_free[bi]],
                        inc=([c_te] if ech == ECH - 1 else []))
            vmm_done = self.c_te.snap()
            xnT_free[bi] = vmm_done
            for gi in range(G):
                P.op("act", lambda e, gi=gi, bi=bi: e.activation(
                    vst[bi][:, gi * 512:(gi + 1) * 512], kv_ps[gi][:, :],
                    AF.Identity, bias=self.bvTb[:, gi:gi + 1]),
                    waits=[vmm_done], inc=([c_act] if gi == G - 1 else []))
            vst_done = self.c_act.snap()
            prev_vevac[0] = vst_done
            for gi in range(G):
                P.op("sp", lambda e, ch=ch, gi=gi, bi=bi: e.dma_start(
                    out=self.v_hbm[gi, :, ch * 512:(ch + 1) * 512],
                    in_=vst[bi][:, gi * 512:(gi + 1) * 512]),
                    waits=[vst_done], inc=[c_sp])
            vst_free[bi] = self.c_sp.snap()

        xnT_prev = None
        for ch in range(T // 512):
            bi = ch % 2
            bufs = bufs2[bi]
            st = st2[bi]

            def src_dma(e, ch=ch, bufs=bufs):
                return e.dma_start(
                    out=bufs["xt"][:, :].rearrange("p (t e) -> p t e", t=4),
                    in_=self.x_full[512 * ch:512 * (ch + 1), :]
                    .rearrange("(t p) e -> p t e", p=128))

            def xnT_wr(e, b8, t0, ech0, bi=bi, bufs=bufs):
                tp = bufs["tp_ps"] if b8 % 2 == 0 else bufs["tp_ps2"]
                return e.tensor_copy(
                    xnT[bi][:, :].rearrange("p (c n) -> p c n", c=ECH)
                    [:, ech0:ech0 + 8, t0 * 128:(t0 + 1) * 128],
                    tp[:, :].rearrange("p (k n) -> p k n", k=8))

            st["xnT_free"] = xnT_free[bi]
            xnT_done_cur = self.ln_chunk(bufs, src_dma, xnT_wr, st)
            if PIPE_P1:
                if xnT_prev is not None:
                    kv_proj(ch - 1, 1 - bi, xnT_prev)
                xnT_prev = xnT_done_cur
            else:
                kv_proj(ch, bi, xnT_done_cur)
        if PIPE_P1:
            kv_proj(T // 512 - 1, (T // 512 - 1) % 2, xnT_prev)

        if DEBUG:
            fin = self.c_sp.snap()
            self.dbg_khbm = self.dram_out("dbg_khbm", [G, D, T], BF16)
            self.dbg_vhbm = self.dram_out("dbg_vhbm", [G, D, T], BF16)
            P.op("sp", lambda e: e.dma_start(out=self.dbg_khbm,
                                             in_=self.k_hbm), waits=[fin],
                 inc=[c_sp])
            P.op("sp", lambda e: e.dma_start(out=self.dbg_vhbm,
                                             in_=self.v_hbm), waits=[fin],
                 inc=[c_sp])

    def phase1b(self, es):
        P = self.P
        c_sp, c_gp, c_te, c_act, c_dve = (self.c_sp, self.c_gp, self.c_te,
                                          self.c_act, self.c_dve)
        s = self.sb
        wkb = [s(es, f"wkb2{i}", [128, ECH * 128], BF16) for i in (0, 1)]
        wvb = [s(es, f"wvb2{i}", [128, ECH * 128], BF16) for i in (0, 1)]
        wqb = [s(es, f"wqb{i}", [128, ECH * 128], BF16) for i in (0, 1)]
        bufs = {
            "xt": s(es, "xtb", [128, 4 * E], F32),
            "sqs": s(es, "sqsb", [128, E], F32),
            "stats": s(es, "statsb", [128, 16], F32),
            "msq": s(es, "msqb", [128, 4], F32),
            "var4": s(es, "var4b", [128, 4], F32),
            "xnb": s(es, "xnbb", [128, 4 * E], BF16),
            "tp_ps": self.ps(es, "tp_psb", [128, 1024], BF16),
            "tp_ps2": self.ps(es, "tp_psb2", [128, 1024], BF16),
        }
        mm_ps = [self.ps(es, f"mm_psb{i}", [128, 512]) for i in range(4)]
        vtb = s(es, "vtb", [128, 1024], BF16)
        vt_free = None
        vtp_free = [None, None]

        xnT_view = self.xnT_own[:, :].rearrange("p (c n) -> p c n", c=ECH)
        st = {}
        for ch in range(TOWN // 512):
            def src_dma(e, ch=ch):
                return e.dma_start(
                    out=bufs["xt"][:, :].rearrange("p (t e) -> p t e", t=4),
                    in_=self.x_own[512 * ch:512 * (ch + 1), :]
                    .rearrange("(t p) e -> p t e", p=128))

            def xnT_wr(e, b8, t0, ech0, ch=ch):
                tp = bufs["tp_ps"] if b8 % 2 == 0 else bufs["tp_ps2"]
                return e.tensor_copy(
                    xnT_view[:, ech0:ech0 + 8,
                             ch * 512 + t0 * 128: ch * 512 + (t0 + 1) * 128],
                    tp[:, :].rearrange("p (k n) -> p k n", k=8))

            self.ln_chunk(bufs, src_dma, xnT_wr, st)
        xnT_done = self.c_dve.snap()

        wq_free = [None, None]
        ps_free = [None, None, None, None]
        for h in range(H):
            wi = h % 2
            P.op("gp", lambda e, h=h, wi=wi: e.dma_start(
                out=wqb[wi][:, :].rearrange("p (c d) -> p c d", c=ECH),
                in_=self.wq_in[h].rearrange("(c p) d -> p c d", p=128)),
                waits=[wq_free[wi]], inc=[c_gp])
            wq_ld = c_gp.snap()
            for tc in range(2):
                pi = tc
                for ech in range(ECH):
                    P.op("te", lambda e, wi=wi, ech=ech, tc=tc, pi=pi: e.matmul(
                        mm_ps[pi][:, :],
                        wqb[wi][:, ech * 128:(ech + 1) * 128],
                        xnT_view[:, ech, tc * 512:(tc + 1) * 512],
                        start=(ech == 0), stop=(ech == ECH - 1)),
                        waits=[xnT_done, wq_ld, ps_free[pi]],
                        inc=([c_te] if ech == ECH - 1 else []))
                qmm = self.c_te.snap()
                P.op("act", lambda e, h=h, tc=tc, pi=pi: e.activation(
                    self.q_sb[:, h * TOWN + tc * 512:
                              h * TOWN + (tc + 1) * 512],
                    mm_ps[pi][:, :], AF.Identity, bias=self.bqTb[:, h:h + 1]),
                    waits=[qmm], inc=[c_act])
                ps_free[pi] = self.c_act.snap()
            wq_free[wi] = self.c_act.snap()

        P.op("dve", lambda e: e.memset(
            self.vown[:, :].rearrange("p (q s) -> p q s", s=129)
            [:, :, 128:129], 1.0), inc=[c_dve])
        ones_set = self.c_dve.snap()

        kvw_free = [None, None]
        for gi in range(G):
            wi = gi % 2
            P.op("gp", lambda e, gi=gi, wi=wi: e.dma_start(
                out=wkb[wi][:, :].rearrange("p (c d) -> p c d", c=ECH),
                in_=self.wk_in[gi].rearrange("(c p) d -> p c d", p=128)),
                waits=[kvw_free[wi]], inc=[c_gp])
            P.op("gp", lambda e, gi=gi, wi=wi: e.dma_start(
                out=wvb[wi][:, :].rearrange("p (c d) -> p c d", c=ECH),
                in_=self.wv_in[gi].rearrange("(c p) d -> p c d", p=128)),
                waits=[kvw_free[wi]], inc=[c_gp])
            kv_wload = c_gp.snap()
            for tc in range(2):
                pi = 2 + tc
                for ech in range(ECH):
                    P.op("te", lambda e, wi=wi, tc=tc, ech=ech, pi=pi: e.matmul(
                        mm_ps[pi][:, :],
                        wkb[wi][:, ech * 128:(ech + 1) * 128],
                        xnT_view[:, ech, tc * 512:(tc + 1) * 512],
                        start=(ech == 0), stop=(ech == ECH - 1)),
                        waits=[xnT_done, kv_wload, ps_free[pi]],
                        inc=([c_te] if ech == ECH - 1 else []))
                kmm = self.c_te.snap()
                P.op("act", lambda e, gi=gi, tc=tc, pi=pi: e.activation(
                    self.kown[:, gi * TOWN + tc * 512:
                              gi * TOWN + (tc + 1) * 512],
                    mm_ps[pi][:, :], AF.Identity, bias=self.bkTb[:, gi:gi + 1]),
                    waits=[kmm], inc=[c_act])
                ps_free[pi] = self.c_act.snap()
            for tc in range(2):
                pi = 2 + tc
                for ech in range(ECH):
                    P.op("te", lambda e, wi=wi, tc=tc, ech=ech, pi=pi: e.matmul(
                        mm_ps[pi][:, :],
                        wvb[wi][:, ech * 128:(ech + 1) * 128],
                        xnT_view[:, ech, tc * 512:(tc + 1) * 512],
                        start=(ech == 0), stop=(ech == ECH - 1)),
                        waits=[xnT_done, kv_wload, ps_free[pi]],
                        inc=([c_te] if ech == ECH - 1 else []))
                vmm = self.c_te.snap()
                P.op("act", lambda e, gi=gi, tc=tc, pi=pi: e.activation(
                    vtb[:, tc * 512:(tc + 1) * 512], mm_ps[pi][:, :],
                    AF.Identity, bias=self.bvTb[:, gi:gi + 1]),
                    waits=[vmm, vt_free], inc=[c_act])
                ps_free[pi] = self.c_act.snap()
            vt_done = self.c_act.snap()
            for half in range(2):
                tp = bufs["tp_ps"] if half == 0 else bufs["tp_ps2"]
                for k in range(4):
                    t = half * 4 + k
                    P.op("te", lambda e, t=t, k=k, tp=tp: e.transpose(
                        tp[:, k * 128:(k + 1) * 128],
                        vtb[:, t * 128:(t + 1) * 128], self.identb[:, :]),
                        waits=[vt_done, vtp_free[half]],
                        inc=([c_te] if k == 3 else []))
                tdone = self.c_te.snap()
                P.op("dve", lambda e, gi=gi, half=half, tp=tp: e.tensor_copy(
                    self.vown[:, :].rearrange("p (q s) -> p q s", s=129)
                    [:, gi * JT + half * 4: gi * JT + (half + 1) * 4, 0:128],
                    tp[:, 0:512].rearrange("p (k n) -> p k n", k=4)),
                    waits=[tdone, ones_set], inc=[c_dve])
                vtp_free[half] = self.c_dve.snap()
            vt_free = self.c_dve.snap()
            kvw_free[wi] = self.c_dve.snap()

        if DEBUG:
            fin = [self.c_act.snap(), self.c_dve.snap()]
            self.dbg_xnt = self.dram_out("dbg_xnt", [128, ECH * TOWN], BF16)
            self.dbg_q = self.dram_out("dbg_q", [128, H * TOWN], BF16)
            self.dbg_kown = self.dram_out("dbg_kown", [128, G * TOWN], BF16)
            self.dbg_vown = self.dram_out("dbg_vown", [128, G * JT * 129], BF16)
            for dst, src in [(self.dbg_xnt, self.xnT_own),
                             (self.dbg_q, self.q_sb),
                             (self.dbg_kown, self.kown),
                             (self.dbg_vown, self.vown)]:
                P.op("sp", lambda e, d=dst, s=src: e.dma_start(out=d,
                                                               in_=s[:, :]),
                     waits=fin, inc=[c_sp])

    def phase15(self, es):
        P = self.P
        c_sp, c_gp, c_te, c_act, c_dve = (self.c_sp, self.c_gp, self.c_te,
                                          self.c_act, self.c_dve)
        s = self.sb
        sw1b = s(es, "sw1b", [128, ECH * HID], BF16)
        swgb = s(es, "swgb", [128, ECH * HID], BF16)
        wdb = s(es, "wdb", [128, ECH * LAT], BF16)
        wgb = s(es, "wgb", [128, ECH * NEXP], BF16)
        prob = s(es, "prob", [128, JT * NEXP], F32)
        vbuf = s(es, "vbuf_g", [128, JT * NEXP], F32)
        cnt = [s(es, f"cnt{i}", [128, JT * NEXP], F32) for i in (0, 1)]
        rbuf = s(es, "rbuf", [128, HID], BF16)
        rbuf2 = s(es, "rbuf2", [128, HID], BF16)
        rsum = s(es, "rsum", [128, JT], F32)
        z_ps = self.ps(es, "z_ps", [128, 512])
        h_ps = self.ps(es, "h_ps", [128, 1024])
        g_ps = self.ps(es, "g_ps", [128, 1024])

        xnT_view = self.xnT_own[:, :].rearrange("p (c n) -> p c n", c=ECH)

        P.op("gp", lambda e: e.dma_start(
            out=sw1b[:, :].rearrange("p (c n) -> p c n", c=ECH),
            in_=self.sw1T_in.rearrange("(c p) n -> p c n", p=128)),
            inc=[c_gp])
        P.op("gp", lambda e: e.dma_start(
            out=swgb[:, :].rearrange("p (c n) -> p c n", c=ECH),
            in_=self.swgT_in.rearrange("(c p) n -> p c n", p=128)),
            inc=[c_gp])
        P.op("gp", lambda e: e.dma_start(
            out=wdb[:, :].rearrange("p (c n) -> p c n", c=ECH),
            in_=self.wdT_in.rearrange("(c p) n -> p c n", p=128)),
            inc=[c_gp])
        P.op("gp", lambda e: e.dma_start(
            out=wgb[:, :].rearrange("p (c n) -> p c n", c=ECH),
            in_=self.wgT_in.rearrange("(c p) n -> p c n", p=128)),
            inc=[c_gp])
        wload = c_gp.snap()

        for t in range(JT):
            for ech in range(ECH):
                P.op("te", lambda e, t=t, ech=ech: e.matmul(
                    z_ps[:, t * NEXP:(t + 1) * NEXP],
                    xnT_view[:, ech, t * 128:(t + 1) * 128],
                    wgb[:, ech * NEXP:(ech + 1) * NEXP],
                    start=(ech == 0), stop=(ech == ECH - 1)),
                    waits=[wload],
                    inc=([c_te] if ech == ECH - 1 and t == JT - 1 else []))
        zmm = self.c_te.snap()
        P.op("act", lambda e: e.activation(
            prob[:, :], z_ps[:, 0:JT * NEXP], AF.Exp, scale=-1.0),
            waits=[zmm], inc=[c_act])
        P.op("dve", lambda e: e.tensor_scalar(
            prob[:, :], prob[:, :], 1.0, None, ALU.add),
            waits=[c_act], inc=[c_dve])
        P.op("dve", lambda e: e.reciprocal(prob[:, :], prob[:, :]),
            waits=[c_dve], inc=[c_dve])
        P.op("dve", lambda e: e.tensor_tensor(
            vbuf[:, :], prob[:, :], self.brep8[:, :], ALU.add),
            waits=[c_dve], inc=[c_dve])
        P.op("dve", lambda e: e.memset(cnt[0][:, :], 0.0),
            waits=[c_dve], inc=[c_dve])
        cur = 0
        for ep in range(NEXP):
            for t in range(JT):
                P.op("dve", lambda e, ep=ep, t=t, cur=cur: e.scalar_tensor_tensor(
                    cnt[1 - cur][:, t * NEXP:(t + 1) * NEXP],
                    vbuf[:, t * NEXP:(t + 1) * NEXP],
                    vbuf[:, t * NEXP + ep:t * NEXP + ep + 1],
                    cnt[cur][:, t * NEXP:(t + 1) * NEXP],
                    ALU.is_lt, ALU.add),
                    waits=[c_dve], inc=[c_dve])
            cur = 1 - cur
        P.op("dve", lambda e, cur=cur: e.tensor_scalar(
            cnt[1 - cur][:, :], cnt[cur][:, :], 7.5, None, ALU.is_le),
            waits=[c_dve], inc=[c_dve])
        P.op("dve", lambda e, cur=cur: e.tensor_tensor(
            self.wdense[:, :], prob[:, :], cnt[1 - cur][:, :], ALU.mult),
            waits=[c_dve], inc=[c_dve])
        P.op("dve", lambda e: e.reduce_sum(
            rsum[:, :].rearrange("p (j o) -> p j o", o=1),
            self.wdense[:, :].rearrange("p (j n) -> p j n", j=JT), axis=AX.X),
            waits=[c_dve], inc=[c_dve])
        P.op("dve", lambda e: e.reciprocal(rsum[:, :], rsum[:, :]),
            waits=[c_dve], inc=[c_dve])
        for t in range(JT):
            P.op("dve", lambda e, t=t: e.tensor_scalar(
                self.wdense[:, t * NEXP:(t + 1) * NEXP],
                self.wdense[:, t * NEXP:(t + 1) * NEXP],
                rsum[:, t:t + 1], RSCALE, ALU.mult, ALU.mult),
                waits=[c_dve], inc=[c_dve])

        r_free = None
        for hch in range(8):
            for tc in range(2):
                for ech in range(ECH):
                    P.op("te", lambda e, hch=hch, tc=tc, ech=ech: e.matmul(
                        h_ps[:, tc * 512:(tc + 1) * 512],
                        sw1b[:, ech * HID + hch * 128:
                             ech * HID + (hch + 1) * 128],
                        xnT_view[:, ech, tc * 512:(tc + 1) * 512],
                        start=(ech == 0), stop=(ech == ECH - 1)),
                        waits=[wload, c_dve], inc=[])
                    P.op("te", lambda e, hch=hch, tc=tc, ech=ech: e.matmul(
                        g_ps[:, tc * 512:(tc + 1) * 512],
                        swgb[:, ech * HID + hch * 128:
                             ech * HID + (hch + 1) * 128],
                        xnT_view[:, ech, tc * 512:(tc + 1) * 512],
                        start=(ech == 0), stop=(ech == ECH - 1)),
                        waits=[wload, c_dve],
                        inc=([c_te] if ech == ECH - 1 and tc == 1 else []))
            hmm = self.c_te.snap()
            P.op("act", lambda e: e.activation(
                rbuf[:, :], g_ps[:, :], AF.Relu),
                waits=[hmm, r_free], inc=[c_act])
            P.op("dve", lambda e: e.tensor_tensor(
                rbuf2[:, :], rbuf[:, :], rbuf[:, :], ALU.mult),
                waits=[c_act], inc=[c_dve])
            P.op("dve", lambda e, hch=hch: e.tensor_tensor(
                self.hhT[:, hch * TOWN:(hch + 1) * TOWN],
                h_ps[:, :], rbuf2[:, :], ALU.mult),
                waits=[c_dve], inc=[c_dve])
            r_free = self.c_dve.snap()

        for lch in range(4):
            for tc in range(2):
                for ech in range(ECH):
                    P.op("te", lambda e, lch=lch, tc=tc, ech=ech: e.matmul(
                        h_ps[:, tc * 512:(tc + 1) * 512],
                        wdb[:, ech * LAT + lch * 128:
                            ech * LAT + (lch + 1) * 128],
                        xnT_view[:, ech, tc * 512:(tc + 1) * 512],
                        start=(ech == 0), stop=(ech == ECH - 1)),
                        waits=[wload, c_act, c_dve],
                        inc=([c_te] if ech == ECH - 1 and tc == 1 else []))
            xmm = self.c_te.snap()
            P.op("act", lambda e, lch=lch: e.activation(
                self.xlT[:, lch * TOWN:(lch + 1) * TOWN], h_ps[:, :],
                AF.Identity),
                waits=[xmm], inc=[c_act])

        if DEBUG:
            fin = [self.c_act.snap(), self.c_dve.snap()]
            self.dbg_w = self.dram_out("dbg_w", [128, JT * NEXP], F32)
            self.dbg_hh = self.dram_out("dbg_hh", [128, 8 * TOWN], BF16)
            self.dbg_xl = self.dram_out("dbg_xl", [128, 4 * TOWN], BF16)
            for dst, src in [(self.dbg_w, self.wdense), (self.dbg_hh, self.hhT),
                             (self.dbg_xl, self.xlT)]:
                P.op("sp", lambda e, d=dst, s=src: e.dma_start(out=d,
                                                               in_=s[:, :]),
                     waits=fin, inc=[c_sp])

    def phase2(self, es):
        P = self.P
        c_sp, c_gp, c_te, c_act, c_dve = (self.c_sp, self.c_gp, self.c_te,
                                          self.c_act, self.c_dve)
        s = self.sb
        NS = 2
        kbuf = [s(es, f"kbuf{i}", [128, T], BF16) for i in (0, 1)]
        vbuf = [s(es, f"vbuf{i}", [128, 64 * 144], BF16) for i in (0, 1)]
        pt = [s(es, f"pt{i}", [128, 1024], BF16) for i in range(NS)]
        sst = s(es, "sst", [128, JT * 128], F32)
        qsq = s(es, "qsqb", [128, 4 * TOWN], BF16)
        ksq = s(es, "ksqb", [128, 512], BF16)
        kmaxb = s(es, "kmaxb", [1, 20], F32)
        cbuf = s(es, "cbuf", [1, 16], F32)
        crep4 = s(es, "crep4", [128, 4], F32)
        dbadj4 = s(es, "dbadj4", [128, 4 * JT], F32)
        rcpb = s(es, "rcpb", [128, JT], F32)
        zbuf = s(es, "zbuf", [128, 512], BF16)
        s_ps = [self.ps(es, f"s_ps{i}", [128, 1024]) for i in range(NS)]
        o_ps = [self.ps(es, f"o_ps{i}", [128, 512]) for i in range(3)]
        cp_ps = self.ps(es, "cp_ps", [128, 512])

        def o_tile(j):
            return o_ps[j // 3][:, (j % 3) * 129:(j % 3) * 129 + 129]

        for i in (0, 1):
            P.op("dve", lambda e, i=i: e.memset(
                vbuf[i][:, :].rearrange("p (q s) -> p q s", s=144)
                [:, :, 128:129], 1.0), inc=[c_dve])
        P.op("dve", lambda e: e.memset(zbuf[:, :], 0.0), inc=[c_dve])
        ones_set = c_dve.snap()

        exp_done = [None] * NS
        av_done = [None] * NS
        head_eplg = None
        sst_free = None
        group_done = [None, None]
        slot = 0

        for gi in range(G):
            bi = gi % 2
            P.op("gp", lambda e, gi=gi, bi=bi: e.dma_start(
                out=kbuf[bi][:, :], in_=self.k_hbm[gi]),
                waits=[group_done[bi]], inc=[c_gp])
            for qt in range(64):
                P.op("sp", lambda e, gi=gi, bi=bi, qt=qt: e.dma_start_transpose(
                    vbuf[bi][:, qt * 144:qt * 144 + 128],
                    self.v_hbm[gi, :, qt * 128:(qt + 1) * 128]),
                    waits=([group_done[bi], ones_set] if qt == 0 else []),
                    inc=[c_sp])
            ready_sp = self.c_sp.snap()
            ready_gp = self.c_gp.snap()
            ready = ready_sp

            guard = [w for w in exp_done if w] + [ready_sp, ready_gp]
            kprev = [None, None]
            for chk in range(T // 512):
                sl = chk % 2
                P.op("act", lambda e, bi=bi, chk=chk: e.activation(
                    ksq[:, :], kbuf[bi][:, chk * 512:(chk + 1) * 512],
                    AF.Square),
                    waits=[ready_gp, kprev[sl], c_dve] +
                          ([] if chk else guard), inc=[c_act])
                sq_done = c_act.snap()
                P.op("te", lambda e, sl=sl: e.matmul(
                    s_ps[0][0:1, sl * 512:(sl + 1) * 512],
                    self.onescol[:, 0:1], ksq[:, :],
                    start=True, stop=True),
                    waits=guard + [sq_done, kprev[sl]], inc=[c_te])
                mm = self.c_te.snap()
                P.op("dve", lambda e, chk=chk, sl=sl: e.reduce_max(
                    kmaxb[0:1, chk + 4:chk + 5],
                    s_ps[0][0:1, sl * 512:(sl + 1) * 512],
                    axis=AX.X),
                    waits=[mm], inc=[c_dve])
                kprev[sl] = c_dve.snap()
            P.op("dve", lambda e: e.reduce_max(
                kmaxb[0:1, 0:1], kmaxb[0:1, 4:4 + T // 512], axis=AX.X),
                waits=[kprev[0], kprev[1]], inc=[c_dve])
            P.op("act", lambda e: e.activation(
                kmaxb[0:1, 1:2], kmaxb[0:1, 0:1], AF.Ln),
                waits=[c_dve], inc=[c_act])
            lnk_done = c_act.snap()

            P.op("act", lambda e, gi=gi: e.activation(
                qsq[:, :], self.q_sb[:, gi * 4 * TOWN:(gi + 1) * 4 * TOWN],
                AF.Square), waits=[self.c_dve.snap()], inc=[c_act])
            qsq_done = c_act.snap()
            qprev = [None, None]
            for i in range(8):
                sl = i % 2
                P.op("te", lambda e, i=i, sl=sl: e.matmul(
                    s_ps[1][0:1, sl * 512:(sl + 1) * 512],
                    self.onescol[:, 0:1],
                    qsq[:, i * 512:(i + 1) * 512],
                    start=True, stop=True),
                    waits=guard + [qsq_done, qprev[sl]], inc=[c_te])
                mm = self.c_te.snap()
                P.op("dve", lambda e, i=i, sl=sl: e.reduce_max(
                    cbuf[0:1, i:i + 1],
                    s_ps[1][0:1, sl * 512:(sl + 1) * 512], axis=AX.X),
                    waits=[mm], inc=[c_dve])
                qprev[sl] = c_dve.snap()
            P.op("dve", lambda e: e.reduce_max(
                cbuf[0:1, 8:12].rearrange("o (h x) -> o h x", h=4),
                cbuf[0:1, 0:8].rearrange("o (h x) -> o h x", h=4),
                axis=AX.X),
                waits=[qprev[0], qprev[1]], inc=[c_dve])
            P.op("act", lambda e: e.activation(
                cbuf[0:1, 12:16], cbuf[0:1, 8:12], AF.Ln),
                waits=[c_dve], inc=[c_act])
            P.op("dve", lambda e: e.tensor_scalar(
                cbuf[0:1, 8:12], cbuf[0:1, 12:16], kmaxb[0:1, 1:2], None,
                ALU.add), waits=[c_act, lnk_done], inc=[c_dve])
            P.op("act", lambda e: e.activation(
                cbuf[0:1, 12:16], cbuf[0:1, 8:12], AF.Exp, scale=0.5),
                waits=[c_dve], inc=[c_act])
            P.op("dve", lambda e: e.tensor_scalar(
                cbuf[0:1, 8:12], cbuf[0:1, 12:16], -SCALE_ATT, None,
                ALU.mult), waits=[c_act], inc=[c_dve])
            P.op("te", lambda e: e.matmul(
                cp_ps[:, 0:4], self.onesfb[0:1, :], cbuf[0:1, 8:12],
                start=True, stop=True), waits=[c_dve], inc=[c_te])
            P.op("dve", lambda e: e.tensor_copy(crep4[:, :], cp_ps[:, 0:4]),
                 waits=[c_te], inc=[c_dve])
            for hh in range(4):
                P.op("dve", lambda e, hh=hh: e.tensor_scalar(
                    dbadj4[:, hh * JT:(hh + 1) * JT], self.dbiasb[:, :],
                    crep4[:, hh:hh + 1], None, ALU.add),
                    waits=[c_dve], inc=[c_dve])
            cprep = c_dve.snap()

            for hh in range(4):
                h = gi * 4 + hh
                o_free = head_eplg
                qoff = h * TOWN
                for b in range(3):
                    P.op("te", lambda e, b=b: e.matmul(
                        o_ps[b][:, :], self.identb[:, :], zbuf[:, :],
                        start=True, stop=False),
                        waits=[ones_set, o_free, cprep],
                        inc=([c_te] if b == 2 else []))

                def emit_S(jobs, sl, cprep):
                    alln = sum(len(j["s_mms"]) for j in jobs)
                    i = 0
                    for base, jb in zip((0, 512), jobs):
                        for dst_off, w, lf, rf in jb["s_mms"]:
                            i += 1
                            P.op("te", lambda e, dst_off=base + dst_off, w=w,
                                 lf=lf, rf=rf, sl=sl: e.matmul(
                                     s_ps[sl][:, dst_off:dst_off + w],
                                     lf(), rf(), start=True, stop=True),
                                 waits=[ready, cprep, exp_done[sl]],
                                 inc=([c_te] if i == alln else []))
                    return self.c_te.snap()

                def emit_EA(jobs, sl, smm):
                    extra = []
                    ntri = sum(4 for jb in jobs if jb["tri"])
                    it = 0
                    for base, jb in zip((0, 512), jobs):
                        if not jb["tri"]:
                            continue
                        for i4 in range(4):
                            it += 1
                            P.op("dve", lambda e, o=base + i4 * 128, sl=sl:
                                 e.tensor_tensor(
                                     s_ps[sl][:, o:o + 128],
                                     s_ps[sl][:, o:o + 128],
                                     self.trib[:, :], ALU.add),
                                 waits=([smm] if it == 1 else []),
                                 inc=([c_dve] if it == ntri else []))
                    if ntri:
                        extra = [self.c_dve.snap()]
                    width = (512 + jobs[1]["width"]) if len(jobs) == 2 \
                        else jobs[0]["width"]
                    P.op("act", lambda e, sl=sl, width=width,
                         bias_ap=jobs[0]["bias"]: e.activation(
                             pt[sl][:, 0:width], s_ps[sl][:, 0:width],
                             AF.Exp, bias=bias_ap, scale=SCALE_ATT),
                         waits=[smm, av_done[sl]] + extra, inc=[c_act])
                    exp_done[sl] = c_act.snap()
                    na = sum(len(jb["avs"]) for jb in jobs)
                    i = 0
                    for base, jb in zip((0, 512), jobs):
                        for j, pt_off, vf in jb["avs"]:
                            i += 1
                            P.op("te", lambda e, j=j, pt_off=base + pt_off,
                                 vf=vf, sl=sl: e.matmul(
                                     o_tile(j), pt[sl][:, pt_off:pt_off + 128],
                                     vf(), start=False, stop=False),
                                 waits=[exp_done[sl]],
                                 inc=([c_te] if i == na else []))
                    av_done[sl] = self.c_te.snap()

                jl = []
                for kt in range(56):
                    jmin = kt // 8 + 1
                    w_all = (8 - jmin) * 128
                    off = 0
                    while off < w_all:
                        w = min(512, w_all - off)
                        q0 = qoff + jmin * 128 + off
                        jlo = jmin + off // 128
                        lf = (lambda bi=bi, kt=kt:
                              kbuf[bi][:, kt * 128:(kt + 1) * 128])
                        rf = (lambda q0=q0, w=w: self.q_sb[:, q0:q0 + w])
                        avs = [(j, (j - jlo) * 128,
                                (lambda bi=bi, kt=kt:
                                 vbuf[bi][:, kt * 144:kt * 144 + 129]))
                               for j in range(jlo, jlo + w // 128)]
                        jl.append(dict(s_mms=[(0, w, lf, rf)], width=w,
                                       bias=crep4[:, hh:hh + 1], avs=avs,
                                       tri=False, bkey=("c", hh)))
                        off += w
                for dd in range(8):
                    for half in range(2):
                        smms = []
                        avs = []
                        for i in range(4):
                            j = half * 4 + i
                            kk = 8 * j + dd
                            lf = (lambda bi=bi, kk=kk:
                                  kbuf[bi][:, kk * 128:(kk + 1) * 128])
                            q0 = qoff + j * 128
                            rf = (lambda q0=q0: self.q_sb[:, q0:q0 + 128])
                            smms.append((i * 128, 128, lf, rf))
                            avs.append((j, i * 128,
                                        (lambda bi=bi, kk=kk:
                                         vbuf[bi][:, kk * 144:kk * 144 + 129])))
                        jl.append(dict(
                            s_mms=smms, width=512,
                            bias=dbadj4[:, hh * JT + dd:hh * JT + dd + 1],
                            avs=avs, tri=False, bkey=("d", hh, dd)))
                for half in range(2):
                    smms = []
                    avs = []
                    for i in range(4):
                        j = half * 4 + i
                        k0 = gi * TOWN + j * 128
                        lf = (lambda k0=k0: self.kown[:, k0:k0 + 128])
                        q0 = qoff + j * 128
                        rf = (lambda q0=q0: self.q_sb[:, q0:q0 + 128])
                        smms.append((i * 128, 128, lf, rf))
                        avs.append((j, i * 128,
                                    (lambda q=gi * JT + j:
                                     self.vown[:, q * 129:(q + 1) * 129])))
                    jl.append(dict(s_mms=smms, width=512,
                                   bias=crep4[:, hh:hh + 1], avs=avs,
                                   tri=True, bkey=("t", hh)))
                plist = []
                p = 0
                while p < len(jl):
                    if (p + 1 < len(jl)
                            and jl[p]["bkey"] == jl[p + 1]["bkey"]):
                        plist.append([jl[p], jl[p + 1]])
                        p += 2
                    else:
                        plist.append([jl[p]])
                        p += 1
                pend = None
                for pr in plist:
                    sl = slot
                    slot = (slot + 1) % NS
                    smm = emit_S(pr, sl, cprep)
                    if pend is not None:
                        emit_EA(*pend)
                    pend = (pr, sl, smm)
                emit_EA(*pend)
                all_av = self.c_te.snap()

                for j in range(JT):
                    P.op("dve", lambda e, j=j: e.reciprocal(
                        rcpb[:, j:j + 1], o_tile(j)[:, 128:129]),
                        waits=[all_av, c_dve], inc=[c_dve])
                    P.op("dve", lambda e, j=j: e.tensor_scalar(
                        sst[:, j * 128:(j + 1) * 128], o_tile(j)[:, 0:128],
                        rcpb[:, j:j + 1], None, ALU.mult),
                        waits=[c_dve, sst_free], inc=[c_dve])
                head_eplg = self.c_dve.snap()
                P.op("sp", lambda e, h=h: e.dma_start(
                    out=self.s_hbm.rearrange("(j p) e -> p j e", p=128)
                    [:, :, h * 128:(h + 1) * 128],
                    in_=sst[:, :].rearrange("p (j d) -> p j d", j=JT)),
                    waits=[head_eplg], inc=[c_sp])
                sst_free = self.c_sp.snap()
            group_done[bi] = self.c_te.snap()

        if DEBUG:
            fin = self.c_sp.snap()
            self.dbg_s = self.dram_out("dbg_s", [TOWN, E], F32)
            P.op("sp", lambda e: e.dma_start(out=self.dbg_s, in_=self.s_hbm),
                 waits=[fin], inc=[c_sp])

    def phase3(self, es):
        P = self.P
        c_sp, c_gp, c_te, c_act, c_dve = (self.c_sp, self.c_gp, self.c_te,
                                          self.c_act, self.c_dve)
        s = self.sb
        acc = [s(es, f"acc{i}", [128, JT * 512], F32) for i in (0, 1)]

        with ExitStack() as pa:
            rw1b = [self.sb(pa, f"rw1b{i}", [128, 4 * HID], BF16)
                    for i in (0, 1)]
            rwgb = [self.sb(pa, f"rwgb{i}", [128, 4 * HID], BF16)
                    for i in (0, 1)]
            rw2b = [self.sb(pa, f"rw2b{i}", [128, 8 * LAT], BF16)
                    for i in (0, 1)]
            abT = self.sb(pa, "abT", [128, 8 * TOWN], BF16)
            rbu = [self.sb(pa, f"rb3{i}", [128, 512], BF16) for i in (0, 1)]
            rb2u = [self.sb(pa, f"rb23{i}", [128, 512], BF16) for i in (0, 1)]
            ab_ps = [self.ps(pa, f"ab_ps{i}", [128, 1024]) for i in (0, 1)]
            y_ps = [self.ps(pa, f"y_ps{i}", [128, 512]) for i in (0, 1)]
            unit_free = [None, None]
            rb_free = [None, None]

            xlT_view = self.xlT[:, :].rearrange("p (l n) -> p l n", l=4)
            P.op("dve", lambda e: e.memset(acc[0][:, :], 0.0), inc=[c_dve])
            acc_zero = c_dve.snap()

            w_free = [None, None]
            y_free = [None, None]
            for ex in range(NEXP_RUN):
                wi = ex % 2
                P.op("gp", lambda e, ex=ex, wi=wi: e.dma_start(
                    out=rw1b[wi][:, :].rearrange("p (l n) -> p l n", l=4),
                    in_=self.rw1T_in[ex].rearrange("(l p) n -> p l n", p=128)),
                    waits=[w_free[wi]], inc=[c_gp])
                P.op("gp", lambda e, ex=ex, wi=wi: e.dma_start(
                    out=rwgb[wi][:, :].rearrange("p (l n) -> p l n", l=4),
                    in_=self.rwgT_in[ex].rearrange("(l p) n -> p l n", p=128)),
                    waits=[c_gp], inc=[c_gp])
                P.op("gp", lambda e, ex=ex, wi=wi: e.dma_start(
                    out=rw2b[wi][:, :].rearrange("p (hh n) -> p hh n", hh=8),
                    in_=self.rw2T_in[ex].rearrange("(hh p) n -> p hh n", p=128)),
                    waits=[c_gp], inc=[c_gp])
                wld = self.c_gp.snap()
                if DEBUG and ex == 0:
                    self.dbg_rwge = self.dram_out("dbg_rwge", [128, 4 * HID], BF16)
                    P.op("sp", lambda e: e.dma_start(out=self.dbg_rwge,
                                                     in_=rwgb[0][:, :]),
                         waits=[wld], inc=[c_sp])

                for hch in range(8):
                    for tc in range(2):
                        pu = (hch * 2 + tc) % 2
                        for lch in range(4):
                            P.op("te", lambda e, wi=wi, hch=hch, tc=tc,
                                 lch=lch, pu=pu: e.matmul(
                                     ab_ps[pu][:, 0:512],
                                     rw1b[wi][:, lch * HID + hch * 128:
                                              lch * HID + (hch + 1) * 128],
                                     xlT_view[:, lch, tc * 512:(tc + 1) * 512],
                                     start=(lch == 0), stop=(lch == 3)),
                                 waits=[wld, unit_free[pu]], inc=[])
                            P.op("te", lambda e, wi=wi, hch=hch, tc=tc,
                                 lch=lch, pu=pu: e.matmul(
                                     ab_ps[pu][:, 512:1024],
                                     rwgb[wi][:, lch * HID + hch * 128:
                                              lch * HID + (hch + 1) * 128],
                                     xlT_view[:, lch, tc * 512:(tc + 1) * 512],
                                     start=(lch == 0), stop=(lch == 3)),
                                 waits=[wld, unit_free[pu]],
                                 inc=([c_te] if lch == 3 else []))
                        abmm = self.c_te.snap()
                        P.op("act", lambda e, pu=pu: e.activation(
                            rbu[pu][:, :], ab_ps[pu][:, 512:1024], AF.Relu),
                            waits=[abmm, rb_free[pu]], inc=[c_act])
                        P.op("dve", lambda e, pu=pu: e.tensor_tensor(
                            rb2u[pu][:, :], rbu[pu][:, :], rbu[pu][:, :],
                            ALU.mult),
                            waits=[c_act], inc=[c_dve])
                        P.op("dve", lambda e, hch=hch, tc=tc, pu=pu:
                             e.tensor_tensor(
                                 abT[:, hch * TOWN + tc * 512:
                                     hch * TOWN + (tc + 1) * 512],
                                 ab_ps[pu][:, 0:512], rb2u[pu][:, :],
                                 ALU.mult),
                             waits=[c_dve], inc=[c_dve])
                        unit_free[pu] = self.c_dve.snap()
                        rb_free[pu] = unit_free[pu]
                ab_done = self.c_dve.snap()

                prev_acc = acc[ex % 2]
                next_acc = acc[(ex + 1) % 2]
                for t in range(JT):
                    yi = t % 2
                    for hch in range(8):
                        P.op("te", lambda e, wi=wi, t=t, hch=hch, yi=yi:
                             e.matmul(
                                 y_ps[yi][:, :],
                                 abT[:, hch * TOWN + t * 128:
                                     hch * TOWN + (t + 1) * 128],
                                 rw2b[wi][:, hch * LAT:(hch + 1) * LAT],
                                 start=(hch == 0), stop=(hch == 7)),
                             waits=[ab_done, y_free[yi]],
                             inc=([c_te] if hch == 7 else []))
                    ymm = self.c_te.snap()
                    P.op("dve", lambda e, ex=ex, t=t, yi=yi,
                         pa_=prev_acc, na=next_acc: e.scalar_tensor_tensor(
                             na[:, t * 512:(t + 1) * 512],
                             y_ps[yi][:, :],
                             self.wdense[:, t * NEXP + ex:t * NEXP + ex + 1],
                             pa_[:, t * 512:(t + 1) * 512],
                             ALU.mult, ALU.add),
                         waits=[ymm, acc_zero], inc=[c_dve])
                    y_free[yi] = self.c_dve.snap()
                w_free[wi] = self.c_dve.snap()
            acc_done = self.c_dve.snap()
            acc_fin = acc[NEXP_RUN % 2]
            if DEBUG:
                self.dbg_acc = self.dram_out("dbg_acc", [128, JT * 512], F32)
                P.op("sp", lambda e: e.dma_start(out=self.dbg_acc,
                                                 in_=acc_fin[:, :]),
                     waits=[acc_done], inc=[c_sp])
                self.dbg_ab = self.dram_out("dbg_ab", [128, 8 * TOWN], BF16)
                P.op("sp", lambda e: e.dma_start(out=self.dbg_ab,
                                                 in_=abT[:, :]),
                     waits=[acc_done], inc=[c_sp])
                self.dbg_xl2 = self.dram_out("dbg_xl2", [128, 4 * TOWN], BF16)
                P.op("sp", lambda e: e.dma_start(out=self.dbg_xl2,
                                                 in_=self.xlT[:, :]),
                     waits=[acc_done], inc=[c_sp])


        with ExitStack() as pb:
            sw2b = self.sb(pb, "sw2b", [128, 8 * E], BF16)
            wupb = self.sb(pb, "wupb", [128, 4 * E], BF16)
            accb = self.sb(pb, "accb", [128, JT * 512], BF16)
            accT = self.sb(pb, "accT", [128, 4 * TOWN], BF16)
            gbuf = self.sb(pb, "gbuf", [128, E], F32)
            stile = self.sb(pb, "stile", [128, E], F32)
            xtile = self.sb(pb, "xtile", [128, E], F32)
            outst = self.sb(pb, "outst", [128, E], F32)
            o4_ps = self.ps(pb, "o4_ps", [128, 2048])
            tp2_ps = self.ps(pb, "tp2_ps", [128, 512], BF16)

            P.op("gp", lambda e: e.dma_start(
                out=sw2b[:, :].rearrange("p (hh n) -> p hh n", hh=8),
                in_=self.sw2T_in.rearrange("(hh p) n -> p hh n", p=128)),
                waits=[acc_done], inc=[c_gp])
            P.op("gp", lambda e: e.dma_start(
                out=wupb[:, :].rearrange("p (l n) -> p l n", l=4),
                in_=self.wupT_in.rearrange("(l p) n -> p l n", p=128)),
                waits=[acc_done], inc=[c_gp])
            wld2 = self.c_gp.snap()

            P.op("dve", lambda e: e.tensor_copy(accb[:, :], acc_fin[:, :]),
                 waits=[acc_done], inc=[c_dve])
            accb_done = self.c_dve.snap()
            tp_free = None
            for t in range(JT):
                for lch in range(4):
                    P.op("te", lambda e, t=t, lch=lch: e.transpose(
                        tp2_ps[:, lch * 128:(lch + 1) * 128],
                        accb[:, t * 512 + lch * 128: t * 512 + (lch + 1) * 128],
                        self.identb[:, :]),
                        waits=[accb_done, tp_free],
                        inc=([c_te] if lch == 3 else []))
                tmm = self.c_te.snap()
                P.op("dve", lambda e, t=t: e.tensor_copy(
                    accT[:, :].rearrange("p (l n) -> p l n", l=4)
                    [:, :, t * 128:(t + 1) * 128],
                    tp2_ps[:, :].rearrange("p (l n) -> p l n", l=4)),
                    waits=[tmm], inc=[c_dve])
                tp_free = self.c_dve.snap()
            accT_done = self.c_dve.snap()

            o4_free = None
            st_free = acc_done
            for t in range(JT):
                P.op("sp", lambda e, t=t: e.dma_start(
                    out=stile[:, :],
                    in_=self.s_hbm[t * 128:(t + 1) * 128, :]),
                    waits=[st_free], inc=[c_sp])
                P.op("sp", lambda e, t=t: e.dma_start(
                    out=xtile[:, :],
                    in_=self.x_own[t * 128:(t + 1) * 128, :]),
                    waits=[st_free], inc=[c_sp])
                sx_in = self.c_sp.snap()
                for nch in range(4):
                    for hch in range(8):
                        P.op("te", lambda e, t=t, nch=nch, hch=hch: e.matmul(
                            o4_ps[:, nch * 512:(nch + 1) * 512],
                            self.hhT[:, hch * TOWN + t * 128:
                                     hch * TOWN + (t + 1) * 128],
                            sw2b[:, hch * E + nch * 512:
                                 hch * E + (nch + 1) * 512],
                            start=(hch == 0), stop=False),
                            waits=[wld2, o4_free], inc=[])
                    for lch in range(4):
                        P.op("te", lambda e, t=t, nch=nch, lch=lch: e.matmul(
                            o4_ps[:, nch * 512:(nch + 1) * 512],
                            accT[:, lch * TOWN + t * 128:
                                 lch * TOWN + (t + 1) * 128],
                            wupb[:, lch * E + nch * 512:
                                 lch * E + (nch + 1) * 512],
                            start=False, stop=(lch == 3)),
                            waits=[accT_done, wld2, o4_free],
                            inc=([c_te] if lch == 3 and nch == 3 else []))
                omm = self.c_te.snap()
                P.op("act", lambda e: e.activation(
                    gbuf[:, :], o4_ps[:, :], AF.Silu),
                    waits=[omm], inc=[c_act])
                o4_free = self.c_act.snap()
                if DEBUG and t == 0:
                    self.dbg_g = self.dram_out("dbg_g", [128, E], F32)
                    P.op("sp", lambda e: e.dma_start(out=self.dbg_g,
                                                     in_=gbuf[:, :]),
                         waits=[o4_free], inc=[c_sp])
                P.op("dve", lambda e: e.tensor_tensor(
                    outst[:, :], stile[:, :], gbuf[:, :], ALU.mult),
                    waits=[c_act, sx_in, st_free], inc=[c_dve])
                P.op("dve", lambda e: e.tensor_tensor(
                    outst[:, :], outst[:, :], xtile[:, :], ALU.add),
                    waits=[c_dve], inc=[c_dve])
                comb = self.c_dve.snap()
                P.op("sp", lambda e, t=t: e.dma_start(
                    out=self.out[t * 128:(t + 1) * 128, :], in_=outst[:, :]),
                    waits=[comb], inc=[c_sp])
                st_free = self.c_sp.snap()

    def finale(self):
        self.P.op("sp", None, waits=[self.c_sp.snap()])



def nc_sem_stack(nc, ctrs):
    class _S:
        def __enter__(self):
            self.es = ExitStack()
            for c in ctrs:
                c.sem = self.es.enter_context(nc.semaphore(c.name))
            return self

        def __exit__(self, *a):
            self.es.close()
            return False
    return _S()


def kernel(**inputs):
    maps = _stage_inputs(inputs)
    b = Builder()
    nc = b.build()
    res = run_bass_kernel_spmd(nc, maps, core_ids=list(range(NCORES)))
    if DEBUG:
        return res
    return _assemble(res.results)
